# version 04
# baseline (speedup 1.0000x reference)
import contextlib
import numpy as np
import ml_dtypes
import concourse.bass as bass
import concourse.mybir as mybir
from concourse.bass_utils import run_bass_kernel_spmd

F32 = mybir.dt.float32
BF16 = mybir.dt.bfloat16
AF = mybir.ActivationFunctionType
ALU = mybir.AluOpType
AX = mybir.AxisListType.X
bf16 = ml_dtypes.bfloat16

B, T, D, H, HD, C = 2, 8192, 1024, 4, 256, 64
PAIR = 2 * C
SEG = 2048
NSEG = 4
NTILE = SEG // 128
NCORE = 8
EPS = 1e-5
RD = 1.0 / D


class _Op:
    __slots__ = ("eng", "kind", "attr", "args", "kwargs", "incs", "sem", "val")


class _Handle:
    def __init__(self, ker, op):
        self.ker, self.op = ker, op
        self.vals = {}

    def inc(self, sem, amt=None):
        if sem in self.vals:
            return self.vals[sem]
        amt = amt if amt is not None else (16 if sem.startswith("q") else 1)
        self.op.incs.append((sem, amt))
        self.ker.cnt[sem] = self.ker.cnt.get(sem, 0) + amt
        self.vals[sem] = self.ker.cnt[sem]
        return self.vals[sem]


class EngQ:
    OWN = {"pe": "pe"}
    AUTOINC = ("act", "dve", "pool")
    NOINC = ("dma_start", "collective_compute", "wait_ge")

    def __init__(self, ker, name):
        self._ker, self._name = ker, name

    def __getattr__(self, attr):
        ker, name = self._ker, self._name

        def call(*args, **kwargs):
            op = _Op()
            op.eng, op.kind, op.attr, op.args, op.kwargs, op.incs = (
                name, "ins", attr, args, kwargs, [])
            ker.ops.append(op)
            h = _Handle(ker, op)
            if name in EngQ.AUTOINC and attr not in EngQ.NOINC:
                h.inc(name)
            return h

        return call

    def wait(self, sem, val, hard=False):
        ker, name = self._ker, self._name
        if val <= 0:
            return
        if not hard and EngQ.OWN.get(name) == sem:
            return
        key = (name, sem)
        if ker.wm.get(key, 0) >= val:
            return
        ker.wm[key] = val
        op = _Op()
        op.eng, op.kind, op.sem, op.val = name, "wait", sem, val
        ker.ops.append(op)


class Res:
    __slots__ = ("writer", "readers")

    def __init__(self):
        self.writer = None
        self.readers = []


class Ker:
    def __init__(self):
        self.ops = []
        self.cnt = {}
        self.wm = {}
        self.res = {}
        for n in ("pe", "act", "dve", "pool", "qs"):
            setattr(self, n, EngQ(self, n))

    def R(self, name):
        if name not in self.res:
            self.res[name] = Res()
        return self.res[name]

    def c(self, sem):
        return self.cnt.get(sem, 0)

    def fence(self, q):
        name = q._name
        q.wait(name, self.cnt.get(name, 0))

    def sread(self, q, *names):
        for r in names:
            w = self.R(r).writer
            if w:
                q.wait(w[0], w[1], hard=True)

    def pre(self, q, reads=(), writes=()):
        for r in reads:
            w = self.R(r).writer
            if w:
                q.wait(*w)
        for wname in writes:
            rr = self.R(wname)
            for rd in rr.readers:
                q.wait(*rd)
            rr.readers = []
            if rr.writer:
                q.wait(*rr.writer)

    def post(self, h, sem, reads=(), writes=()):
        v = h.inc(sem)
        for r in reads:
            self.R(r).readers.append((sem, v))
        for w in writes:
            rr = self.R(w)
            rr.writer = (sem, v)
            rr.readers = []
        return v

    def emit(self, nc, sems):
        by_eng = {"pe": [], "act": [], "dve": [], "pool": [], "qs": []}
        for op in self.ops:
            by_eng[op.eng].append(op)

        def runner(eng, ops):
            for op in ops:
                if op.kind == "wait":
                    eng.wait_ge(sems[op.sem], op.val)
                else:
                    ins = getattr(eng, op.attr)(*op.args, **op.kwargs)
                    for (sname, amt) in op.incs:
                        ins.then_inc(sems[sname], amt)

        with nc.Block() as block:
            @block.tensor
            def _(eng):
                runner(eng, by_eng["pe"])

            @block.scalar
            def _(eng):
                runner(eng, by_eng["act"])

            @block.vector
            def _(eng):
                runner(eng, by_eng["dve"])

            @block.gpsimd
            def _(eng):
                runner(eng, by_eng["pool"])

            @block.sync
            def _(eng):
                runner(eng, by_eng["qs"])


def _host_consts():
    i_ = np.arange(PAIR)[:, None].astype(np.int64)
    j_ = np.arange(PAIR)[None, :].astype(np.int64)
    same = (i_ // C) == (j_ // C)
    strict = same & (i_ > j_)
    big = 1.0e4
    diffN = np.where(strict, (i_ - j_).astype(np.float64), big).astype(np.float32)
    signN = np.where(strict, -1.0, 0.0).astype(np.float32)
    diffNT = np.ascontiguousarray(diffN.T)
    signNT = np.ascontiguousarray(signN.T)
    diffITX = np.where(strict.T, (i_.T - j_.T).astype(np.float64), big)
    signITX = np.where(strict.T, 1.0, 0.0)
    jj = np.arange(C)[:, None]
    ii = np.arange(C)[None, :]
    diffITX[0:C, C:PAIR] = 63 - jj + ii
    signITX[0:C, C:PAIR] = 1.0
    diffX = np.full((PAIR, PAIR), big)
    diffX[C:PAIR, 0:C] = np.tile((63 - np.arange(C))[None, :], (C, 1))
    signX = np.zeros((PAIR, PAIR))
    signX[C:PAIR, 0:C] = 1.0
    colidx = np.tile(np.arange(PAIR, dtype=np.float32)[None, :], (PAIR, 1))
    c127 = (127.0 - np.arange(PAIR, dtype=np.float32))[:, None]
    c64 = np.full((PAIR, 1), 64.0, np.float32)
    c128 = np.full((PAIR, 1), 128.0, np.float32)
    return dict(
        diffN=diffN, signN=signN, diffNT=diffNT, signNT=signNT,
        diffITX=diffITX.astype(np.float32), signITX=signITX.astype(np.float32),
        diffX=diffX.astype(np.float32), signX=signX.astype(np.float32),
        colidx=colidx, c127=c127, c64=c64, c128=c128,
        ident=np.eye(PAIR, dtype=np.float32).astype(bf16),
        foldF=(np.arange(128)[:, None] % 64 == np.arange(64)[None, :]
               ).astype(np.float32).astype(bf16),
        ones_row=np.ones((1, PAIR), np.float32).astype(bf16),
        ones_col=np.ones((PAIR, 1), np.float32).astype(bf16),
        negones_row=np.full((1, HD), -1.0, np.float32).astype(bf16),
    )


_CONST_DTYPES = {"ident": BF16, "ones_row": BF16, "ones_col": BF16,
                 "negones_row": BF16, "foldF": BF16}


def _w_cols(Wx):
    out = np.zeros((128, 8), dtype=np.float32)
    p = np.arange(128)
    for j in range(8):
        out[:, j] = Wx[p % 64, 2 * j + p // 64]
    return out


STAGE = 5
PSUB = 99
DEBUG = False


def build_nc(stage=None):
    stage = STAGE if stage is None else stage
    nc = bass.Bass("TRN2", target_bir_lowering=False, debug=False)
    ap = {}

    def din(name, shape, dtype=F32):
        ap[name] = nc.dram_tensor(name, shape, dtype, kind="ExternalInput").ap()
        return ap[name]

    din("x_own", [SEG, D])
    din("Wwh", [HD, D])
    din("Wr", [D, D])
    din("Win", [D, 64])
    din("W1c", [128, 8]); din("b1c", [128, 8]); din("W2c", [128, 8])
    din("binc", [128, 1])
    din("Wout", [64, D]); din("bout", [1, D]); din("b2col", [64, 1])
    din("decay_h", [1, 1]); din("log_alpha_h", [1, 1])
    din("segmask", [1, 4])
    din("EyeH", [128, 8 * HD], BF16)
    for k, v in _host_consts().items():
        din(k, list(v.shape), _CONST_DTYPES.get(k, F32))

    out_own = nc.dram_tensor("out_own", [SEG, D], F32, kind="ExternalOutput").ap()
    if DEBUG:
        dbg_ag = nc.dram_tensor("dbg_ag", [D + 2, SEG], BF16, kind="ExternalOutput").ap()
        dbg_o = nc.dram_tensor("dbg_o", [T, HD], BF16, kind="ExternalOutput").ap()
        dbg_z = nc.dram_tensor("dbg_z", [SEG, D], F32, kind="ExternalOutput").ap()
        dbg_p = nc.dram_tensor("dbg_p", [128, 4096], F32, kind="ExternalOutput").ap()
        dbg_pb = nc.dram_tensor("dbg_pb", [128, 8192], BF16, kind="ExternalOutput").ap()

    ag_in = nc.dram_tensor("ag_in", [D + 2, SEG], BF16).ap()
    ag_out = nc.dram_tensor("ag_out", [NSEG * (D + 2), SEG], BF16).ap()
    ago_in = nc.dram_tensor("ago_in", [T, HD], BF16).ap()
    ago_out = nc.dram_tensor("ago_out", [NSEG * T, HD], BF16).ap()
    z_spill = nc.dram_tensor("z_spill", [SEG, D], F32).ap()
    bc_buf = nc.dram_tensor("bc_buf", [8, 1], F32).ap()

    K = Ker()
    sem_names = ["pe", "act", "dve", "pool", "qs", "qg", "cc"]

    with contextlib.ExitStack() as ctx:
        def sb(name, shape, dtype):
            return ctx.enter_context(nc.sbuf_tensor(name, shape, dtype))

        def psum(name, shape):
            return ctx.enter_context(nc.psum_tensor(name, shape, F32))

        sems = {n: ctx.enter_context(nc.semaphore(n)) for n in sem_names}

        ident = sb("identb", [128, 128], BF16)
        ones_row = sb("onesrow", [1, 128], BF16)
        ones_col = sb("onescol", [128, 1], BF16)
        negones_row = sb("negonesrow", [1, HD], BF16)
        EyeH = sb("EyeHs", [128, 8 * HD], BF16)
        Wwt = sb("Wwt", [128, 8 * HD], BF16)
        negwwsum = sb("negwwsum", [1, HD], BF16)
        Wrt = sb("Wrt", [128, 8 * D], BF16)
        Win2 = sb("Win2", [128, 8 * 128], BF16)
        WoutA = sb("WoutA", [65, 8 * 128], BF16)
        w1c = sb("w1c", [128, 8], F32)
        b1eff = sb("b1eff", [128, 8], F32)
        diagW2 = sb("diagW2", [128, 8 * 64], BF16)
        foldF = sb("foldFs", [128, 64], BF16)
        maskN = sb("maskN", [128, 128], F32)
        maskNT = sb("maskNT", [128, 128], F32)
        maskITX = sb("maskITX", [128, 128], F32)
        maskX = sb("maskX", [128, 128], F32)
        gpcol = sb("gpcol", [128, 128], F32)
        gtpair = sb("gtpair", [128, 1], F32)
        gC2col = sb("gC2col", [128, 1], F32)
        gCcol = sb("gCcol", [128, 1], F32)
        alphacol = sb("alphacol", [128, 1], F32)
        lngcol = sb("lngcol", [128, 1], F32)
        maskI = sb("maskI", [128, 4 * 128], BF16)
        smallA = sb("smallA", [1, 8], F32)
        smallB = sb("smallB", [1, 8], F32)

        def sbd(name, shape, dtype):
            return [sb(name + "A", shape, dtype), sb(name + "B", shape, dtype)]

        xt = sb("xt", [128, 2 * D], F32)
        sqscr = sb("sqscr", [128, D], BF16)
        y0d = sbd("y0", [128, D], BF16)
        y0fmd = sbd("y0fm", [128, D], BF16)
        Htiled = sbd("Htile", [128, 8 * 128], BF16)
        gaugd = sbd("gaug", [65, 128], BF16)
        ffmd = sbd("ffm", [128, D], BF16)
        x1td = sbd("x1t", [128, D], F32)
        x1bfd = sbd("x1bf", [128, D], BF16)
        x1fmd = sbd("x1fm", [128, D], BF16)
        statcd = sbd("statc", [128, 12], F32)
        statcb = sb("statcb", [128, 2], BF16)
        statrow = sb("statrow", [2, SEG], BF16)
        zrowd = sbd("zrow", [128, D], F32)

        x1fmi = sb("x1fmi", [128, 2 * D], BF16)
        statin = sb("statin", [2, SEG], BF16)
        stat2cd = sbd("stat2c", [128, 4], F32)
        vtok = sb("vtok", [128, NTILE * HD], BF16)
        wktok = sb("wktok", [128, NTILE * HD], BF16)
        rkd0 = sb("rkd0", [128, SEG + 1], BF16)
        rkd1 = sb("rkd1", [128, SEG + 1], BF16)
        rkg0 = sb("rkg0", [128, SEG], BF16)
        rkg1 = sb("rkg1", [128, SEG], BF16)
        wkcT0 = sb("wkcT0", [128, SEG], BF16)
        wkcT1 = sb("wkcT1", [128, SEG], BF16)
        t1tokd = sbd("t1tok", [128, HD], BF16)
        t1sqd = sbd("t1sq", [128, HD], BF16)
        rktokd = sbd("rktok", [128, HD], BF16)
        nrmd = sbd("nrm", [128, 4], F32)
        PTp = sb("PTp", [128, NTILE * 128], BF16)
        PTgCd = sbd("PTgC", [128, 64], BF16)
        ITXp = sb("ITXp", [128, NTILE * 128], BF16)
        Xbad = sbd("Xba", [128, 64], BF16)
        MbaTn = sb("MbaTn", [64, NTILE * 64], BF16)
        powbufd = sbd("powbuf", [128, 256], BF16)
        ptprevd = sbd("ptprev", [128, 128], BF16)
        vnewsb = sb("vnewsb", [128, HD], BF16)
        vssb = sb("vssb", [128, HD], BF16)
        osb = sb("osb", [128, HD], BF16)
        S32 = sb("S32", [128, 2 * HD], F32)
        Shi = sb("Shi", [128, 2 * HD], BF16)

        oblkd = sbd("oblk", [128, 16 * HD], BF16)
        o_fmd = sbd("o_fm", [128, 8 * 128], BF16)
        x2td = sbd("x2t", [128, D], F32)
        ztile = sb("ztile", [128, 2 * D], F32)

        scr = sb("scratch", [128, 3 * 1024 + 73], F32)
        scr2 = sb("scratch2", [128, 24], F32)
        scrb = sb("scratchb", [128, 1024], BF16)

        pb = [psum("pb%d" % i, [128, 512]) for i in range(6)]
        pwide = psum("pwide", [128, 1024])

        qs, pool, pe, act, dve = K.qs, K.pool, K.pe, K.act, K.dve
        pre, post = K.pre, K.post

        def mm(out, lhsT, rhs, start, stop, tp=None):
            kw = {"start": start, "stop": stop}
            if tp is not None:
                kw["tile_position"] = tp
            return pe.matmul(out, lhsT, rhs, **kw)

        BK = {"pb%d" % i: pb[i] for i in range(6)}
        BK["pw0"] = pwide[:, 0:512]
        BK["pw1"] = pwide[:, 512:1024]

        ev_tog = [0]

        def evac(dst, dtag, srcp, ptag, scale=None, extra_reads=()):
            e = ev_tog[0]
            ev_tog[0] ^= 1
            if e == 0:
                pre(act, reads=(ptag,) + tuple(extra_reads), writes=(dtag,))
                if scale is None:
                    h = act.copy(dst, srcp)
                else:
                    h = act.activation(dst, srcp, AF.Identity,
                                       scale=np.float32(scale))
                post(h, "act", reads=(ptag,) + tuple(extra_reads), writes=(dtag,))
            else:
                pre(dve, reads=(ptag,) + tuple(extra_reads), writes=(dtag,))
                if scale is None:
                    h = dve.tensor_copy(dst, srcp)
                else:
                    h = dve.tensor_scalar(dst, srcp, float(scale), None, ALU.mult)
                post(h, "dve", reads=(ptag,) + tuple(extra_reads), writes=(dtag,))

        h = qs.dma_start(out=ident[:], in_=ap["ident"]); h.inc("qs")
        h = qs.dma_start(out=foldF[:], in_=ap["foldF"]); h.inc("qs")
        h = qs.dma_start(out=ones_row[:], in_=ap["ones_row"]); h.inc("qs")
        h = qs.dma_start(out=ones_col[:], in_=ap["ones_col"]); h.inc("qs")
        h = qs.dma_start(out=negones_row[:], in_=ap["negones_row"]); h.inc("qs")
        h = qs.dma_start(out=EyeH[:], in_=ap["EyeH"]); h.inc("qs")
        h = qs.dma_start(out=w1c[:], in_=ap["W1c"]); h.inc("qs")
        h = qs.dma_start(out=scr2[:, 0:8], in_=ap["b1c"]); h.inc("qs")
        h = qs.dma_start(out=scr2[:, 8:16], in_=ap["W2c"]); h.inc("qs")
        h = qs.dma_start(out=scr2[:, 16:17], in_=ap["binc"]); h.inc("qs")
        h = qs.dma_start(out=smallA[0:1, 0:1], in_=ap["decay_h"]); h.inc("qs")
        h = qs.dma_start(out=smallA[0:1, 1:2], in_=ap["log_alpha_h"]); h.inc("qs")
        h = qs.dma_start(out=scr2[0:1, 17:21], in_=ap["segmask"]); h.inc("qs")
        c0 = K.c("qs")

        act.wait("qs", c0)
        h = act.activation(smallB[0:1, 0:1], smallA[0:1, 0:1], AF.Exp, scale=np.float32(-1.0))
        post(h, "act", writes=("smB",))
        pre(dve, reads=("smB",))
        h = dve.tensor_scalar(smallB[0:1, 0:1], smallB[0:1, 0:1], 1.0, None, ALU.add)
        K.fence(dve)
        h = dve.reciprocal(smallB[0:1, 1:2], smallB[0:1, 0:1])
        post(h, "dve", writes=("smB",))
        pre(act, reads=("smB",))
        h = act.activation(smallB[0:1, 2:3], smallB[0:1, 1:2], AF.Ln)
        h = act.activation(smallB[0:1, 3:4], smallA[0:1, 1:2], AF.Exp)
        post(h, "act", writes=("smB2",))
        pre(qs, reads=("smB2",))
        h = qs.dma_start(out=bc_buf[0:1, 0:1], in_=smallB[0:1, 2:3]); h.inc("qs")
        h = qs.dma_start(out=bc_buf[1:2, 0:1], in_=smallB[0:1, 3:4]); cbc = h.inc("qs")
        qs.wait("qs", cbc)
        h = qs.dma_start(out=lngcol[:], in_=bc_buf[0:1, 0:1].broadcast_to([128, 1]))
        post(h, "qs", writes=("lngcol",))
        h = qs.dma_start(out=alphacol[:], in_=bc_buf[1:2, 0:1].broadcast_to([128, 1]))
        post(h, "qs", writes=("alphacol",))

        mask_specs = [("diffN", "signN", maskN), ("diffNT", "signNT", maskNT),
                      ("diffITX", "signITX", maskITX), ("diffX", "signX", maskX)]
        for idx, (dn, sn, dst) in enumerate(mask_specs):
            o1 = 2048 + 256 * (idx % 2)
            o2 = o1 + 128
            pre(qs, writes=("mload%d" % (idx % 2),))
            h = qs.dma_start(out=scr[:, o1:o1 + 128], in_=ap[dn]); h.inc("qs")
            h = qs.dma_start(out=scr[:, o2:o2 + 128], in_=ap[sn])
            cm = post(h, "qs", writes=("mload%d" % (idx % 2),))
            act.wait("qs", cm)
            pre(act, reads=("lngcol",), writes=("msk%d" % idx,))
            h = act.activation(dst[:], scr[:, o1:o1 + 128], AF.Exp,
                               scale=lngcol[:, 0:1])
            post(h, "act", reads=("mload%d" % (idx % 2),), writes=("msk%d" % idx,))
            pre(dve, reads=("msk%d" % idx,), writes=("msk%d" % idx,))
            h = dve.tensor_tensor(dst[:], dst[:], scr[:, o2:o2 + 128], ALU.mult)
            post(h, "dve", reads=("mload%d" % (idx % 2),), writes=("msk%d" % idx,))
        h = qs.dma_start(out=scr[:, 2944:3072], in_=ap["colidx"]); h.inc("qs")
        h = qs.dma_start(out=scr[:, 3072:3073], in_=ap["c127"]); h.inc("qs")
        h = qs.dma_start(out=scr[:, 3073:3074], in_=ap["c64"]); h.inc("qs")
        h = qs.dma_start(out=scr[:, 3074:3075], in_=ap["c128"]); cgp = h.inc("qs")
        act.wait("qs", cgp)
        pre(act, reads=("lngcol",))
        h = act.activation(gpcol[:], scr[:, 2944:3072], AF.Exp, scale=lngcol[:, 0:1])
        post(h, "act", writes=("gpcol",))
        h = act.activation(gtpair[:], scr[:, 3072:3073], AF.Exp, scale=lngcol[:, 0:1])
        post(h, "act", writes=("gtpair",))
        h = act.activation(gCcol[:], scr[:, 3073:3074], AF.Exp, scale=lngcol[:, 0:1])
        post(h, "act", writes=("gCcol",))
        h = act.activation(gC2col[:], scr[:, 3074:3075], AF.Exp, scale=lngcol[:, 0:1])
        post(h, "act", writes=("gC2col",))

        dve.wait("qs", c0)
        h = dve.tensor_copy(scrb[0:1, 0:4], scr2[0:1, 17:21])
        post(h, "dve", writes=("segb",))
        pre(pe, reads=("segb",), writes=("pb0",))
        h = mm(pb[0][0:128, 0:4], ones_row[0:1, 0:128],
               scrb[0:1, 0:4], True, True)
        post(h, "pe", reads=("segb",), writes=("pb0",))
        pre(act, reads=("pb0",), writes=("segc",))
        h = act.copy(statcd[0][:, 8:12], pb[0][0:128, 0:4])
        post(h, "act", reads=("pb0",), writes=("segc",))
        pre(dve, reads=("segc",), writes=("maskI",))
        for k in range(4):
            h = dve.tensor_scalar(maskI[:, 128 * k:128 * k + 128], ident[:],
                                  statcd[0][:, 8 + k:9 + k], None, ALU.mult)
        post(h, "dve", reads=("segc",), writes=("maskI",))

        dve.wait("qs", c0)
        h = dve.tensor_scalar(b1eff[:], w1c[:], scr2[:, 16:17], None, ALU.mult)
        K.fence(dve)
        h = dve.tensor_tensor(b1eff[:], b1eff[:], scr2[:, 0:8], ALU.add)
        post(h, "dve", writes=("b1eff",))
        pre(dve, writes=("diagW2",))
        for j in range(8):
            h = dve.tensor_scalar(diagW2[:, 64 * j:64 * j + 64], foldF[:],
                                  scr2[:, 8 + j:9 + j], None, ALU.mult)
        post(h, "dve", writes=("diagW2",))
        h = dve.memset(gaugd[0][64:65, :], 1.0)
        h = dve.memset(gaugd[1][64:65, :], 1.0)
        post(h, "dve", writes=("gaug1",))

        pre(qs, writes=("scrstage",))
        h = qs.dma_start(out=scr[:, 1024:2048], in_=ap["Wwh"][0:128, :]); h.inc("qs")
        h = qs.dma_start(out=scr[:, 0:1024], in_=ap["Wwh"][128:256, :])
        cww = post(h, "qs", writes=("scrstage",))
        dve.wait("qs", cww)
        for half in range(2):
            pre(dve, reads=("scrstage",), writes=("scrb",))
            h = dve.tensor_copy(scrb[:, 0:1024],
                                scr[:, 1024:2048] if half == 0 else scr[:, 0:1024])
            post(h, "dve", reads=("scrstage",), writes=("scrb",))
            pre(pe, reads=("scrb",), writes=("pb0", "pb1"))
            for k in range(8):
                h = mm(pb[k // 4][:, 128 * (k % 4):128 * (k % 4) + 128],
                       scrb[:, 128 * k:128 * k + 128], ident[:], True, True)
            post(h, "pe", reads=("scrb",), writes=("pb0", "pb1"))
            pre(act, reads=("pb0", "pb1"), writes=("Wwt",))
            for k in range(8):
                h = act.copy(Wwt[:, 256 * k + 128 * half:256 * k + 128 * half + 128],
                             pb[k // 4][:, 128 * (k % 4):128 * (k % 4) + 128])
            post(h, "act", reads=("pb0", "pb1"), writes=("Wwt",))
        pre(pe, reads=("Wwt",), writes=("pb2",))
        for k in range(8):
            h = mm(pb[2][0:1, 0:256], ones_col[:, 0:1], Wwt[:, 256 * k:256 * k + 256],
                   k == 0, k == 7)
        post(h, "pe", reads=("Wwt",), writes=("pb2",))
        pre(act, reads=("pb2",), writes=("negwwsum",))
        h = act.activation(negwwsum[:], pb[2][0:1, 0:256], AF.Identity, scale=np.float32(-1.0))
        post(h, "act", reads=("pb2",), writes=("negwwsum",))

        for r in range(8):
            pre(qs, writes=("scrstage",))
            h = qs.dma_start(out=scr[:, 1024:2048],
                             in_=ap["Wr"][128 * r:128 * r + 128, :])
            cw = post(h, "qs", writes=("scrstage",))
            dve.wait("qs", cw)
            pre(dve, reads=("scrstage",), writes=("scrb",))
            h = dve.tensor_copy(scrb[:, 0:1024], scr[:, 1024:2048])
            post(h, "dve", reads=("scrstage",), writes=("scrb",))
            pre(pe, reads=("scrb",), writes=("pb0", "pb1"))
            for cix in range(8):
                h = mm(pb[cix // 4][:, 128 * (cix % 4):128 * (cix % 4) + 128],
                       scrb[:, 128 * cix:128 * cix + 128], ident[:], True, True)
            post(h, "pe", reads=("scrb",), writes=("pb0", "pb1"))
            pre(act, reads=("pb0", "pb1"), writes=("Wrt",))
            for cix in range(8):
                h = act.copy(Wrt[:, 1024 * cix + 128 * r:1024 * cix + 128 * r + 128],
                             pb[cix // 4][:, 128 * (cix % 4):128 * (cix % 4) + 128])
            post(h, "act", reads=("pb0", "pb1"), writes=("Wrt",))

        pre(qs, writes=("scrstage",))
        h = qs.dma_start(out=scr[:, 1024:1536].rearrange("p (c m) -> p c m", c=8),
                         in_=ap["Win"].rearrange("(c p) m -> p c m", p=128))
        cw = post(h, "qs", writes=("scrstage",))
        dve.wait("qs", cw)
        pre(dve, reads=("scrstage",), writes=("Win2",))
        for cix in range(8):
            h = dve.tensor_copy(Win2[:, 128 * cix:128 * cix + 64],
                                scr[:, 1024 + 64 * cix:1024 + 64 * cix + 64])
            h = dve.tensor_copy(Win2[:, 128 * cix + 64:128 * cix + 128],
                                scr[:, 1024 + 64 * cix:1024 + 64 * cix + 64])
        post(h, "dve", reads=("scrstage",), writes=("Win2",))

        pre(qs, writes=("scrstage",))
        h = qs.dma_start(out=scr[0:64, 1024:2048], in_=ap["Wout"]); h.inc("qs")
        h = qs.dma_start(out=scr[0:1, 0:1024], in_=ap["bout"]); h.inc("qs")
        h = qs.dma_start(out=scr[0:64, 3075:3076], in_=ap["b2col"])
        cw = post(h, "qs", writes=("scrstage",))
        dve.wait("qs", cw)
        pre(dve, reads=("scrstage",), writes=("WoutA", "b2b"))
        h = dve.tensor_scalar(WoutA[0:64, :], scr[0:64, 1024:2048], 0.5, None, ALU.mult)
        post(h, "dve", reads=("scrstage",), writes=("WoutA",))
        h = dve.tensor_copy(statcb[0:64, 0:1], scr[0:64, 3075:3076])
        post(h, "dve", reads=("scrstage",), writes=("b2b",))
        for nh in range(2):
            pre(pe, reads=("WoutA", "b2b"), writes=("pb3",))
            h = mm(pb[3][0:1, 0:512], statcb[0:64, 0:1],
                   WoutA[0:64, 512 * nh:512 * nh + 512], True, True)
            post(h, "pe", reads=("WoutA", "b2b"), writes=("pb3",))
            pre(dve, reads=("pb3", "scrstage"), writes=("WoutArow",))
            h = dve.scalar_tensor_tensor(WoutA[64:65, 512 * nh:512 * nh + 512],
                                         scr[0:1, 512 * nh:512 + 512 * nh], 0.5,
                                         pb[3][0:1, 0:512], ALU.mult, ALU.add)
            post(h, "dve", reads=("pb3", "scrstage"), writes=("WoutArow",))

        def ln_stats(srcp, tag, pp):
            statc = statcd[pp]
            stag = "statc%d" % pp
            pre(dve, reads=(tag,), writes=(stag,))
            h = dve.reduce_sum(statc[:, 0:1], srcp, axis=AX)
            post(h, "dve", reads=(tag,), writes=(stag,))
            pre(dve, writes=(stag,))
            h = dve.memset(statc[:, 1:2], 0.0)
            post(h, "dve", writes=(stag,))
            pre(act, reads=(tag, stag), writes=("sqscr", stag))
            h = act.activation(sqscr[:], srcp, AF.Square, accum_out=statc[:, 1:2])
            post(h, "act", reads=(tag,), writes=(stag, "sqscr"))
            pre(dve, reads=(stag,), writes=(stag,))
            h = dve.tensor_scalar(statc[:, 5:6], statc[:, 0:1], RD, None, ALU.mult)
            post(h, "dve", writes=("mu5_%d" % pp,))
            h = dve.tensor_scalar(statc[:, 2:3], statc[:, 1:2], RD,
                                  EPS, ALU.mult, ALU.add)
            K.sread(dve, "mu5_%d" % pp)
            h = dve.tensor_scalar(statc[:, 3:4], statc[:, 5:6],
                                  statc[:, 5:6], -1.0, ALU.mult, ALU.mult)
            K.fence(dve)
            h = dve.tensor_tensor(statc[:, 3:4], statc[:, 2:3], statc[:, 3:4], ALU.add)
            post(h, "dve", writes=(stag,))
            pre(act, reads=(stag,), writes=(stag,))
            h = act.activation(statc[:, 4:5], statc[:, 3:4], AF.Sqrt)
            post(h, "act", writes=(stag,))
            pre(dve, reads=(stag,), writes=(stag,))
            h = dve.reciprocal(statc[:, 6:7], statc[:, 4:5])
            post(h, "dve", writes=(stag,))
            K.fence(dve)
            h = dve.tensor_scalar(statc[:, 7:8], statc[:, 5:6],
                                  statc[:, 6:7], -1.0, ALU.mult, ALU.mult)
            post(h, "dve", writes=(stag,))

        def ffn_block(xin_ap, xin_tag, xout_ap, xout_tag, pp):
            statc = statcd[pp]
            stag = "statc%d" % pp
            y0, y0fm, Htile = y0d[pp], y0fmd[pp], Htiled[pp]
            gaug, ffm = gaugd[pp], ffmd[pp]
            ty0, ty0fm, tH = "y0_%d" % pp, "y0fm_%d" % pp, "Htile_%d" % pp
            tga, tffm = "gaug_%d" % pp, "ffm_%d" % pp
            pre(act, reads=(xin_tag, stag), writes=(ty0,))
            K.sread(act, stag)
            h = act.activation(y0[:], xin_ap, AF.Identity,
                               scale=statc[:, 6:7], bias=statc[:, 7:8])
            post(h, "act", reads=(xin_tag, stag), writes=(ty0,))
            pre(pe, reads=(ty0,), writes=("pb0", "pb1"))
            for k in range(8):
                h = mm(pb[k // 4][:, 128 * (k % 4):128 * (k % 4) + 128],
                       y0[:, 128 * k:128 * k + 128], ident[:], True, True)
            post(h, "pe", reads=(ty0,), writes=("pb0", "pb1"))
            evac(y0fm[:, 0:512], ty0fm, pb[0][:, 0:512], "pb0")
            evac(y0fm[:, 512:1024], ty0fm, pb[1][:, 0:512], "pb1")
            pre(pe, reads=(ty0fm, "Win2"), writes=("pb2",))
            for k in range(8):
                h = mm(pb[2][:, 0:128], Win2[:, 128 * k:128 * k + 128],
                       y0fm[:, 128 * k:128 * k + 128], k == 0, k == 7)
            post(h, "pe", reads=(ty0fm,), writes=("pb2",))
            pre(act, reads=("pb2", "b1eff"), writes=(tH,))
            for j in range(8):
                h = act.activation(Htile[:, 128 * j:128 * j + 128], pb[2][:, 0:128],
                                   AF.Relu, scale=w1c[:, j:j + 1], bias=b1eff[:, j:j + 1])
            post(h, "act", reads=("pb2",), writes=(tH,))
            pre(pe, reads=(tH, "diagW2"), writes=("pb3",))
            for j in range(8):
                h = mm(pb[3][0:64, 0:128], diagW2[:, 64 * j:64 * j + 64],
                       Htile[:, 128 * j:128 * j + 128], j == 0, j == 7)
            post(h, "pe", reads=(tH,), writes=("pb3",))
            evac(gaug[0:64, :], tga, pb[3][0:64, 0:128], "pb3",
                 extra_reads=("gaug1",))
            pre(pe, reads=(tga, "WoutA", "WoutArow", "gaug1"), writes=("pb4", "pb5"))
            for cix in range(8):
                h = mm(pb[4 + cix // 4][:, 128 * (cix % 4):128 * (cix % 4) + 128],
                       WoutA[:, 128 * cix:128 * cix + 128], gaug[:, 0:128], True, True)
            post(h, "pe", reads=(tga,), writes=("pb4", "pb5"))
            evac(ffm[:, 0:512], tffm, pb[4][:, 0:512], "pb4")
            evac(ffm[:, 512:1024], tffm, pb[5][:, 0:512], "pb5")
            pre(pe, reads=(tffm,), writes=("pw0", "pw1"))
            for cix in range(8):
                h = mm(pwide[:, 128 * cix:128 * cix + 128],
                       ffm[:, 128 * cix:128 * cix + 128], ident[:], True, True)
            post(h, "pe", reads=(tffm,), writes=("pw0", "pw1"))
            pre(dve, reads=("pw0", "pw1", xin_tag), writes=(xout_tag,))
            h = dve.scalar_tensor_tensor(xout_ap, xin_ap, 1.0, pwide[:, 0:1024],
                                         ALU.mult, ALU.add)
            post(h, "dve", reads=("pw0", "pw1", xin_tag), writes=(xout_tag,))

        for t in range(NTILE if stage >= 1 else 0):
            half = t % 2
            xs = xt[:, D * half:D * half + D]
            xtag = "xt%d" % half
            statc = statcd[half]
            stag = "statc%d" % half
            x1t, x1bf, x1fm, zrow = (x1td[half], x1bfd[half], x1fmd[half],
                                     zrowd[half])
            tx1 = "x1t_%d" % half
            pre(qs, writes=(xtag,))
            h = qs.dma_start(out=xs, in_=ap["x_own"][128 * t:128 * t + 128, :])
            cl = post(h, "qs", writes=(xtag,))
            dve.wait("qs", cl)
            act.wait("qs", cl)
            ln_stats(xs, xtag, half)
            if DEBUG and t == 0:
                pre(qs, reads=(xtag, stag))
                h = qs.dma_start(out=dbg_p[:, 0:1024], in_=xs); h.inc("qs")
                h = qs.dma_start(out=dbg_p[:, 1024:1036], in_=statc[:])
                post(h, "qs", reads=(xtag, stag))
            ffn_block(xs, xtag, x1t[:], tx1, half)
            if DEBUG and t == 0:
                pre(qs, reads=(tx1, "y0_0", "y0fm_0", "Htile_0", "gaug_0", "ffm_0"))
                h = qs.dma_start(out=dbg_p[:, 2048:3072], in_=x1t[:]); h.inc("qs")
                h = qs.dma_start(out=dbg_pb[:, 0:1024], in_=y0d[0][:]); h.inc("qs")
                h = qs.dma_start(out=dbg_pb[:, 1024:2048], in_=y0fmd[0][:]); h.inc("qs")
                h = qs.dma_start(out=dbg_pb[:, 2048:3072], in_=Htiled[0][:]); h.inc("qs")
                h = qs.dma_start(out=dbg_pb[0:65, 3072:3200], in_=gaugd[0][:]); h.inc("qs")
                h = qs.dma_start(out=dbg_pb[:, 4096:5120], in_=ffmd[0][:])
                post(h, "qs", reads=(tx1, "y0_0", "y0fm_0", "Htile_0", "gaug_0", "ffm_0"))
            ln_stats(x1t[:], tx1, half)
            pre(dve, reads=(tx1, stag), writes=("zrow%d" % half, stag))
            h = dve.tensor_scalar(statc[:, 2:3], statc[:, 6:7], 1.0, None, ALU.add)
            post(h, "dve", writes=("zp2",))
            K.fence(dve)
            h = dve.tensor_scalar(zrow[:], x1t[:], statc[:, 2:3],
                                  statc[:, 7:8], ALU.mult, ALU.add)
            post(h, "dve", reads=(tx1,), writes=("zrow%d" % half, stag))
            pre(qs, reads=("zrow%d" % half,))
            h = qs.dma_start(out=z_spill[128 * t:128 * t + 128, :], in_=zrow[:])
            post(h, "qs", reads=("zrow%d" % half,))
            pre(dve, reads=(stag,), writes=("statcb",))
            h = dve.tensor_copy(statcb[:, 0:1], statc[:, 5:6])
            h = dve.tensor_copy(statcb[:, 1:2], statc[:, 6:7])
            post(h, "dve", reads=(stag,), writes=("statcb",))
            pre(pe, reads=("statcb",), writes=("pb2",))
            h = mm(pb[2][0:2, 128:256], statcb[:, 0:2], ident[:], True, True)
            post(h, "pe", reads=("statcb",), writes=("pb2",))
            pre(act, reads=("pb2",), writes=("statrow",))
            h = act.copy(statrow[:, 128 * t:128 * t + 128], pb[2][0:2, 128:256])
            post(h, "act", reads=("pb2",), writes=("statrow",))
            pre(dve, reads=(tx1,), writes=("x1bf%d" % half,))
            h = dve.tensor_copy(x1bf[:], x1t[:])
            post(h, "dve", reads=(tx1,), writes=("x1bf%d" % half,))
            pre(pe, reads=("x1bf%d" % half,), writes=("pb0", "pb1"))
            for k in range(8):
                h = mm(pb[k // 4][:, 128 * (k % 4):128 * (k % 4) + 128],
                       x1bf[:, 128 * k:128 * k + 128], ident[:], True, True)
            post(h, "pe", reads=("x1bf%d" % half,), writes=("pb0", "pb1"))
            evac(x1fm[:, 0:512], "x1fm%d" % half, pb[0][:, 0:512], "pb0")
            evac(x1fm[:, 512:1024], "x1fm%d" % half, pb[1][:, 0:512], "pb1")
            pre(qs, reads=("x1fm%d" % half,))
            h = qs.dma_start(
                out=ag_in[0:D, 128 * t:128 * t + 128].rearrange("(k p) t -> p k t", p=128),
                in_=x1fm[:].rearrange("p (k t) -> p k t", k=8))
            post(h, "qs", reads=("x1fm%d" % half,))
        if stage >= 1:
            pre(qs, reads=("statrow",))
            h = qs.dma_start(out=ag_in[D:D + 2, :], in_=statrow[:])
            post(h, "qs", reads=("statrow",))
        c_p1 = K.c("qs")
        if stage >= 2:
            pool.wait("qs", c_p1)
            h = pool.collective_compute(
                "AllGather", ALU.bypass,
                replica_groups=[[0, 1, 2, 3], [4, 5, 6, 7]],
                ins=[ag_in], outs=[ag_out])
            h.inc("cc")
        cc_x1 = K.c("cc")

        pre(dve, writes=("S32",))
        h = dve.memset(S32[:], 0.0)
        post(h, "dve", writes=("S32",))
        pre(act, reads=("S32",), writes=("Shi",))
        h = act.copy(Shi[:], S32[:])
        post(h, "act", reads=("S32",), writes=("Shi",))

        for s in range(NSEG if stage >= 2 else 0):
            qs.wait("cc", cc_x1)
            base = s * (D + 2)
            pre(qs, writes=("statin",))
            h = qs.dma_start(out=statin[:], in_=ag_out[base + D:base + D + 2, :])
            post(h, "qs", writes=("statin",))
            if s == 0:
                pre(dve, writes=("rkcarry",))
                h = dve.memset(rkd0[:, 0:1], 0.0)
                h = dve.memset(rkd1[:, 0:1], 0.0)
                post(h, "dve", writes=("rkcarry",))
            else:
                pre(dve, reads=("rkd",), writes=("rkcarry",))
                h = dve.tensor_copy(rkd0[:, 0:1], rkd0[:, SEG:SEG + 1])
                h = dve.tensor_copy(rkd1[:, 0:1], rkd1[:, SEG:SEG + 1])
                post(h, "dve", reads=("rkd",), writes=("rkcarry",))

            for t in range(NTILE):
                half = t % 2
                xin = x1fmi[:, D * half:D * half + D]
                xtag = "x1fmi%d" % half
                stat2c = stat2cd[half]
                ts2 = "stat2c%d" % half
                t1tok, t1sq, rktok, nrm = (t1tokd[half], t1sqd[half],
                                           rktokd[half], nrmd[half])
                tt1, tsq, trk, tnr = ("t1tok%d" % half, "t1sq%d" % half,
                                      "rktok%d" % half, "nrm%d" % half)
                bv = ("pb0", "pb4")[half]
                bt1 = ("pb1", "pb5")[half]
                baux = ("pb2", "pw0")[half]
                bwk = ("pb3", "pw1")[half]
                pre(qs, reads=("statin",), writes=(xtag,))
                h = qs.dma_start(
                    out=xin.rearrange("p (k t) -> p k t", k=8),
                    in_=ag_out[base:base + D, 128 * t:128 * t + 128]
                        .rearrange("(k p) t -> p k t", p=128))
                cl = post(h, "qs", writes=(xtag,))
                pe.wait("qs", cl)
                pre(pe, reads=("statin",), writes=(baux,))
                h = mm(BK[baux][0:128, 256:258], statin[0:2, 128 * t:128 * t + 128],
                       ident[0:2, 0:2], True, True)
                post(h, "pe", reads=("statin",), writes=(baux,))
                pre(act, reads=(baux,), writes=(ts2,))
                h = act.copy(stat2c[:, 0:2], BK[baux][0:128, 256:258])
                post(h, "act", reads=(baux,), writes=(ts2,))
                pre(pe, reads=(xtag, "Wwt", "statin", "negwwsum"), writes=(bv,))
                for k in range(8):
                    h = mm(BK[bv][:, 0:256], xin[:, 128 * k:128 * k + 128],
                           Wwt[:, 256 * k:256 * k + 256], k == 0, False)
                h = mm(BK[bv][:, 0:256], statin[0:1, 128 * t:128 * t + 128],
                       negwwsum[:], False, True)
                post(h, "pe", reads=(xtag, "statin"), writes=(bv,))
                pre(act, reads=(bv, ts2), writes=("vtok",))
                K.sread(act, ts2)
                h = act.activation(vtok[:, 256 * t:256 * t + 256], BK[bv][:, 0:256],
                                   AF.Identity, scale=stat2c[:, 1:2])
                post(h, "act", reads=(bv, ts2), writes=("vtok",))
                pre(pe, reads=(xtag, "EyeH", "statin"), writes=(bt1,))
                for k in range(8):
                    h = mm(BK[bt1][:, 0:256], xin[:, 128 * k:128 * k + 128],
                           EyeH[:, 256 * k:256 * k + 256], k == 0, False)
                h = mm(BK[bt1][:, 0:256], statin[0:1, 128 * t:128 * t + 128],
                       negones_row[:], False, True)
                post(h, "pe", reads=(xtag, "statin"), writes=(bt1,))
                evac(t1tok[:], tt1, BK[bt1][:, 0:256], bt1)
                pre(pool, reads=(tt1,), writes=(tsq,))
                h = pool.tensor_tensor(t1sq[:], t1tok[:], t1tok[:], ALU.mult)
                post(h, "pool", reads=(tt1,), writes=(tsq,))
                pre(dve, reads=(tsq,), writes=(tnr,))
                h = dve.reduce_sum(nrm[:, 0:1], t1sq[:], axis=AX)
                post(h, "dve", reads=(tsq,), writes=(tnr,))
                pre(act, reads=(tnr,), writes=(tnr,))
                h = act.activation(nrm[:, 1:2], nrm[:, 0:1], AF.Sqrt)
                post(h, "act", writes=(tnr,))
                pre(dve, reads=(tnr,), writes=(tnr,))
                h = dve.tensor_scalar(nrm[:, 2:3], nrm[:, 1:2], 1e-12, None, ALU.max)
                K.fence(dve)
                h = dve.reciprocal(nrm[:, 3:4], nrm[:, 2:3])
                post(h, "dve", writes=(tnr,))
                pre(act, reads=(tnr, tt1), writes=(trk,))
                K.sread(act, tnr)
                h = act.activation(rktok[:], t1tok[:], AF.Identity, scale=nrm[:, 3:4])
                post(h, "act", reads=(tnr, tt1), writes=(trk,))
                pre(pe, reads=(trk,), writes=(baux,))
                h = mm(BK[baux][:, 0:128], rktok[:, 0:128], ident[:], True, True)
                h = mm(BK[baux][:, 128:256], rktok[:, 128:256], ident[:], True, True)
                post(h, "pe", reads=(trk,), writes=(baux,))
                pre(act, reads=(baux,), writes=("rkd",))
                h = act.copy(rkd0[:, 128 * t + 1:128 * t + 129], BK[baux][:, 0:128])
                post(h, "act", reads=(baux,), writes=("rkd",))
                pre(dve, reads=(baux,), writes=("rkd",))
                h = dve.tensor_copy(rkd1[:, 128 * t + 1:128 * t + 129],
                                    BK[baux][:, 128:256])
                post(h, "dve", reads=(baux,), writes=("rkd",))
                pre(dve, reads=("rkd", "gpcol"), writes=("rkg",))
                h = dve.tensor_tensor(rkg0[:, 128 * t:128 * t + 128],
                                      rkd0[:, 128 * t + 1:128 * t + 129], gpcol[:], ALU.mult)
                h = dve.tensor_tensor(rkg1[:, 128 * t:128 * t + 128],
                                      rkd1[:, 128 * t + 1:128 * t + 129], gpcol[:], ALU.mult)
                post(h, "dve", reads=("rkd",), writes=("rkg",))
                rds = ("rkd", "rkcarry") if t == 0 else ("rkd",)
                pre(pe, reads=rds, writes=(bwk,))
                h = mm(BK[bwk][:, 0:128], rkd0[:, 128 * t:128 * t + 128], ident[:], True, True)
                h = mm(BK[bwk][:, 128:256], rkd1[:, 128 * t:128 * t + 128], ident[:], True, True)
                post(h, "pe", reads=rds, writes=(bwk,))
                evac(wktok[:, 256 * t:256 * t + 256], "wktok", BK[bwk][:, 0:256], bwk)

            psub = PSUB
            for p in range(NTILE if stage >= 3 else 0):
                half = p % 2
                powbuf, ptprev = powbufd[half], ptprevd[half]
                Xba, PTgC = Xbad[half], PTgCd[half]
                tpw, tpt = "powbuf%d" % half, "ptprev%d" % half
                txb, tgc = "Xba%d" % half, "PTgC%d" % half
                bG = ("pb0", "pb4")[half]
                bPT = ("pb1", "pb5")[half]
                bSQ = ("pb2", "pw1")[half]
                bWA = ("pb3", "pb0")[half]
                bWB = ("pw0", "pb2")[half]
                wk0 = rkd0[:, 128 * p:128 * p + 128]
                wk1 = rkd1[:, 128 * p:128 * p + 128]
                rk0 = rkd0[:, 128 * p + 1:128 * p + 129]
                rk1 = rkd1[:, 128 * p + 1:128 * p + 129]
                rds = ("rkd", "rkcarry") if p == 0 else ("rkd",)
                pre(pe, reads=rds, writes=(bG,))
                h = mm(BK[bG][:, 0:128], wk0, wk0, True, False)
                h = mm(BK[bG][:, 0:128], wk1, wk1, False, True)
                h = mm(BK[bG][:, 128:256], wk0, rk0, True, False)
                h = mm(BK[bG][:, 128:256], wk1, rk1, False, True)
                post(h, "pe", reads=rds, writes=(bG,))
                pre(dve, reads=(bG, "msk0", "msk1", "msk2", "msk3"),
                    writes=(tpw, "ITXp", txb))
                h = dve.tensor_tensor(powbuf[:, 0:128], BK[bG][:, 0:128], maskN[:], ALU.mult)
                h = dve.tensor_tensor(powbuf[:, 128:256], BK[bG][:, 0:128], maskNT[:], ALU.mult)
                h = dve.tensor_tensor(ITXp[:, 128 * p:128 * p + 128], BK[bG][:, 128:256],
                                      maskITX[:], ALU.mult)
                h = dve.tensor_tensor(Xba[64:128, 0:64], BK[bG][64:128, 0:64],
                                      maskX[64:128, 0:64], ALU.mult)
                post(h, "dve", reads=(bG,), writes=(tpw, "ITXp", txb))
                pre(pe, reads=(tpw,), writes=(bPT,))
                h = mm(BK[bPT][:, 0:128], ident[:], ident[:], True, False)
                h = mm(BK[bPT][:, 0:128], powbuf[:, 0:128], ident[:], False, False)
                post(h, "pe", reads=(tpw,), writes=(bPT,))
                evac(ptprev[:], tpt, BK[bPT][:, 0:128], bPT)
                for lvl in range(5):
                    pre(pe, reads=(tpw,), writes=(bSQ,))
                    h = mm(BK[bSQ][:, 0:128], powbuf[:, 128:256], powbuf[:, 0:128],
                           True, True)
                    if lvl < 4:
                        h = mm(BK[bSQ][:, 128:256], powbuf[:, 0:128], powbuf[:, 128:256],
                               True, True)
                    post(h, "pe", reads=(tpw,), writes=(bSQ,))
                    w = 256 if lvl < 4 else 128
                    evac(powbuf[:, 0:w], tpw, BK[bSQ][:, 0:w], bSQ)
                    pre(pe, reads=(tpw, tpt), writes=())
                    h = mm(BK[bPT][:, 0:128], powbuf[:, 0:128], ptprev[:],
                           False, lvl == 4)
                    post(h, "pe", reads=(tpw, tpt), writes=(bPT,))
                    if lvl < 4:
                        evac(ptprev[:], tpt, BK[bPT][:, 0:128], bPT)
                    else:
                        evac(PTp[:, 128 * p:128 * p + 128], "PTp",
                             BK[bPT][:, 0:128], bPT)
                pre(dve, reads=("PTp", "gCcol"), writes=(tgc,))
                h = dve.tensor_scalar(PTgC[64:128, 0:64],
                                      PTp[64:128, 128 * p + 64:128 * p + 128],
                                      gCcol[64:128, 0:1], None, ALU.mult)
                post(h, "dve", reads=("PTp",), writes=(tgc,))
                pre(pe, reads=("wktok", "PTp", tgc), writes=(bWA, bWB))
                h = mm(BK[bWA][:, 0:64], wktok[0:64, 256 * p:256 * p + 128],
                       PTp[0:64, 128 * p:128 * p + 64], True, True)
                h = mm(BK[bWA][:, 64:128], wktok[0:64, 256 * p + 128:256 * p + 256],
                       PTp[0:64, 128 * p:128 * p + 64], True, True)
                h = mm(BK[bWB][:, 0:64], wktok[64:128, 256 * p:256 * p + 128],
                       PTgC[64:128, 0:64], True, True)
                h = mm(BK[bWB][:, 64:128], wktok[64:128, 256 * p + 128:256 * p + 256],
                       PTgC[64:128, 0:64], True, True)
                post(h, "pe", reads=("wktok", "PTp", tgc), writes=(bWA, bWB))
                evac(wkcT0[:, 128 * p:128 * p + 64], "wkcT", BK[bWA][:, 0:64], bWA,
                     scale=-1.0)
                evac(wkcT1[:, 128 * p:128 * p + 64], "wkcT", BK[bWA][:, 64:128], bWA,
                     scale=-1.0)
                evac(wkcT0[:, 128 * p + 64:128 * p + 128], "wkcT", BK[bWB][:, 0:64],
                     bWB, scale=-1.0)
                evac(wkcT1[:, 128 * p + 64:128 * p + 128], "wkcT", BK[bWB][:, 64:128],
                     bWB, scale=-1.0)
                pre(pe, reads=(txb, "PTp"), writes=(bWB,))
                h = mm(BK[bWB][0:64, 256:320], Xba[64:128, 0:64],
                       PTp[64:128, 128 * p + 64:128 * p + 128], True, True)
                post(h, "pe", reads=(txb, "PTp"), writes=(bWB,))
                evac(MbaTn[0:64, 64 * p:64 * p + 64], "MbaTn",
                     BK[bWB][0:64, 256:320], bWB, scale=-1.0)

            for p in range(NTILE if stage >= 4 else 0):
                pre(pe, reads=("PTp", "vtok", "wkcT", "Shi"), writes=("pb0",))
                h = mm(pb[0][0:64, 0:256], PTp[0:64, 128 * p:128 * p + 64],
                       vtok[0:64, 256 * p:256 * p + 256], True, False)
                h = mm(pb[0][0:64, 0:256], wkcT0[:, 128 * p:128 * p + 64],
                       Shi[:, 0:256], False, False)
                h = mm(pb[0][0:64, 0:256], wkcT1[:, 128 * p:128 * p + 64],
                       Shi[:, 256:512], False, True)
                post(h, "pe", reads=("PTp", "vtok", "wkcT", "Shi"), writes=("pb0",))
                pre(act, reads=("pb0",), writes=("vnew_a",))
                h = act.copy(vnewsb[0:64, :], pb[0][0:64, 0:256])
                post(h, "act", reads=("pb0",), writes=("vnew_a",))
                pre(pe, reads=("PTp", "vtok", "wkcT", "Shi", "MbaTn", "vnew_a"),
                    writes=("pb1",))
                h = mm(pb[1][64:128, 0:256], PTp[64:128, 128 * p + 64:128 * p + 128],
                       vtok[64:128, 256 * p:256 * p + 256], True, False, tp=(64, 64))
                h = mm(pb[1][64:128, 0:256], wkcT0[:, 128 * p + 64:128 * p + 128],
                       Shi[:, 0:256], False, False, tp=(0, 64))
                h = mm(pb[1][64:128, 0:256], wkcT1[:, 128 * p + 64:128 * p + 128],
                       Shi[:, 256:512], False, False, tp=(0, 64))
                h = mm(pb[1][64:128, 0:256], MbaTn[0:64, 64 * p:64 * p + 64],
                       vnewsb[0:64, :], False, True, tp=(0, 64))
                post(h, "pe", reads=("PTp", "vtok", "wkcT", "Shi", "MbaTn", "vnew_a"),
                     writes=("pb1",))
                pre(act, reads=("pb1",), writes=("vnew_b",))
                h = act.copy(vnewsb[64:128, :], pb[1][64:128, 0:256])
                post(h, "act", reads=("pb1",), writes=("vnew_b",))
                pre(dve, reads=("pb0", "pb1", "gtpair"), writes=("vssb",))
                h = dve.tensor_scalar(vssb[0:64, :], pb[0][0:64, 0:256],
                                      gtpair[0:64, 0:1], None, ALU.mult)
                h = dve.tensor_scalar(vssb[64:128, :], pb[1][64:128, 0:256],
                                      gtpair[64:128, 0:1], None, ALU.mult)
                post(h, "dve", reads=("pb0", "pb1"), writes=("vssb",))
                pre(pe, reads=("rkg", "Shi", "ITXp", "vnew_a", "vnew_b"), writes=("pb2",))
                h = mm(pb[2][:, 0:256], rkg0[:, 128 * p:128 * p + 128], Shi[:, 0:256],
                       True, False)
                h = mm(pb[2][:, 0:256], rkg1[:, 128 * p:128 * p + 128], Shi[:, 256:512],
                       False, False)
                h = mm(pb[2][:, 0:256], ITXp[:, 128 * p:128 * p + 128], vnewsb[:],
                       False, True)
                post(h, "pe", reads=("rkg", "Shi", "ITXp", "vnew_a", "vnew_b"),
                     writes=("pb2",))
                pre(act, reads=("pb2", "alphacol"), writes=("osb",))
                h = act.activation(osb[:], pb[2][:, 0:256], AF.Identity,
                                   scale=alphacol[:, 0:1])
                post(h, "act", reads=("pb2",), writes=("osb",))
                pre(pool, reads=("osb",))
                row = 128 * (NTILE * s + p)
                h = pool.dma_start(out=ago_in[row:row + 128, :], in_=osb[:])
                post(h, "qg", reads=("osb",))
                pre(pe, reads=("wktok", "vssb"), writes=("pb3",))
                h = mm(pb[3][:, 0:256], wktok[:, 256 * p:256 * p + 128], vssb[:],
                       True, True)
                h = mm(pb[3][:, 256:512], wktok[:, 256 * p + 128:256 * p + 256], vssb[:],
                       True, True)
                post(h, "pe", reads=("wktok", "vssb"), writes=("pb3",))
                pre(dve, reads=("pb3", "gC2col", "S32"), writes=("S32",))
                K.fence(dve)
                h = dve.scalar_tensor_tensor(S32[:], S32[:], gC2col[:, 0:1],
                                             pb[3][:, 0:512], ALU.mult, ALU.add)
                post(h, "dve", reads=("pb3",), writes=("S32",))
                pre(act, reads=("S32",), writes=("Shi",))
                h = act.copy(Shi[:], S32[:])
                post(h, "act", reads=("S32",), writes=("Shi",))

        if stage >= 5:
            pool.wait("qg", K.c("qg"))
            h = pool.collective_compute(
                "AllGather", ALU.bypass,
                replica_groups=[[0, 1, 2, 3], [4, 5, 6, 7]],
                ins=[ago_in], outs=[ago_out])
            h.inc("cc")
        cc_o = K.c("cc")

        qs.wait("cc", cc_o)
        for t in range(NTILE if stage >= 5 else 0):
            half = t % 2
            zt = ztile[:, D * half:D * half + D]
            ztag = "ztile%d" % half
            oblk = oblkd[half]
            o_fm = o_fmd[half]
            x2t = x2td[half]
            tob, tofm, tx2 = "oblk%d" % half, "o_fm%d" % half, "x2t_%d" % half
            btr = (("pb0", "pb1"), ("pb4", "pb5"))[half]
            pre(qs, writes=(tob, ztag))
            for j in range(4):
                for k in range(4):
                    row = j * T + k * SEG + 128 * t
                    h = qs.dma_start(out=oblk[:, HD * (4 * j + k):HD * (4 * j + k) + HD],
                                     in_=ago_out[row:row + 128, :])
                    h.inc("qs")
            h = qs.dma_start(out=zt, in_=z_spill[128 * t:128 * t + 128, :])
            cl = post(h, "qs", writes=(tob, ztag))
            pe.wait("qs", cl)
            pre(pe, reads=(tob, "maskI"), writes=btr)
            for j in range(4):
                for r in range(2):
                    cix = 2 * j + r
                    bank = BK[btr[cix // 4]]
                    off = 128 * (cix % 4)
                    for k in range(4):
                        h = mm(bank[:, off:off + 128],
                               oblk[:, HD * (4 * j + k) + 128 * r:
                                    HD * (4 * j + k) + 128 * r + 128],
                               maskI[:, 128 * k:128 * k + 128], k == 0, k == 3)
            post(h, "pe", reads=(tob, "maskI"), writes=btr)
            evac(o_fm[:, 0:512], tofm, BK[btr[0]][:, 0:512], btr[0])
            evac(o_fm[:, 512:1024], tofm, BK[btr[1]][:, 0:512], btr[1])
            pre(pe, reads=(tofm, "Wrt"), writes=("pw0", "pw1"))
            for cix in range(8):
                for nh in range(2):
                    h = mm(pwide[:, 512 * nh:512 * nh + 512],
                           o_fm[:, 128 * cix:128 * cix + 128],
                           Wrt[:, 1024 * cix + 512 * nh:1024 * cix + 512 * nh + 512],
                           cix == 0, cix == 7)
            post(h, "pe", reads=(tofm, "Wrt"), writes=("pw0", "pw1"))
            pre(dve, reads=("pw0", "pw1", ztag), writes=(tx2,))
            h = dve.scalar_tensor_tensor(x2t[:], zt, 1.0, pwide[:, 0:1024],
                                         ALU.mult, ALU.add)
            post(h, "dve", reads=("pw0", "pw1", ztag), writes=(tx2,))
            ln_stats(x2t[:], tx2, half)
            ffn_block(x2t[:], tx2, x1td[half][:], "x1t_%d" % half, half)
            pre(qs, reads=("x1t_%d" % half,))
            h = qs.dma_start(out=out_own[128 * t:128 * t + 128, :], in_=x1td[half][:])
            post(h, "qs", reads=("x1t_%d" % half,))
        if DEBUG:
            qs.wait("qs", K.c("qs"))
            qs.wait("qg", K.c("qg"))
            h = qs.dma_start(out=dbg_ag, in_=ag_in); h.inc("qs")
            h = qs.dma_start(out=dbg_o, in_=ago_in); h.inc("qs")
            h = qs.dma_start(out=dbg_z, in_=z_spill); h.inc("qs")
        if stage < 5:
            pre(dve, writes=("zrow",))
            h = dve.memset(zrow[:], 0.0)
            post(h, "dve", writes=("zrow",))
            for t in range(NTILE):
                pre(qs, reads=("zrow",))
                h = qs.dma_start(out=out_own[128 * t:128 * t + 128, :], in_=zrow[:])
                post(h, "qs", reads=("zrow",))
        qs.wait("qs", K.c("qs"))
        pool.wait("qg", K.c("qg"))

        K.emit(nc, sems)
    return nc


_NC_CACHE = None


def _build_in_maps(inputs):
    x = np.asarray(inputs["L0_attn_out"], dtype=np.float32)
    Ww = np.asarray(inputs["L0_attn_Ww"], dtype=np.float32)
    Wr = np.asarray(inputs["L0_attn_Wr"], dtype=np.float32)
    decay = np.asarray(inputs["L0_attn_decay"], dtype=np.float32)
    log_alpha = np.asarray(inputs["L0_attn_log_alpha"], dtype=np.float32)
    Win = np.asarray(inputs["L0_ffn_Win"], dtype=np.float32)
    bin_ = np.asarray(inputs["L0_ffn_bin_"], dtype=np.float32)
    W1 = np.asarray(inputs["L0_ffn_W1"], dtype=np.float32)
    b1 = np.asarray(inputs["L0_ffn_b1"], dtype=np.float32)
    W2 = np.asarray(inputs["L0_ffn_W2"], dtype=np.float32)
    b2 = np.asarray(inputs["L0_ffn_b2"], dtype=np.float32)
    Wout = np.asarray(inputs["L0_ffn_Wout"], dtype=np.float32)
    bout = np.asarray(inputs["L0_ffn_bout"], dtype=np.float32)

    consts = _host_consts()
    W1c = _w_cols(W1)
    b1c = _w_cols(b1)
    W2c = _w_cols(W2)
    binc = np.ascontiguousarray(bin_[np.arange(128) % 64][:, None].astype(np.float32))
    in_maps = []
    for c in range(NCORE):
        b, h = c // 4, c % 4
        s = c % 4
        eyeh = np.zeros((128, 8 * HD), dtype=np.float32)
        eyeh[np.arange(128), HD * 2 * h + np.arange(128)] = 1.0
        eyeh[np.arange(128), HD * (2 * h + 1) + 128 + np.arange(128)] = 1.0
        m = {
            "x_own": np.ascontiguousarray(x[b, s * SEG:(s + 1) * SEG, :]),
            "Wwh": np.ascontiguousarray(Ww[h * HD:(h + 1) * HD, :]),
            "Wr": Wr,
            "Win": Win,
            "W1c": W1c, "b1c": b1c, "W2c": W2c, "binc": binc,
            "Wout": Wout, "bout": np.ascontiguousarray(bout[None, :]),
            "b2col": np.ascontiguousarray(b2[:, None]),
            "decay_h": np.ascontiguousarray(decay[h:h + 1][:, None]),
            "log_alpha_h": np.ascontiguousarray(log_alpha[h:h + 1][:, None]),
            "segmask": np.ascontiguousarray(
                (np.arange(4) == s).astype(np.float32)[None, :]),
            "EyeH": eyeh.astype(bf16),
        }
        for kk, v in consts.items():
            m[kk] = v
        in_maps.append(m)
    return in_maps


def kernel(**inputs):
    global _NC_CACHE
    if _NC_CACHE is None:
        _NC_CACHE = build_nc()
    in_maps = _build_in_maps(inputs)
    res = run_bass_kernel_spmd(_NC_CACHE, in_maps, core_ids=list(range(NCORE)))
    out = np.zeros((B, T, D), dtype=np.float32)
    for c in range(NCORE):
        b, s = c // 4, c % 4
        out[b, s * SEG:(s + 1) * SEG, :] = res.results[c]["out_own"]
    return out


# version 05
# speedup vs baseline: 1.0598x; 1.0598x over previous
import contextlib
import numpy as np
import ml_dtypes
import concourse.bass as bass
import concourse.mybir as mybir
from concourse.bass_utils import run_bass_kernel_spmd

F32 = mybir.dt.float32
BF16 = mybir.dt.bfloat16
AF = mybir.ActivationFunctionType
ALU = mybir.AluOpType
AX = mybir.AxisListType.X
bf16 = ml_dtypes.bfloat16

B, T, D, H, HD, C = 2, 8192, 1024, 4, 256, 64
PAIR = 2 * C
SEG = 2048
NSEG = 4
NTILE = SEG // 128
NCORE = 8
EPS = 1e-5
RD = 1.0 / D


class _Op:
    __slots__ = ("eng", "kind", "attr", "args", "kwargs", "incs", "sem", "val")


class _Handle:
    def __init__(self, ker, op):
        self.ker, self.op = ker, op
        self.vals = {}

    def inc(self, sem, amt=None):
        if sem in self.vals:
            return self.vals[sem]
        amt = amt if amt is not None else (16 if sem.startswith("q") else 1)
        self.op.incs.append((sem, amt))
        self.ker.cnt[sem] = self.ker.cnt.get(sem, 0) + amt
        self.vals[sem] = self.ker.cnt[sem]
        return self.vals[sem]


class EngQ:
    OWN = {"pe": "pe"}
    AUTOINC = ("act", "dve", "pool")
    NOINC = ("dma_start", "collective_compute", "wait_ge")

    def __init__(self, ker, name):
        self._ker, self._name = ker, name

    def __getattr__(self, attr):
        ker, name = self._ker, self._name

        def call(*args, **kwargs):
            op = _Op()
            op.eng, op.kind, op.attr, op.args, op.kwargs, op.incs = (
                name, "ins", attr, args, kwargs, [])
            ker.ops.append(op)
            h = _Handle(ker, op)
            if name in EngQ.AUTOINC and attr not in EngQ.NOINC:
                h.inc(name)
            return h

        return call

    def wait(self, sem, val, hard=False):
        ker, name = self._ker, self._name
        if val <= 0:
            return
        if not hard and EngQ.OWN.get(name) == sem:
            return
        key = (name, sem)
        if ker.wm.get(key, 0) >= val:
            return
        ker.wm[key] = val
        op = _Op()
        op.eng, op.kind, op.sem, op.val = name, "wait", sem, val
        ker.ops.append(op)


class Res:
    __slots__ = ("writer", "readers")

    def __init__(self):
        self.writer = None
        self.readers = []


class Ker:
    def __init__(self):
        self.ops = []
        self.cnt = {}
        self.wm = {}
        self.res = {}
        for n in ("pe", "act", "dve", "pool", "qs"):
            setattr(self, n, EngQ(self, n))

    def R(self, name):
        if name not in self.res:
            self.res[name] = Res()
        return self.res[name]

    def c(self, sem):
        return self.cnt.get(sem, 0)

    def fence(self, q):
        name = q._name
        q.wait(name, self.cnt.get(name, 0))

    def sread(self, q, *names):
        for r in names:
            w = self.R(r).writer
            if w:
                q.wait(w[0], w[1], hard=True)

    def pre(self, q, reads=(), writes=()):
        for r in reads:
            w = self.R(r).writer
            if w:
                q.wait(*w)
        for wname in writes:
            rr = self.R(wname)
            for rd in rr.readers:
                q.wait(*rd)
            rr.readers = []
            if rr.writer:
                q.wait(*rr.writer)

    def post(self, h, sem, reads=(), writes=()):
        v = h.inc(sem)
        for r in reads:
            self.R(r).readers.append((sem, v))
        for w in writes:
            rr = self.R(w)
            rr.writer = (sem, v)
            rr.readers = []
        return v

    def emit(self, nc, sems):
        by_eng = {"pe": [], "act": [], "dve": [], "pool": [], "qs": []}
        for op in self.ops:
            by_eng[op.eng].append(op)

        def runner(eng, ops):
            for op in ops:
                if op.kind == "wait":
                    eng.wait_ge(sems[op.sem], op.val)
                else:
                    ins = getattr(eng, op.attr)(*op.args, **op.kwargs)
                    for (sname, amt) in op.incs:
                        ins.then_inc(sems[sname], amt)

        with nc.Block() as block:
            @block.tensor
            def _(eng):
                runner(eng, by_eng["pe"])

            @block.scalar
            def _(eng):
                runner(eng, by_eng["act"])

            @block.vector
            def _(eng):
                runner(eng, by_eng["dve"])

            @block.gpsimd
            def _(eng):
                runner(eng, by_eng["pool"])

            @block.sync
            def _(eng):
                runner(eng, by_eng["qs"])


def _host_consts():
    i_ = np.arange(PAIR)[:, None].astype(np.int64)
    j_ = np.arange(PAIR)[None, :].astype(np.int64)
    same = (i_ // C) == (j_ // C)
    strict = same & (i_ > j_)
    big = 1.0e4
    diffN = np.where(strict, (i_ - j_).astype(np.float64), big).astype(np.float32)
    signN = np.where(strict, -1.0, 0.0).astype(np.float32)
    diffNT = np.ascontiguousarray(diffN.T)
    signNT = np.ascontiguousarray(signN.T)
    diffITX = np.where(strict.T, (i_.T - j_.T).astype(np.float64), big)
    signITX = np.where(strict.T, 1.0, 0.0)
    jj = np.arange(C)[:, None]
    ii = np.arange(C)[None, :]
    diffITX[0:C, C:PAIR] = 63 - jj + ii
    signITX[0:C, C:PAIR] = 1.0
    diffX = np.full((PAIR, PAIR), big)
    diffX[C:PAIR, 0:C] = np.tile((63 - np.arange(C))[None, :], (C, 1))
    signX = np.zeros((PAIR, PAIR))
    signX[C:PAIR, 0:C] = 1.0
    colidx = np.tile(np.arange(PAIR, dtype=np.float32)[None, :], (PAIR, 1))
    c127 = (127.0 - np.arange(PAIR, dtype=np.float32))[:, None]
    c64 = np.full((PAIR, 1), 64.0, np.float32)
    c128 = np.full((PAIR, 1), 128.0, np.float32)
    return dict(
        diffN=diffN, signN=signN, diffNT=diffNT, signNT=signNT,
        diffITX=diffITX.astype(np.float32), signITX=signITX.astype(np.float32),
        diffX=diffX.astype(np.float32), signX=signX.astype(np.float32),
        colidx=colidx, c127=c127, c64=c64, c128=c128,
        ident=np.eye(PAIR, dtype=np.float32).astype(bf16),
        foldF=(np.arange(128)[:, None] % 64 == np.arange(64)[None, :]
               ).astype(np.float32).astype(bf16),
        ones_row=np.ones((1, PAIR), np.float32).astype(bf16),
        ones_col=np.ones((PAIR, 1), np.float32).astype(bf16),
        negones_row=np.full((1, HD), -1.0, np.float32).astype(bf16),
    )


_CONST_DTYPES = {"ident": BF16, "ones_row": BF16, "ones_col": BF16,
                 "negones_row": BF16, "foldF": BF16}


def _w_cols(Wx):
    out = np.zeros((128, 8), dtype=np.float32)
    p = np.arange(128)
    for j in range(8):
        out[:, j] = Wx[p % 64, 2 * j + p // 64]
    return out


STAGE = 5
PSUB = 99
DEBUG = False
OAG_PIECED = True


def build_nc(stage=None):
    stage = STAGE if stage is None else stage
    nc = bass.Bass("TRN2", target_bir_lowering=False, debug=False)
    ap = {}

    def din(name, shape, dtype=F32):
        ap[name] = nc.dram_tensor(name, shape, dtype, kind="ExternalInput").ap()
        return ap[name]

    din("x_own", [SEG, D])
    din("Wwh", [HD, D])
    din("Wr", [D, D])
    din("Win", [D, 64])
    din("W1c", [128, 8]); din("b1c", [128, 8]); din("W2c", [128, 8])
    din("binc", [128, 1])
    din("Wout", [64, D]); din("bout", [1, D]); din("b2col", [64, 1])
    din("decay_h", [1, 1]); din("log_alpha_h", [1, 1])
    din("segmask", [1, 4])
    din("EyeH", [128, 8 * HD], BF16)
    for k, v in _host_consts().items():
        din(k, list(v.shape), _CONST_DTYPES.get(k, F32))

    out_own = nc.dram_tensor("out_own", [SEG, D], F32, kind="ExternalOutput").ap()
    if DEBUG:
        dbg_ag = nc.dram_tensor("dbg_ag", [D + 2, SEG], BF16, kind="ExternalOutput").ap()
        dbg_o = nc.dram_tensor("dbg_o", [T, HD], BF16, kind="ExternalOutput").ap()
        dbg_z = nc.dram_tensor("dbg_z", [SEG, D], F32, kind="ExternalOutput").ap()
        dbg_p = nc.dram_tensor("dbg_p", [128, 4096], F32, kind="ExternalOutput").ap()
        dbg_pb = nc.dram_tensor("dbg_pb", [128, 8192], BF16, kind="ExternalOutput").ap()

    ag_in = nc.dram_tensor("ag_in", [4 * (D + 2), 512], BF16).ap()
    ag_out = nc.dram_tensor("ag_out", [16 * (D + 2), 512], BF16).ap()
    ago_in_l = [nc.dram_tensor("ago_in%d" % i, [SEG, HD], BF16).ap()
                for i in range(NSEG)]
    ago_out = nc.dram_tensor("ago_out", [NSEG * T, HD], BF16).ap()
    z_spill = nc.dram_tensor("z_spill", [SEG, D], F32).ap()
    bc_buf = nc.dram_tensor("bc_buf", [8, 1], F32).ap()

    K = Ker()
    sem_names = ["pe", "act", "dve", "pool", "qs", "qg", "cc"]

    with contextlib.ExitStack() as ctx:
        def sb(name, shape, dtype):
            return ctx.enter_context(nc.sbuf_tensor(name, shape, dtype))

        def psum(name, shape):
            return ctx.enter_context(nc.psum_tensor(name, shape, F32))

        sems = {n: ctx.enter_context(nc.semaphore(n)) for n in sem_names}

        ident = sb("identb", [128, 128], BF16)
        ones_row = sb("onesrow", [1, 128], BF16)
        ones_col = sb("onescol", [128, 1], BF16)
        negones_row = sb("negonesrow", [1, HD], BF16)
        EyeH = sb("EyeHs", [128, 8 * HD], BF16)
        Wwt = sb("Wwt", [128, 8 * HD], BF16)
        negwwsum = sb("negwwsum", [1, HD], BF16)
        Wrt = sb("Wrt", [128, 8 * D], BF16)
        Win2 = sb("Win2", [128, 8 * 128], BF16)
        WoutA = sb("WoutA", [65, 8 * 128], BF16)
        w1c = sb("w1c", [128, 8], F32)
        b1eff = sb("b1eff", [128, 8], F32)
        diagW2 = sb("diagW2", [128, 8 * 64], BF16)
        foldF = sb("foldFs", [128, 64], BF16)
        maskN = sb("maskN", [128, 128], F32)
        maskNT = sb("maskNT", [128, 128], F32)
        maskITX = sb("maskITX", [128, 128], F32)
        maskX = sb("maskX", [128, 128], F32)
        gpcol = sb("gpcol", [128, 128], F32)
        gtpair = sb("gtpair", [128, 1], F32)
        gC2col = sb("gC2col", [128, 1], F32)
        gCcol = sb("gCcol", [128, 1], F32)
        alphacol = sb("alphacol", [128, 1], F32)
        lngcol = sb("lngcol", [128, 1], F32)
        maskI = sb("maskI", [128, 4 * 128], BF16)
        smallA = sb("smallA", [1, 8], F32)
        smallB = sb("smallB", [1, 8], F32)

        def sbd(name, shape, dtype):
            return [sb(name + "A", shape, dtype), sb(name + "B", shape, dtype)]

        xt = sb("xt", [128, 2 * D], F32)
        sqscr = sb("sqscr", [128, D], BF16)
        y0d = sbd("y0", [128, D], BF16)
        y0fmd = sbd("y0fm", [128, D], BF16)
        Htiled = sbd("Htile", [128, 8 * 128], BF16)
        gaugd = sbd("gaug", [65, 128], BF16)
        ffmd = sbd("ffm", [128, D], BF16)
        x1td = sbd("x1t", [128, D], F32)
        x1bfd = sbd("x1bf", [128, D], BF16)
        x1fmd = sbd("x1fm", [128, D], BF16)
        statcd = sbd("statc", [128, 12], F32)
        statcb = sb("statcb", [128, 2], BF16)
        statrow = sb("statrow", [2, SEG], BF16)
        zrowd = sbd("zrow", [128, D], F32)

        x1fmi = sb("x1fmi", [128, 2 * D], BF16)
        statin = sb("statin", [2, SEG], BF16)
        stat2cd = sbd("stat2c", [128, 4], F32)
        vtok = sb("vtok", [128, NTILE * HD], BF16)
        wktok = sb("wktok", [128, NTILE * HD], BF16)
        rkd0 = sb("rkd0", [128, SEG + 1], BF16)
        rkd1 = sb("rkd1", [128, SEG + 1], BF16)
        rkg0 = sb("rkg0", [128, SEG], BF16)
        rkg1 = sb("rkg1", [128, SEG], BF16)
        wkcT0 = sb("wkcT0", [128, SEG], BF16)
        wkcT1 = sb("wkcT1", [128, SEG], BF16)
        t1tokd = sbd("t1tok", [128, HD], BF16)
        t1sqd = sbd("t1sq", [128, HD], BF16)
        rktokd = sbd("rktok", [128, HD], BF16)
        nrmd = sbd("nrm", [128, 4], F32)
        PTp = sb("PTp", [128, NTILE * 128], BF16)
        PTgCd = sbd("PTgC", [128, 64], BF16)
        ITXp = sb("ITXp", [128, NTILE * 128], BF16)
        Xbad = sbd("Xba", [128, 64], BF16)
        MbaTn = sb("MbaTn", [64, NTILE * 64], BF16)
        powbufd = sbd("powbuf", [128, 256], BF16)
        ptprevd = sbd("ptprev", [128, 128], BF16)
        vnewsb = sb("vnewsb", [128, HD], BF16)
        vssb = sb("vssb", [128, HD], BF16)
        osb = sb("osb", [128, HD], BF16)
        S32 = sb("S32", [128, 2 * HD], F32)
        Shi = sb("Shi", [128, 2 * HD], BF16)

        oblkd = sbd("oblk", [128, 16 * HD], BF16)
        o_fmd = sbd("o_fm", [128, 8 * 128], BF16)
        x2td = sbd("x2t", [128, D], F32)
        ztile = sb("ztile", [128, 2 * D], F32)

        scr = sb("scratch", [128, 3 * 1024 + 73], F32)
        scr2 = sb("scratch2", [128, 24], F32)
        scrb = sb("scratchb", [128, 1024], BF16)

        pb = [psum("pb%d" % i, [128, 512]) for i in range(6)]
        pwide = psum("pwide", [128, 1024])

        qs, pool, pe, act, dve = K.qs, K.pool, K.pe, K.act, K.dve
        pre, post = K.pre, K.post

        def mm(out, lhsT, rhs, start, stop, tp=None):
            kw = {"start": start, "stop": stop}
            if tp is not None:
                kw["tile_position"] = tp
            return pe.matmul(out, lhsT, rhs, **kw)

        BK = {"pb%d" % i: pb[i] for i in range(6)}
        BK["pw0"] = pwide[:, 0:512]
        BK["pw1"] = pwide[:, 512:1024]

        ev_tog = [0]

        def evac(dst, dtag, srcp, ptag, scale=None, extra_reads=()):
            e = ev_tog[0]
            ev_tog[0] ^= 1
            if e == 0:
                pre(act, reads=(ptag,) + tuple(extra_reads), writes=(dtag,))
                if scale is None:
                    h = act.copy(dst, srcp)
                else:
                    h = act.activation(dst, srcp, AF.Identity,
                                       scale=np.float32(scale))
                post(h, "act", reads=(ptag,) + tuple(extra_reads), writes=(dtag,))
            else:
                pre(dve, reads=(ptag,) + tuple(extra_reads), writes=(dtag,))
                if scale is None:
                    h = dve.tensor_copy(dst, srcp)
                else:
                    h = dve.tensor_scalar(dst, srcp, float(scale), None, ALU.mult)
                post(h, "dve", reads=(ptag,) + tuple(extra_reads), writes=(dtag,))

        h = qs.dma_start(out=ident[:], in_=ap["ident"]); h.inc("qs")
        h = qs.dma_start(out=foldF[:], in_=ap["foldF"]); h.inc("qs")
        h = qs.dma_start(out=ones_row[:], in_=ap["ones_row"]); h.inc("qs")
        h = qs.dma_start(out=ones_col[:], in_=ap["ones_col"]); h.inc("qs")
        h = qs.dma_start(out=negones_row[:], in_=ap["negones_row"]); h.inc("qs")
        h = qs.dma_start(out=EyeH[:], in_=ap["EyeH"]); h.inc("qs")
        h = qs.dma_start(out=w1c[:], in_=ap["W1c"]); h.inc("qs")
        h = qs.dma_start(out=scr2[:, 0:8], in_=ap["b1c"]); h.inc("qs")
        h = qs.dma_start(out=scr2[:, 8:16], in_=ap["W2c"]); h.inc("qs")
        h = qs.dma_start(out=scr2[:, 16:17], in_=ap["binc"]); h.inc("qs")
        h = qs.dma_start(out=smallA[0:1, 0:1], in_=ap["decay_h"]); h.inc("qs")
        h = qs.dma_start(out=smallA[0:1, 1:2], in_=ap["log_alpha_h"]); h.inc("qs")
        h = qs.dma_start(out=scr2[0:1, 17:21], in_=ap["segmask"]); h.inc("qs")
        c0 = K.c("qs")

        act.wait("qs", c0)
        h = act.activation(smallB[0:1, 0:1], smallA[0:1, 0:1], AF.Exp, scale=np.float32(-1.0))
        post(h, "act", writes=("smB",))
        pre(dve, reads=("smB",))
        h = dve.tensor_scalar(smallB[0:1, 0:1], smallB[0:1, 0:1], 1.0, None, ALU.add)
        K.fence(dve)
        h = dve.reciprocal(smallB[0:1, 1:2], smallB[0:1, 0:1])
        post(h, "dve", writes=("smB",))
        pre(act, reads=("smB",))
        h = act.activation(smallB[0:1, 2:3], smallB[0:1, 1:2], AF.Ln)
        h = act.activation(smallB[0:1, 3:4], smallA[0:1, 1:2], AF.Exp)
        post(h, "act", writes=("smB2",))
        pre(qs, reads=("smB2",))
        h = qs.dma_start(out=bc_buf[0:1, 0:1], in_=smallB[0:1, 2:3]); h.inc("qs")
        h = qs.dma_start(out=bc_buf[1:2, 0:1], in_=smallB[0:1, 3:4]); cbc = h.inc("qs")
        qs.wait("qs", cbc)
        h = qs.dma_start(out=lngcol[:], in_=bc_buf[0:1, 0:1].broadcast_to([128, 1]))
        post(h, "qs", writes=("lngcol",))
        h = qs.dma_start(out=alphacol[:], in_=bc_buf[1:2, 0:1].broadcast_to([128, 1]))
        post(h, "qs", writes=("alphacol",))

        mask_specs = [("diffN", "signN", maskN), ("diffNT", "signNT", maskNT),
                      ("diffITX", "signITX", maskITX), ("diffX", "signX", maskX)]
        for idx, (dn, sn, dst) in enumerate(mask_specs):
            o1 = 2048 + 256 * (idx % 2)
            o2 = o1 + 128
            pre(qs, writes=("mload%d" % (idx % 2),))
            h = qs.dma_start(out=scr[:, o1:o1 + 128], in_=ap[dn]); h.inc("qs")
            h = qs.dma_start(out=scr[:, o2:o2 + 128], in_=ap[sn])
            cm = post(h, "qs", writes=("mload%d" % (idx % 2),))
            act.wait("qs", cm)
            pre(act, reads=("lngcol",), writes=("msk%d" % idx,))
            h = act.activation(dst[:], scr[:, o1:o1 + 128], AF.Exp,
                               scale=lngcol[:, 0:1])
            post(h, "act", reads=("mload%d" % (idx % 2),), writes=("msk%d" % idx,))
            pre(dve, reads=("msk%d" % idx,), writes=("msk%d" % idx,))
            h = dve.tensor_tensor(dst[:], dst[:], scr[:, o2:o2 + 128], ALU.mult)
            post(h, "dve", reads=("mload%d" % (idx % 2),), writes=("msk%d" % idx,))
        h = qs.dma_start(out=scr[:, 2944:3072], in_=ap["colidx"]); h.inc("qs")
        h = qs.dma_start(out=scr[:, 3072:3073], in_=ap["c127"]); h.inc("qs")
        h = qs.dma_start(out=scr[:, 3073:3074], in_=ap["c64"]); h.inc("qs")
        h = qs.dma_start(out=scr[:, 3074:3075], in_=ap["c128"]); cgp = h.inc("qs")
        act.wait("qs", cgp)
        pre(act, reads=("lngcol",))
        h = act.activation(gpcol[:], scr[:, 2944:3072], AF.Exp, scale=lngcol[:, 0:1])
        post(h, "act", writes=("gpcol",))
        h = act.activation(gtpair[:], scr[:, 3072:3073], AF.Exp, scale=lngcol[:, 0:1])
        post(h, "act", writes=("gtpair",))
        h = act.activation(gCcol[:], scr[:, 3073:3074], AF.Exp, scale=lngcol[:, 0:1])
        post(h, "act", writes=("gCcol",))
        h = act.activation(gC2col[:], scr[:, 3074:3075], AF.Exp, scale=lngcol[:, 0:1])
        post(h, "act", writes=("gC2col",))

        dve.wait("qs", c0)
        h = dve.tensor_copy(scrb[0:1, 0:4], scr2[0:1, 17:21])
        post(h, "dve", writes=("segb",))
        pre(pe, reads=("segb",), writes=("pb0",))
        h = mm(pb[0][0:128, 0:4], ones_row[0:1, 0:128],
               scrb[0:1, 0:4], True, True)
        post(h, "pe", reads=("segb",), writes=("pb0",))
        pre(act, reads=("pb0",), writes=("segc",))
        h = act.copy(statcd[0][:, 8:12], pb[0][0:128, 0:4])
        post(h, "act", reads=("pb0",), writes=("segc",))
        pre(dve, reads=("segc",), writes=("maskI",))
        for k in range(4):
            h = dve.tensor_scalar(maskI[:, 128 * k:128 * k + 128], ident[:],
                                  statcd[0][:, 8 + k:9 + k], None, ALU.mult)
        post(h, "dve", reads=("segc",), writes=("maskI",))

        dve.wait("qs", c0)
        h = dve.tensor_scalar(b1eff[:], w1c[:], scr2[:, 16:17], None, ALU.mult)
        K.fence(dve)
        h = dve.tensor_tensor(b1eff[:], b1eff[:], scr2[:, 0:8], ALU.add)
        post(h, "dve", writes=("b1eff",))
        pre(dve, writes=("diagW2",))
        for j in range(8):
            h = dve.tensor_scalar(diagW2[:, 64 * j:64 * j + 64], foldF[:],
                                  scr2[:, 8 + j:9 + j], None, ALU.mult)
        post(h, "dve", writes=("diagW2",))
        h = dve.memset(gaugd[0][64:65, :], 1.0)
        h = dve.memset(gaugd[1][64:65, :], 1.0)
        post(h, "dve", writes=("gaug1",))

        pre(qs, writes=("scrstage",))
        h = qs.dma_start(out=scr[:, 1024:2048], in_=ap["Wwh"][0:128, :]); h.inc("qs")
        h = qs.dma_start(out=scr[:, 0:1024], in_=ap["Wwh"][128:256, :])
        cww = post(h, "qs", writes=("scrstage",))
        dve.wait("qs", cww)
        for half in range(2):
            pre(dve, reads=("scrstage",), writes=("scrb",))
            h = dve.tensor_copy(scrb[:, 0:1024],
                                scr[:, 1024:2048] if half == 0 else scr[:, 0:1024])
            post(h, "dve", reads=("scrstage",), writes=("scrb",))
            pre(pe, reads=("scrb",), writes=("pb0", "pb1"))
            for k in range(8):
                h = mm(pb[k // 4][:, 128 * (k % 4):128 * (k % 4) + 128],
                       scrb[:, 128 * k:128 * k + 128], ident[:], True, True)
            post(h, "pe", reads=("scrb",), writes=("pb0", "pb1"))
            pre(act, reads=("pb0", "pb1"), writes=("Wwt",))
            for k in range(8):
                h = act.copy(Wwt[:, 256 * k + 128 * half:256 * k + 128 * half + 128],
                             pb[k // 4][:, 128 * (k % 4):128 * (k % 4) + 128])
            post(h, "act", reads=("pb0", "pb1"), writes=("Wwt",))
        pre(pe, reads=("Wwt",), writes=("pb2",))
        for k in range(8):
            h = mm(pb[2][0:1, 0:256], ones_col[:, 0:1], Wwt[:, 256 * k:256 * k + 256],
                   k == 0, k == 7)
        post(h, "pe", reads=("Wwt",), writes=("pb2",))
        pre(act, reads=("pb2",), writes=("negwwsum",))
        h = act.activation(negwwsum[:], pb[2][0:1, 0:256], AF.Identity, scale=np.float32(-1.0))
        post(h, "act", reads=("pb2",), writes=("negwwsum",))

        for r in range(8):
            pre(qs, writes=("scrstage",))
            h = qs.dma_start(out=scr[:, 1024:2048],
                             in_=ap["Wr"][128 * r:128 * r + 128, :])
            cw = post(h, "qs", writes=("scrstage",))
            dve.wait("qs", cw)
            pre(dve, reads=("scrstage",), writes=("scrb",))
            h = dve.tensor_copy(scrb[:, 0:1024], scr[:, 1024:2048])
            post(h, "dve", reads=("scrstage",), writes=("scrb",))
            pre(pe, reads=("scrb",), writes=("pb0", "pb1"))
            for cix in range(8):
                h = mm(pb[cix // 4][:, 128 * (cix % 4):128 * (cix % 4) + 128],
                       scrb[:, 128 * cix:128 * cix + 128], ident[:], True, True)
            post(h, "pe", reads=("scrb",), writes=("pb0", "pb1"))
            pre(act, reads=("pb0", "pb1"), writes=("Wrt",))
            for cix in range(8):
                h = act.copy(Wrt[:, 1024 * cix + 128 * r:1024 * cix + 128 * r + 128],
                             pb[cix // 4][:, 128 * (cix % 4):128 * (cix % 4) + 128])
            post(h, "act", reads=("pb0", "pb1"), writes=("Wrt",))

        pre(qs, writes=("scrstage",))
        h = qs.dma_start(out=scr[:, 1024:1536].rearrange("p (c m) -> p c m", c=8),
                         in_=ap["Win"].rearrange("(c p) m -> p c m", p=128))
        cw = post(h, "qs", writes=("scrstage",))
        dve.wait("qs", cw)
        pre(dve, reads=("scrstage",), writes=("Win2",))
        for cix in range(8):
            h = dve.tensor_copy(Win2[:, 128 * cix:128 * cix + 64],
                                scr[:, 1024 + 64 * cix:1024 + 64 * cix + 64])
            h = dve.tensor_copy(Win2[:, 128 * cix + 64:128 * cix + 128],
                                scr[:, 1024 + 64 * cix:1024 + 64 * cix + 64])
        post(h, "dve", reads=("scrstage",), writes=("Win2",))

        pre(qs, writes=("scrstage",))
        h = qs.dma_start(out=scr[0:64, 1024:2048], in_=ap["Wout"]); h.inc("qs")
        h = qs.dma_start(out=scr[0:1, 0:1024], in_=ap["bout"]); h.inc("qs")
        h = qs.dma_start(out=scr[0:64, 3075:3076], in_=ap["b2col"])
        cw = post(h, "qs", writes=("scrstage",))
        dve.wait("qs", cw)
        pre(dve, reads=("scrstage",), writes=("WoutA", "b2b"))
        h = dve.tensor_scalar(WoutA[0:64, :], scr[0:64, 1024:2048], 0.5, None, ALU.mult)
        post(h, "dve", reads=("scrstage",), writes=("WoutA",))
        h = dve.tensor_copy(statcb[0:64, 0:1], scr[0:64, 3075:3076])
        post(h, "dve", reads=("scrstage",), writes=("b2b",))
        for nh in range(2):
            pre(pe, reads=("WoutA", "b2b"), writes=("pb3",))
            h = mm(pb[3][0:1, 0:512], statcb[0:64, 0:1],
                   WoutA[0:64, 512 * nh:512 * nh + 512], True, True)
            post(h, "pe", reads=("WoutA", "b2b"), writes=("pb3",))
            pre(dve, reads=("pb3", "scrstage"), writes=("WoutArow",))
            h = dve.scalar_tensor_tensor(WoutA[64:65, 512 * nh:512 * nh + 512],
                                         scr[0:1, 512 * nh:512 + 512 * nh], 0.5,
                                         pb[3][0:1, 0:512], ALU.mult, ALU.add)
            post(h, "dve", reads=("pb3", "scrstage"), writes=("WoutArow",))

        def ln_stats(srcp, tag, pp):
            statc = statcd[pp]
            stag = "statc%d" % pp
            pre(dve, reads=(tag,), writes=(stag,))
            h = dve.reduce_sum(statc[:, 0:1], srcp, axis=AX)
            post(h, "dve", reads=(tag,), writes=(stag,))
            pre(dve, writes=(stag,))
            h = dve.memset(statc[:, 1:2], 0.0)
            post(h, "dve", writes=(stag,))
            pre(act, reads=(tag, stag), writes=("sqscr", stag))
            h = act.activation(sqscr[:], srcp, AF.Square, accum_out=statc[:, 1:2])
            post(h, "act", reads=(tag,), writes=(stag, "sqscr"))
            pre(dve, reads=(stag,), writes=(stag,))
            h = dve.tensor_scalar(statc[:, 5:6], statc[:, 0:1], RD, None, ALU.mult)
            post(h, "dve", writes=("mu5_%d" % pp,))
            h = dve.tensor_scalar(statc[:, 2:3], statc[:, 1:2], RD,
                                  EPS, ALU.mult, ALU.add)
            K.sread(dve, "mu5_%d" % pp)
            h = dve.tensor_scalar(statc[:, 3:4], statc[:, 5:6],
                                  statc[:, 5:6], -1.0, ALU.mult, ALU.mult)
            K.fence(dve)
            h = dve.tensor_tensor(statc[:, 3:4], statc[:, 2:3], statc[:, 3:4], ALU.add)
            post(h, "dve", writes=(stag,))
            pre(act, reads=(stag,), writes=(stag,))
            h = act.activation(statc[:, 4:5], statc[:, 3:4], AF.Sqrt)
            post(h, "act", writes=(stag,))
            pre(dve, reads=(stag,), writes=(stag,))
            h = dve.reciprocal(statc[:, 6:7], statc[:, 4:5])
            post(h, "dve", writes=(stag,))
            K.fence(dve)
            h = dve.tensor_scalar(statc[:, 7:8], statc[:, 5:6],
                                  statc[:, 6:7], -1.0, ALU.mult, ALU.mult)
            post(h, "dve", writes=(stag,))

        def ffn_block(xin_ap, xin_tag, xout_ap, xout_tag, pp):
            statc = statcd[pp]
            stag = "statc%d" % pp
            y0, y0fm, Htile = y0d[pp], y0fmd[pp], Htiled[pp]
            gaug, ffm = gaugd[pp], ffmd[pp]
            ty0, ty0fm, tH = "y0_%d" % pp, "y0fm_%d" % pp, "Htile_%d" % pp
            tga, tffm = "gaug_%d" % pp, "ffm_%d" % pp
            pre(act, reads=(xin_tag, stag), writes=(ty0,))
            K.sread(act, stag)
            h = act.activation(y0[:], xin_ap, AF.Identity,
                               scale=statc[:, 6:7], bias=statc[:, 7:8])
            post(h, "act", reads=(xin_tag, stag), writes=(ty0,))
            pre(pe, reads=(ty0,), writes=("pb0", "pb1"))
            for k in range(8):
                h = mm(pb[k // 4][:, 128 * (k % 4):128 * (k % 4) + 128],
                       y0[:, 128 * k:128 * k + 128], ident[:], True, True)
            post(h, "pe", reads=(ty0,), writes=("pb0", "pb1"))
            evac(y0fm[:, 0:512], ty0fm, pb[0][:, 0:512], "pb0")
            evac(y0fm[:, 512:1024], ty0fm, pb[1][:, 0:512], "pb1")
            pre(pe, reads=(ty0fm, "Win2"), writes=("pb2",))
            for k in range(8):
                h = mm(pb[2][:, 0:128], Win2[:, 128 * k:128 * k + 128],
                       y0fm[:, 128 * k:128 * k + 128], k == 0, k == 7)
            post(h, "pe", reads=(ty0fm,), writes=("pb2",))
            pre(act, reads=("pb2", "b1eff"), writes=(tH,))
            for j in range(8):
                h = act.activation(Htile[:, 128 * j:128 * j + 128], pb[2][:, 0:128],
                                   AF.Relu, scale=w1c[:, j:j + 1], bias=b1eff[:, j:j + 1])
            post(h, "act", reads=("pb2",), writes=(tH,))
            pre(pe, reads=(tH, "diagW2"), writes=("pb3",))
            for j in range(8):
                h = mm(pb[3][0:64, 0:128], diagW2[:, 64 * j:64 * j + 64],
                       Htile[:, 128 * j:128 * j + 128], j == 0, j == 7)
            post(h, "pe", reads=(tH,), writes=("pb3",))
            evac(gaug[0:64, :], tga, pb[3][0:64, 0:128], "pb3",
                 extra_reads=("gaug1",))
            pre(pe, reads=(tga, "WoutA", "WoutArow", "gaug1"), writes=("pb4", "pb5"))
            for cix in range(8):
                h = mm(pb[4 + cix // 4][:, 128 * (cix % 4):128 * (cix % 4) + 128],
                       WoutA[:, 128 * cix:128 * cix + 128], gaug[:, 0:128], True, True)
            post(h, "pe", reads=(tga,), writes=("pb4", "pb5"))
            evac(ffm[:, 0:512], tffm, pb[4][:, 0:512], "pb4")
            evac(ffm[:, 512:1024], tffm, pb[5][:, 0:512], "pb5")
            pre(pe, reads=(tffm,), writes=("pw0", "pw1"))
            for cix in range(8):
                h = mm(pwide[:, 128 * cix:128 * cix + 128],
                       ffm[:, 128 * cix:128 * cix + 128], ident[:], True, True)
            post(h, "pe", reads=(tffm,), writes=("pw0", "pw1"))
            pre(dve, reads=("pw0", "pw1", xin_tag), writes=(xout_tag,))
            h = dve.scalar_tensor_tensor(xout_ap, xin_ap, 1.0, pwide[:, 0:1024],
                                         ALU.mult, ALU.add)
            post(h, "dve", reads=("pw0", "pw1", xin_tag), writes=(xout_tag,))

        for t in range(NTILE if stage >= 1 else 0):
            half = t % 2
            xs = xt[:, D * half:D * half + D]
            xtag = "xt%d" % half
            statc = statcd[half]
            stag = "statc%d" % half
            x1t, x1bf, x1fm, zrow = (x1td[half], x1bfd[half], x1fmd[half],
                                     zrowd[half])
            tx1 = "x1t_%d" % half
            pre(qs, writes=(xtag,))
            h = qs.dma_start(out=xs, in_=ap["x_own"][128 * t:128 * t + 128, :])
            cl = post(h, "qs", writes=(xtag,))
            dve.wait("qs", cl)
            act.wait("qs", cl)
            ln_stats(xs, xtag, half)
            if DEBUG and t == 0:
                pre(qs, reads=(xtag, stag))
                h = qs.dma_start(out=dbg_p[:, 0:1024], in_=xs); h.inc("qs")
                h = qs.dma_start(out=dbg_p[:, 1024:1036], in_=statc[:])
                post(h, "qs", reads=(xtag, stag))
            ffn_block(xs, xtag, x1t[:], tx1, half)
            if DEBUG and t == 0:
                pre(qs, reads=(tx1, "y0_0", "y0fm_0", "Htile_0", "gaug_0", "ffm_0"))
                h = qs.dma_start(out=dbg_p[:, 2048:3072], in_=x1t[:]); h.inc("qs")
                h = qs.dma_start(out=dbg_pb[:, 0:1024], in_=y0d[0][:]); h.inc("qs")
                h = qs.dma_start(out=dbg_pb[:, 1024:2048], in_=y0fmd[0][:]); h.inc("qs")
                h = qs.dma_start(out=dbg_pb[:, 2048:3072], in_=Htiled[0][:]); h.inc("qs")
                h = qs.dma_start(out=dbg_pb[0:65, 3072:3200], in_=gaugd[0][:]); h.inc("qs")
                h = qs.dma_start(out=dbg_pb[:, 4096:5120], in_=ffmd[0][:])
                post(h, "qs", reads=(tx1, "y0_0", "y0fm_0", "Htile_0", "gaug_0", "ffm_0"))
            ln_stats(x1t[:], tx1, half)
            pre(dve, reads=(tx1, stag), writes=("zrow%d" % half, stag))
            h = dve.tensor_scalar(statc[:, 2:3], statc[:, 6:7], 1.0, None, ALU.add)
            post(h, "dve", writes=("zp2",))
            K.fence(dve)
            h = dve.tensor_scalar(zrow[:], x1t[:], statc[:, 2:3],
                                  statc[:, 7:8], ALU.mult, ALU.add)
            post(h, "dve", reads=(tx1,), writes=("zrow%d" % half, stag))
            pre(qs, reads=("zrow%d" % half,))
            h = qs.dma_start(out=z_spill[128 * t:128 * t + 128, :], in_=zrow[:])
            post(h, "qs", reads=("zrow%d" % half,))
            pre(dve, reads=(stag,), writes=("statcb",))
            h = dve.tensor_copy(statcb[:, 0:1], statc[:, 5:6])
            h = dve.tensor_copy(statcb[:, 1:2], statc[:, 6:7])
            post(h, "dve", reads=(stag,), writes=("statcb",))
            pre(pe, reads=("statcb",), writes=("pb2",))
            h = mm(pb[2][0:2, 128:256], statcb[:, 0:2], ident[:], True, True)
            post(h, "pe", reads=("statcb",), writes=("pb2",))
            pre(act, reads=("pb2",), writes=("statrow",))
            h = act.copy(statrow[:, 128 * t:128 * t + 128], pb[2][0:2, 128:256])
            post(h, "act", reads=("pb2",), writes=("statrow",))
            pre(dve, reads=(tx1,), writes=("x1bf%d" % half,))
            h = dve.tensor_copy(x1bf[:], x1t[:])
            post(h, "dve", reads=(tx1,), writes=("x1bf%d" % half,))
            pre(pe, reads=("x1bf%d" % half,), writes=("pb0", "pb1"))
            for k in range(8):
                h = mm(pb[k // 4][:, 128 * (k % 4):128 * (k % 4) + 128],
                       x1bf[:, 128 * k:128 * k + 128], ident[:], True, True)
            post(h, "pe", reads=("x1bf%d" % half,), writes=("pb0", "pb1"))
            evac(x1fm[:, 0:512], "x1fm%d" % half, pb[0][:, 0:512], "pb0")
            evac(x1fm[:, 512:1024], "x1fm%d" % half, pb[1][:, 0:512], "pb1")
            pre(qs, reads=("x1fm%d" % half,))
            qb = (t // 4) * (D + 2)
            h = qs.dma_start(
                out=ag_in[qb:qb + D, 128 * (t % 4):128 * (t % 4) + 128]
                    .rearrange("(k p) t -> p k t", p=128),
                in_=x1fm[:].rearrange("p (k t) -> p k t", k=8))
            post(h, "qs", reads=("x1fm%d" % half,))
            if t % 4 == 3:
                q4 = t // 4
                pre(qs, reads=("statrow",))
                h = qs.dma_start(out=ag_in[qb + D:qb + D + 2, :],
                                 in_=statrow[:, 512 * q4:512 * q4 + 512])
                cq = post(h, "qs", reads=("statrow",))
                if stage >= 2:
                    pool.wait("qs", cq)
                    h = pool.collective_compute(
                        "AllGather", ALU.bypass,
                        replica_groups=[[0, 1, 2, 3], [4, 5, 6, 7]],
                        ins=[ag_in[qb:qb + (D + 2), :]],
                        outs=[ag_out[4 * qb:4 * qb + 4 * (D + 2), :]])
                    h.inc("cc")
        cc_x1 = K.c("cc")

        pre(dve, writes=("S32",))
        h = dve.memset(S32[:], 0.0)
        post(h, "dve", writes=("S32",))
        pre(act, reads=("S32",), writes=("Shi",))
        h = act.copy(Shi[:], S32[:])
        post(h, "act", reads=("S32",), writes=("Shi",))

        ccx_piece = [cc_x1 - 3, cc_x1 - 2, cc_x1 - 1, cc_x1]
        qg_seg = {}
        for s in range(NSEG if stage >= 2 else 0):
            pre(qs, writes=("statin",))
            for q4 in range(4):
                qs.wait("cc", ccx_piece[q4])
                pbase = q4 * 4 * (D + 2) + s * (D + 2)
                h = qs.dma_start(out=statin[:, 512 * q4:512 * q4 + 512],
                                 in_=ag_out[pbase + D:pbase + D + 2, :])
                if q4 < 3:
                    h.inc("qs")
            post(h, "qs", writes=("statin",))
            if s == 0:
                pre(dve, writes=("rkcarry",))
                h = dve.memset(rkd0[:, 0:1], 0.0)
                h = dve.memset(rkd1[:, 0:1], 0.0)
                post(h, "dve", writes=("rkcarry",))
            else:
                pre(dve, reads=("rkd",), writes=("rkcarry",))
                h = dve.tensor_copy(rkd0[:, 0:1], rkd0[:, SEG:SEG + 1])
                h = dve.tensor_copy(rkd1[:, 0:1], rkd1[:, SEG:SEG + 1])
                post(h, "dve", reads=("rkd",), writes=("rkcarry",))

            for t in range(NTILE):
                half = t % 2
                xin = x1fmi[:, D * half:D * half + D]
                xtag = "x1fmi%d" % half
                stat2c = stat2cd[half]
                ts2 = "stat2c%d" % half
                t1tok, t1sq, rktok, nrm = (t1tokd[half], t1sqd[half],
                                           rktokd[half], nrmd[half])
                tt1, tsq, trk, tnr = ("t1tok%d" % half, "t1sq%d" % half,
                                      "rktok%d" % half, "nrm%d" % half)
                bv = ("pb0", "pb4")[half]
                bt1 = ("pb1", "pb5")[half]
                baux = ("pb2", "pw0")[half]
                bwk = ("pb3", "pw1")[half]
                qs.wait("cc", ccx_piece[t // 4])
                pbase = (t // 4) * 4 * (D + 2) + s * (D + 2)
                pre(qs, reads=("statin",), writes=(xtag,))
                h = qs.dma_start(
                    out=xin.rearrange("p (k t) -> p k t", k=8),
                    in_=ag_out[pbase:pbase + D,
                               128 * (t % 4):128 * (t % 4) + 128]
                        .rearrange("(k p) t -> p k t", p=128))
                cl = post(h, "qs", writes=(xtag,))
                pe.wait("qs", cl)
                pre(pe, reads=("statin",), writes=(baux,))
                h = mm(BK[baux][0:128, 256:258], statin[0:2, 128 * t:128 * t + 128],
                       ident[0:2, 0:2], True, True)
                post(h, "pe", reads=("statin",), writes=(baux,))
                pre(act, reads=(baux,), writes=(ts2,))
                h = act.copy(stat2c[:, 0:2], BK[baux][0:128, 256:258])
                post(h, "act", reads=(baux,), writes=(ts2,))
                pre(pe, reads=(xtag, "Wwt", "statin", "negwwsum"), writes=(bv,))
                for k in range(8):
                    h = mm(BK[bv][:, 0:256], xin[:, 128 * k:128 * k + 128],
                           Wwt[:, 256 * k:256 * k + 256], k == 0, False)
                h = mm(BK[bv][:, 0:256], statin[0:1, 128 * t:128 * t + 128],
                       negwwsum[:], False, True)
                post(h, "pe", reads=(xtag, "statin"), writes=(bv,))
                pre(act, reads=(bv, ts2), writes=("vtok",))
                K.sread(act, ts2)
                h = act.activation(vtok[:, 256 * t:256 * t + 256], BK[bv][:, 0:256],
                                   AF.Identity, scale=stat2c[:, 1:2])
                post(h, "act", reads=(bv, ts2), writes=("vtok",))
                pre(pe, reads=(xtag, "EyeH", "statin"), writes=(bt1,))
                for k in range(8):
                    h = mm(BK[bt1][:, 0:256], xin[:, 128 * k:128 * k + 128],
                           EyeH[:, 256 * k:256 * k + 256], k == 0, False)
                h = mm(BK[bt1][:, 0:256], statin[0:1, 128 * t:128 * t + 128],
                       negones_row[:], False, True)
                post(h, "pe", reads=(xtag, "statin"), writes=(bt1,))
                evac(t1tok[:], tt1, BK[bt1][:, 0:256], bt1)
                pre(pool, reads=(tt1,), writes=(tsq,))
                h = pool.tensor_tensor(t1sq[:], t1tok[:], t1tok[:], ALU.mult)
                post(h, "pool", reads=(tt1,), writes=(tsq,))
                pre(dve, reads=(tsq,), writes=(tnr,))
                h = dve.reduce_sum(nrm[:, 0:1], t1sq[:], axis=AX)
                post(h, "dve", reads=(tsq,), writes=(tnr,))
                pre(act, reads=(tnr,), writes=(tnr,))
                h = act.activation(nrm[:, 1:2], nrm[:, 0:1], AF.Sqrt)
                post(h, "act", writes=(tnr,))
                pre(dve, reads=(tnr,), writes=(tnr,))
                h = dve.tensor_scalar(nrm[:, 2:3], nrm[:, 1:2], 1e-12, None, ALU.max)
                K.fence(dve)
                h = dve.reciprocal(nrm[:, 3:4], nrm[:, 2:3])
                post(h, "dve", writes=(tnr,))
                pre(act, reads=(tnr, tt1), writes=(trk,))
                K.sread(act, tnr)
                h = act.activation(rktok[:], t1tok[:], AF.Identity, scale=nrm[:, 3:4])
                post(h, "act", reads=(tnr, tt1), writes=(trk,))
                pre(pe, reads=(trk,), writes=(baux,))
                h = mm(BK[baux][:, 0:128], rktok[:, 0:128], ident[:], True, True)
                h = mm(BK[baux][:, 128:256], rktok[:, 128:256], ident[:], True, True)
                post(h, "pe", reads=(trk,), writes=(baux,))
                pre(act, reads=(baux,), writes=("rkd",))
                h = act.copy(rkd0[:, 128 * t + 1:128 * t + 129], BK[baux][:, 0:128])
                post(h, "act", reads=(baux,), writes=("rkd",))
                pre(dve, reads=(baux,), writes=("rkd",))
                h = dve.tensor_copy(rkd1[:, 128 * t + 1:128 * t + 129],
                                    BK[baux][:, 128:256])
                post(h, "dve", reads=(baux,), writes=("rkd",))
                pre(dve, reads=("rkd", "gpcol"), writes=("rkg",))
                h = dve.tensor_tensor(rkg0[:, 128 * t:128 * t + 128],
                                      rkd0[:, 128 * t + 1:128 * t + 129], gpcol[:], ALU.mult)
                h = dve.tensor_tensor(rkg1[:, 128 * t:128 * t + 128],
                                      rkd1[:, 128 * t + 1:128 * t + 129], gpcol[:], ALU.mult)
                post(h, "dve", reads=("rkd",), writes=("rkg",))
                rds = ("rkd", "rkcarry") if t == 0 else ("rkd",)
                pre(pe, reads=rds, writes=(bwk,))
                h = mm(BK[bwk][:, 0:128], rkd0[:, 128 * t:128 * t + 128], ident[:], True, True)
                h = mm(BK[bwk][:, 128:256], rkd1[:, 128 * t:128 * t + 128], ident[:], True, True)
                post(h, "pe", reads=rds, writes=(bwk,))
                evac(wktok[:, 256 * t:256 * t + 256], "wktok", BK[bwk][:, 0:256], bwk)

            psub = PSUB
            for p in range(NTILE if stage >= 3 else 0):
                half = p % 2
                powbuf, ptprev = powbufd[half], ptprevd[half]
                Xba, PTgC = Xbad[half], PTgCd[half]
                tpw, tpt = "powbuf%d" % half, "ptprev%d" % half
                txb, tgc = "Xba%d" % half, "PTgC%d" % half
                bG = ("pb0", "pb4")[half]
                bPT = ("pb1", "pb5")[half]
                bSQ = ("pb2", "pw1")[half]
                bWA = ("pb3", "pb0")[half]
                bWB = ("pw0", "pb2")[half]
                wk0 = rkd0[:, 128 * p:128 * p + 128]
                wk1 = rkd1[:, 128 * p:128 * p + 128]
                rk0 = rkd0[:, 128 * p + 1:128 * p + 129]
                rk1 = rkd1[:, 128 * p + 1:128 * p + 129]
                rds = ("rkd", "rkcarry") if p == 0 else ("rkd",)
                pre(pe, reads=rds, writes=(bG,))
                h = mm(BK[bG][:, 0:128], wk0, wk0, True, False)
                h = mm(BK[bG][:, 0:128], wk1, wk1, False, True)
                h = mm(BK[bG][:, 128:256], wk0, rk0, True, False)
                h = mm(BK[bG][:, 128:256], wk1, rk1, False, True)
                post(h, "pe", reads=rds, writes=(bG,))
                pre(dve, reads=(bG, "msk0", "msk1", "msk2", "msk3"),
                    writes=(tpw, "ITXp", txb))
                h = dve.tensor_tensor(powbuf[:, 0:128], BK[bG][:, 0:128], maskN[:], ALU.mult)
                h = dve.tensor_tensor(powbuf[:, 128:256], BK[bG][:, 0:128], maskNT[:], ALU.mult)
                h = dve.tensor_tensor(ITXp[:, 128 * p:128 * p + 128], BK[bG][:, 128:256],
                                      maskITX[:], ALU.mult)
                h = dve.tensor_tensor(Xba[64:128, 0:64], BK[bG][64:128, 0:64],
                                      maskX[64:128, 0:64], ALU.mult)
                post(h, "dve", reads=(bG,), writes=(tpw, "ITXp", txb))
                pre(pe, reads=(tpw,), writes=(bPT,))
                h = mm(BK[bPT][:, 0:128], ident[:], ident[:], True, False)
                h = mm(BK[bPT][:, 0:128], powbuf[:, 0:128], ident[:], False, False)
                post(h, "pe", reads=(tpw,), writes=(bPT,))
                evac(ptprev[:], tpt, BK[bPT][:, 0:128], bPT)
                for lvl in range(5):
                    pre(pe, reads=(tpw,), writes=(bSQ,))
                    h = mm(BK[bSQ][:, 0:128], powbuf[:, 128:256], powbuf[:, 0:128],
                           True, True)
                    if lvl < 4:
                        h = mm(BK[bSQ][:, 128:256], powbuf[:, 0:128], powbuf[:, 128:256],
                               True, True)
                    post(h, "pe", reads=(tpw,), writes=(bSQ,))
                    w = 256 if lvl < 4 else 128
                    evac(powbuf[:, 0:w], tpw, BK[bSQ][:, 0:w], bSQ)
                    pre(pe, reads=(tpw, tpt), writes=())
                    h = mm(BK[bPT][:, 0:128], powbuf[:, 0:128], ptprev[:],
                           False, lvl == 4)
                    post(h, "pe", reads=(tpw, tpt), writes=(bPT,))
                    if lvl < 4:
                        evac(ptprev[:], tpt, BK[bPT][:, 0:128], bPT)
                    else:
                        evac(PTp[:, 128 * p:128 * p + 128], "PTp",
                             BK[bPT][:, 0:128], bPT)
                pre(dve, reads=("PTp", "gCcol"), writes=(tgc,))
                h = dve.tensor_scalar(PTgC[64:128, 0:64],
                                      PTp[64:128, 128 * p + 64:128 * p + 128],
                                      gCcol[64:128, 0:1], None, ALU.mult)
                post(h, "dve", reads=("PTp",), writes=(tgc,))
                pre(pe, reads=("wktok", "PTp", tgc), writes=(bWA, bWB))
                h = mm(BK[bWA][:, 0:64], wktok[0:64, 256 * p:256 * p + 128],
                       PTp[0:64, 128 * p:128 * p + 64], True, True)
                h = mm(BK[bWA][:, 64:128], wktok[0:64, 256 * p + 128:256 * p + 256],
                       PTp[0:64, 128 * p:128 * p + 64], True, True)
                h = mm(BK[bWB][:, 0:64], wktok[64:128, 256 * p:256 * p + 128],
                       PTgC[64:128, 0:64], True, True)
                h = mm(BK[bWB][:, 64:128], wktok[64:128, 256 * p + 128:256 * p + 256],
                       PTgC[64:128, 0:64], True, True)
                post(h, "pe", reads=("wktok", "PTp", tgc), writes=(bWA, bWB))
                evac(wkcT0[:, 128 * p:128 * p + 64], "wkcT", BK[bWA][:, 0:64], bWA,
                     scale=-1.0)
                evac(wkcT1[:, 128 * p:128 * p + 64], "wkcT", BK[bWA][:, 64:128], bWA,
                     scale=-1.0)
                evac(wkcT0[:, 128 * p + 64:128 * p + 128], "wkcT", BK[bWB][:, 0:64],
                     bWB, scale=-1.0)
                evac(wkcT1[:, 128 * p + 64:128 * p + 128], "wkcT", BK[bWB][:, 64:128],
                     bWB, scale=-1.0)
                pre(pe, reads=(txb, "PTp"), writes=(bWB,))
                h = mm(BK[bWB][0:64, 256:320], Xba[64:128, 0:64],
                       PTp[64:128, 128 * p + 64:128 * p + 128], True, True)
                post(h, "pe", reads=(txb, "PTp"), writes=(bWB,))
                evac(MbaTn[0:64, 64 * p:64 * p + 64], "MbaTn",
                     BK[bWB][0:64, 256:320], bWB, scale=-1.0)

            if stage >= 5 and OAG_PIECED and s > 0:
                pool.wait("qg", qg_seg[s - 1])
                h = pool.collective_compute(
                    "AllGather", ALU.bypass,
                    replica_groups=[[0, 1, 2, 3], [4, 5, 6, 7]],
                    ins=[ago_in_l[s - 1]],
                    outs=[ago_out[(s - 1) * 4 * SEG:s * 4 * SEG, :]])
                h.inc("cc")
            for p in range(NTILE if stage >= 4 else 0):
                pre(pe, reads=("PTp", "vtok", "wkcT", "Shi"), writes=("pb0",))
                h = mm(pb[0][0:64, 0:256], PTp[0:64, 128 * p:128 * p + 64],
                       vtok[0:64, 256 * p:256 * p + 256], True, False)
                h = mm(pb[0][0:64, 0:256], wkcT0[:, 128 * p:128 * p + 64],
                       Shi[:, 0:256], False, False)
                h = mm(pb[0][0:64, 0:256], wkcT1[:, 128 * p:128 * p + 64],
                       Shi[:, 256:512], False, True)
                post(h, "pe", reads=("PTp", "vtok", "wkcT", "Shi"), writes=("pb0",))
                pre(act, reads=("pb0",), writes=("vnew_a",))
                h = act.copy(vnewsb[0:64, :], pb[0][0:64, 0:256])
                post(h, "act", reads=("pb0",), writes=("vnew_a",))
                pre(pe, reads=("PTp", "vtok", "wkcT", "Shi", "MbaTn", "vnew_a"),
                    writes=("pb1",))
                h = mm(pb[1][64:128, 0:256], PTp[64:128, 128 * p + 64:128 * p + 128],
                       vtok[64:128, 256 * p:256 * p + 256], True, False, tp=(64, 64))
                h = mm(pb[1][64:128, 0:256], wkcT0[:, 128 * p + 64:128 * p + 128],
                       Shi[:, 0:256], False, False, tp=(0, 64))
                h = mm(pb[1][64:128, 0:256], wkcT1[:, 128 * p + 64:128 * p + 128],
                       Shi[:, 256:512], False, False, tp=(0, 64))
                h = mm(pb[1][64:128, 0:256], MbaTn[0:64, 64 * p:64 * p + 64],
                       vnewsb[0:64, :], False, True, tp=(0, 64))
                post(h, "pe", reads=("PTp", "vtok", "wkcT", "Shi", "MbaTn", "vnew_a"),
                     writes=("pb1",))
                pre(act, reads=("pb1",), writes=("vnew_b",))
                h = act.copy(vnewsb[64:128, :], pb[1][64:128, 0:256])
                post(h, "act", reads=("pb1",), writes=("vnew_b",))
                pre(dve, reads=("pb0", "pb1", "gtpair"), writes=("vssb",))
                h = dve.tensor_scalar(vssb[0:64, :], pb[0][0:64, 0:256],
                                      gtpair[0:64, 0:1], None, ALU.mult)
                h = dve.tensor_scalar(vssb[64:128, :], pb[1][64:128, 0:256],
                                      gtpair[64:128, 0:1], None, ALU.mult)
                post(h, "dve", reads=("pb0", "pb1"), writes=("vssb",))
                pre(pe, reads=("rkg", "Shi", "ITXp", "vnew_a", "vnew_b"), writes=("pb2",))
                h = mm(pb[2][:, 0:256], rkg0[:, 128 * p:128 * p + 128], Shi[:, 0:256],
                       True, False)
                h = mm(pb[2][:, 0:256], rkg1[:, 128 * p:128 * p + 128], Shi[:, 256:512],
                       False, False)
                h = mm(pb[2][:, 0:256], ITXp[:, 128 * p:128 * p + 128], vnewsb[:],
                       False, True)
                post(h, "pe", reads=("rkg", "Shi", "ITXp", "vnew_a", "vnew_b"),
                     writes=("pb2",))
                pre(act, reads=("pb2", "alphacol"), writes=("osb",))
                h = act.activation(osb[:], pb[2][:, 0:256], AF.Identity,
                                   scale=alphacol[:, 0:1])
                post(h, "act", reads=("pb2",), writes=("osb",))
                pre(qs, reads=("osb",))
                row = 128 * p
                h = qs.dma_start(out=ago_in_l[s][row:row + 128, :], in_=osb[:])
                post(h, "qg", reads=("osb",))
                pre(pe, reads=("wktok", "vssb"), writes=("pb3",))
                h = mm(pb[3][:, 0:256], wktok[:, 256 * p:256 * p + 128], vssb[:],
                       True, True)
                h = mm(pb[3][:, 256:512], wktok[:, 256 * p + 128:256 * p + 256], vssb[:],
                       True, True)
                post(h, "pe", reads=("wktok", "vssb"), writes=("pb3",))
                pre(dve, reads=("pb3", "gC2col", "S32"), writes=("S32",))
                K.fence(dve)
                h = dve.scalar_tensor_tensor(S32[:], S32[:], gC2col[:, 0:1],
                                             pb[3][:, 0:512], ALU.mult, ALU.add)
                post(h, "dve", reads=("pb3",), writes=("S32",))
                pre(act, reads=("S32",), writes=("Shi",))
                h = act.copy(Shi[:], S32[:])
                post(h, "act", reads=("S32",), writes=("Shi",))
            if DEBUG:
                pre(qs, reads=("S32",))
                h = qs.dma_start(out=dbg_p[:, 512 * s:512 * s + 512], in_=S32[:])
                post(h, "qs", reads=("S32",))
                if s == 1:
                    pre(qs, reads=("vtok", "wkcT"))
                    h = qs.dma_start(out=dbg_pb[:, 0:4096], in_=vtok[:]); h.inc("qs")
                    h = qs.dma_start(out=dbg_pb[:, 4096:6144], in_=wkcT0[:]); h.inc("qs")
                    h = qs.dma_start(out=dbg_pb[:, 6144:8192], in_=wkcT1[:])
                    post(h, "qs", reads=("vtok", "wkcT"))
            qg_seg[s] = K.c("qg")
            if stage >= 5 and OAG_PIECED and s == NSEG - 1:
                pool.wait("qg", qg_seg[s])
                h = pool.collective_compute(
                    "AllGather", ALU.bypass,
                    replica_groups=[[0, 1, 2, 3], [4, 5, 6, 7]],
                    ins=[ago_in_l[s]],
                    outs=[ago_out[s * 4 * SEG:(s + 1) * 4 * SEG, :]])
                h.inc("cc")
        if stage >= 5 and not OAG_PIECED:
            pool.wait("qg", K.c("qg"))
            for s4 in range(NSEG):
                h = pool.collective_compute(
                    "AllGather", ALU.bypass,
                    replica_groups=[[0, 1, 2, 3], [4, 5, 6, 7]],
                    ins=[ago_in_l[s4]],
                    outs=[ago_out[s4 * 4 * SEG:(s4 + 1) * 4 * SEG, :]])
                h.inc("cc")
        cc_o = K.c("cc")

        cc_o = K.c("cc")

        qs.wait("cc", cc_o)
        for t in range(NTILE if stage >= 5 else 0):
            half = t % 2
            zt = ztile[:, D * half:D * half + D]
            ztag = "ztile%d" % half
            oblk = oblkd[half]
            o_fm = o_fmd[half]
            x2t = x2td[half]
            tob, tofm, tx2 = "oblk%d" % half, "o_fm%d" % half, "x2t_%d" % half
            btr = (("pb0", "pb1"), ("pb4", "pb5"))[half]
            pre(qs, writes=(tob, ztag))
            for j in range(4):
                for k in range(4):
                    row = k * 4 * SEG + j * SEG + 128 * t
                    h = qs.dma_start(out=oblk[:, HD * (4 * j + k):HD * (4 * j + k) + HD],
                                     in_=ago_out[row:row + 128, :])
                    h.inc("qs")
            h = qs.dma_start(out=zt, in_=z_spill[128 * t:128 * t + 128, :])
            cl = post(h, "qs", writes=(tob, ztag))
            pe.wait("qs", cl)
            pre(pe, reads=(tob, "maskI"), writes=btr)
            for j in range(4):
                for r in range(2):
                    cix = 2 * j + r
                    bank = BK[btr[cix // 4]]
                    off = 128 * (cix % 4)
                    for k in range(4):
                        h = mm(bank[:, off:off + 128],
                               oblk[:, HD * (4 * j + k) + 128 * r:
                                    HD * (4 * j + k) + 128 * r + 128],
                               maskI[:, 128 * k:128 * k + 128], k == 0, k == 3)
            post(h, "pe", reads=(tob, "maskI"), writes=btr)
            evac(o_fm[:, 0:512], tofm, BK[btr[0]][:, 0:512], btr[0])
            evac(o_fm[:, 512:1024], tofm, BK[btr[1]][:, 0:512], btr[1])
            pre(pe, reads=(tofm, "Wrt"), writes=("pw0", "pw1"))
            for cix in range(8):
                for nh in range(2):
                    h = mm(pwide[:, 512 * nh:512 * nh + 512],
                           o_fm[:, 128 * cix:128 * cix + 128],
                           Wrt[:, 1024 * cix + 512 * nh:1024 * cix + 512 * nh + 512],
                           cix == 0, cix == 7)
            post(h, "pe", reads=(tofm, "Wrt"), writes=("pw0", "pw1"))
            pre(dve, reads=("pw0", "pw1", ztag), writes=(tx2,))
            h = dve.scalar_tensor_tensor(x2t[:], zt, 1.0, pwide[:, 0:1024],
                                         ALU.mult, ALU.add)
            post(h, "dve", reads=("pw0", "pw1", ztag), writes=(tx2,))
            ln_stats(x2t[:], tx2, half)
            ffn_block(x2t[:], tx2, x1td[half][:], "x1t_%d" % half, half)
            pre(qs, reads=("x1t_%d" % half,))
            h = qs.dma_start(out=out_own[128 * t:128 * t + 128, :], in_=x1td[half][:])
            post(h, "qs", reads=("x1t_%d" % half,))
        if DEBUG:
            qs.wait("qs", K.c("qs"))
            qs.wait("qg", K.c("qg"))
            for s4 in range(NSEG):
                h = qs.dma_start(out=dbg_o[s4 * SEG:(s4 + 1) * SEG, :],
                                 in_=ago_in_l[s4]); h.inc("qs")
            h = qs.dma_start(out=dbg_z, in_=z_spill); h.inc("qs")
        if stage < 5:
            pre(dve, writes=("zrow",))
            h = dve.memset(zrow[:], 0.0)
            post(h, "dve", writes=("zrow",))
            for t in range(NTILE):
                pre(qs, reads=("zrow",))
                h = qs.dma_start(out=out_own[128 * t:128 * t + 128, :], in_=zrow[:])
                post(h, "qs", reads=("zrow",))
        qs.wait("qs", K.c("qs"))
        pool.wait("qg", K.c("qg"))

        K.emit(nc, sems)
    return nc


_NC_CACHE = None


def _build_in_maps(inputs):
    x = np.asarray(inputs["L0_attn_out"], dtype=np.float32)
    Ww = np.asarray(inputs["L0_attn_Ww"], dtype=np.float32)
    Wr = np.asarray(inputs["L0_attn_Wr"], dtype=np.float32)
    decay = np.asarray(inputs["L0_attn_decay"], dtype=np.float32)
    log_alpha = np.asarray(inputs["L0_attn_log_alpha"], dtype=np.float32)
    Win = np.asarray(inputs["L0_ffn_Win"], dtype=np.float32)
    bin_ = np.asarray(inputs["L0_ffn_bin_"], dtype=np.float32)
    W1 = np.asarray(inputs["L0_ffn_W1"], dtype=np.float32)
    b1 = np.asarray(inputs["L0_ffn_b1"], dtype=np.float32)
    W2 = np.asarray(inputs["L0_ffn_W2"], dtype=np.float32)
    b2 = np.asarray(inputs["L0_ffn_b2"], dtype=np.float32)
    Wout = np.asarray(inputs["L0_ffn_Wout"], dtype=np.float32)
    bout = np.asarray(inputs["L0_ffn_bout"], dtype=np.float32)

    consts = _host_consts()
    W1c = _w_cols(W1)
    b1c = _w_cols(b1)
    W2c = _w_cols(W2)
    binc = np.ascontiguousarray(bin_[np.arange(128) % 64][:, None].astype(np.float32))
    in_maps = []
    for c in range(NCORE):
        b, h = c // 4, c % 4
        s = c % 4
        eyeh = np.zeros((128, 8 * HD), dtype=np.float32)
        eyeh[np.arange(128), HD * 2 * h + np.arange(128)] = 1.0
        eyeh[np.arange(128), HD * (2 * h + 1) + 128 + np.arange(128)] = 1.0
        m = {
            "x_own": np.ascontiguousarray(x[b, s * SEG:(s + 1) * SEG, :]),
            "Wwh": np.ascontiguousarray(Ww[h * HD:(h + 1) * HD, :]),
            "Wr": Wr,
            "Win": Win,
            "W1c": W1c, "b1c": b1c, "W2c": W2c, "binc": binc,
            "Wout": Wout, "bout": np.ascontiguousarray(bout[None, :]),
            "b2col": np.ascontiguousarray(b2[:, None]),
            "decay_h": np.ascontiguousarray(decay[h:h + 1][:, None]),
            "log_alpha_h": np.ascontiguousarray(log_alpha[h:h + 1][:, None]),
            "segmask": np.ascontiguousarray(
                (np.arange(4) == s).astype(np.float32)[None, :]),
            "EyeH": eyeh.astype(bf16),
        }
        for kk, v in consts.items():
            m[kk] = v
        in_maps.append(m)
    return in_maps


def kernel(**inputs):
    global _NC_CACHE
    if _NC_CACHE is None:
        _NC_CACHE = build_nc()
    in_maps = _build_in_maps(inputs)
    res = run_bass_kernel_spmd(_NC_CACHE, in_maps, core_ids=list(range(NCORE)))
    out = np.zeros((B, T, D), dtype=np.float32)
    for c in range(NCORE):
        b, s = c // 4, c % 4
        out[b, s * SEG:(s + 1) * SEG, :] = res.results[c]["out_own"]
    return out


# version 06
# speedup vs baseline: 1.2423x; 1.1722x over previous
import contextlib
import numpy as np
import ml_dtypes
import concourse.bass as bass
import concourse.mybir as mybir
from concourse.bass_utils import run_bass_kernel_spmd

F32 = mybir.dt.float32
BF16 = mybir.dt.bfloat16
AF = mybir.ActivationFunctionType
ALU = mybir.AluOpType
AX = mybir.AxisListType.X
bf16 = ml_dtypes.bfloat16

B, T, D, H, HD, C = 2, 8192, 1024, 4, 256, 64
PAIR = 2 * C
SEG = 2048
NSEG = 4
NTILE = SEG // 128
NCORE = 8
EPS = 1e-5
RD = 1.0 / D


class _Op:
    __slots__ = ("eng", "kind", "attr", "args", "kwargs", "incs", "sem", "val")


class _Handle:
    def __init__(self, ker, op):
        self.ker, self.op = ker, op
        self.vals = {}

    def inc(self, sem, amt=None):
        if sem in self.vals:
            return self.vals[sem]
        amt = amt if amt is not None else (16 if sem.startswith("q") else 1)
        self.op.incs.append((sem, amt))
        self.ker.cnt[sem] = self.ker.cnt.get(sem, 0) + amt
        self.vals[sem] = self.ker.cnt[sem]
        return self.vals[sem]


class EngQ:
    OWN = {"pe": "pe"}
    AUTOINC = ("act", "dve", "pool")
    NOINC = ("dma_start", "collective_compute", "wait_ge")

    def __init__(self, ker, name):
        self._ker, self._name = ker, name

    def __getattr__(self, attr):
        ker, name = self._ker, self._name

        def call(*args, **kwargs):
            op = _Op()
            op.eng, op.kind, op.attr, op.args, op.kwargs, op.incs = (
                name, "ins", attr, args, kwargs, [])
            ker.ops.append(op)
            h = _Handle(ker, op)
            if name in EngQ.AUTOINC and attr not in EngQ.NOINC:
                h.inc(name)
            return h

        return call

    def wait(self, sem, val, hard=False):
        ker, name = self._ker, self._name
        if val <= 0:
            return
        if not hard and EngQ.OWN.get(name) == sem:
            return
        key = (name, sem)
        if ker.wm.get(key, 0) >= val:
            return
        ker.wm[key] = val
        op = _Op()
        op.eng, op.kind, op.sem, op.val = name, "wait", sem, val
        ker.ops.append(op)


class Res:
    __slots__ = ("writer", "readers")

    def __init__(self):
        self.writer = None
        self.readers = []


class Ker:
    def __init__(self):
        self.ops = []
        self.cnt = {}
        self.wm = {}
        self.res = {}
        for n in ("pe", "act", "dve", "pool", "qs"):
            setattr(self, n, EngQ(self, n))

    def R(self, name):
        if name not in self.res:
            self.res[name] = Res()
        return self.res[name]

    def c(self, sem):
        return self.cnt.get(sem, 0)

    def fence(self, q):
        name = q._name
        q.wait(name, self.cnt.get(name, 0))

    def sread(self, q, *names):
        for r in names:
            w = self.R(r).writer
            if w:
                q.wait(w[0], w[1], hard=True)

    def pre(self, q, reads=(), writes=()):
        for r in reads:
            w = self.R(r).writer
            if w:
                q.wait(*w)
        for wname in writes:
            rr = self.R(wname)
            for rd in rr.readers:
                q.wait(*rd)
            rr.readers = []
            if rr.writer:
                q.wait(*rr.writer)

    def post(self, h, sem, reads=(), writes=()):
        v = h.inc(sem)
        for r in reads:
            self.R(r).readers.append((sem, v))
        for w in writes:
            rr = self.R(w)
            rr.writer = (sem, v)
            rr.readers = []
        return v

    def emit(self, nc, sems):
        by_eng = {"pe": [], "act": [], "dve": [], "pool": [], "qs": []}
        for op in self.ops:
            by_eng[op.eng].append(op)

        def runner(eng, ops):
            for op in ops:
                if op.kind == "wait":
                    eng.wait_ge(sems[op.sem], op.val)
                else:
                    ins = getattr(eng, op.attr)(*op.args, **op.kwargs)
                    for (sname, amt) in op.incs:
                        ins.then_inc(sems[sname], amt)

        with nc.Block() as block:
            @block.tensor
            def _(eng):
                runner(eng, by_eng["pe"])

            @block.scalar
            def _(eng):
                runner(eng, by_eng["act"])

            @block.vector
            def _(eng):
                runner(eng, by_eng["dve"])

            @block.gpsimd
            def _(eng):
                runner(eng, by_eng["pool"])

            @block.sync
            def _(eng):
                runner(eng, by_eng["qs"])


def _host_consts():
    i_ = np.arange(PAIR)[:, None].astype(np.int64)
    j_ = np.arange(PAIR)[None, :].astype(np.int64)
    same = (i_ // C) == (j_ // C)
    strict = same & (i_ > j_)
    big = 1.0e4
    diffN = np.where(strict, (i_ - j_).astype(np.float64), big).astype(np.float32)
    signN = np.where(strict, -1.0, 0.0).astype(np.float32)
    diffNT = np.ascontiguousarray(diffN.T)
    signNT = np.ascontiguousarray(signN.T)
    diffITX = np.where(strict.T, (i_.T - j_.T).astype(np.float64), big)
    signITX = np.where(strict.T, 1.0, 0.0)
    jj = np.arange(C)[:, None]
    ii = np.arange(C)[None, :]
    diffITX[0:C, C:PAIR] = 63 - jj + ii
    signITX[0:C, C:PAIR] = 1.0
    diffX = np.full((PAIR, PAIR), big)
    diffX[C:PAIR, 0:C] = np.tile((63 - np.arange(C))[None, :], (C, 1))
    signX = np.zeros((PAIR, PAIR))
    signX[C:PAIR, 0:C] = 1.0
    colidx = np.tile(np.arange(PAIR, dtype=np.float32)[None, :], (PAIR, 1))
    c127 = (127.0 - np.arange(PAIR, dtype=np.float32))[:, None]
    c64 = np.full((PAIR, 1), 64.0, np.float32)
    c128 = np.full((PAIR, 1), 128.0, np.float32)
    return dict(
        diffN=diffN, signN=signN, diffNT=diffNT, signNT=signNT,
        diffITX=diffITX.astype(np.float32), signITX=signITX.astype(np.float32),
        diffX=diffX.astype(np.float32), signX=signX.astype(np.float32),
        colidx=colidx, c127=c127, c64=c64, c128=c128,
        ident=np.eye(PAIR, dtype=np.float32).astype(bf16),
        foldF=(np.arange(128)[:, None] % 64 == np.arange(64)[None, :]
               ).astype(np.float32).astype(bf16),
        ones_row=np.ones((1, PAIR), np.float32).astype(bf16),
        ones_col=np.ones((PAIR, 1), np.float32).astype(bf16),
        negones_row=np.full((1, HD), -1.0, np.float32).astype(bf16),
    )


_CONST_DTYPES = {"ident": BF16, "ones_row": BF16, "ones_col": BF16,
                 "negones_row": BF16, "foldF": BF16}


def _w_cols(Wx):
    out = np.zeros((128, 8), dtype=np.float32)
    p = np.arange(128)
    for j in range(8):
        out[:, j] = Wx[p % 64, 2 * j + p // 64]
    return out


STAGE = 5
PSUB = 99
DEBUG = False
OAG_PIECED = True


def build_nc(stage=None):
    stage = STAGE if stage is None else stage
    nc = bass.Bass("TRN2", target_bir_lowering=False, debug=False)
    ap = {}

    def din(name, shape, dtype=F32):
        ap[name] = nc.dram_tensor(name, shape, dtype, kind="ExternalInput").ap()
        return ap[name]

    din("x_own", [SEG, D])
    din("Wwh", [HD, D])
    din("Wr", [D, D])
    din("Win", [D, 64])
    din("W1c", [128, 8]); din("b1c", [128, 8]); din("W2c", [128, 8])
    din("binc", [128, 1])
    din("Wout", [64, D]); din("bout", [1, D]); din("b2col", [64, 1])
    din("decay_h", [1, 1]); din("log_alpha_h", [1, 1])
    din("segmask", [1, 4])
    din("EyeH", [128, 8 * HD], BF16)
    for k, v in _host_consts().items():
        din(k, list(v.shape), _CONST_DTYPES.get(k, F32))

    out_own = nc.dram_tensor("out_own", [SEG, D], F32, kind="ExternalOutput").ap()
    if DEBUG:
        dbg_ag = nc.dram_tensor("dbg_ag", [D + 2, SEG], BF16, kind="ExternalOutput").ap()
        dbg_o = nc.dram_tensor("dbg_o", [T, HD], BF16, kind="ExternalOutput").ap()
        dbg_z = nc.dram_tensor("dbg_z", [SEG, D], F32, kind="ExternalOutput").ap()
        dbg_p = nc.dram_tensor("dbg_p", [128, 4096], F32, kind="ExternalOutput").ap()
        dbg_pb = nc.dram_tensor("dbg_pb", [128, 8192], BF16, kind="ExternalOutput").ap()

    ag_in = nc.dram_tensor("ag_in", [4 * (D + 2), 512], BF16).ap()
    ag_out = nc.dram_tensor("ag_out", [16 * (D + 2), 512], BF16).ap()
    ago_in_l = [nc.dram_tensor("ago_in%d" % i, [SEG, HD], BF16).ap()
                for i in range(NSEG)]
    ago_out = nc.dram_tensor("ago_out", [NSEG * T, HD], BF16).ap()
    z_spill = nc.dram_tensor("z_spill", [SEG, D], F32).ap()
    bc_buf = nc.dram_tensor("bc_buf", [8, 1], F32).ap()

    K = Ker()
    sem_names = ["pe", "act", "dve", "pool", "qs", "qg", "cc"]

    with contextlib.ExitStack() as ctx:
        def sb(name, shape, dtype):
            return ctx.enter_context(nc.sbuf_tensor(name, shape, dtype))

        def psum(name, shape):
            return ctx.enter_context(nc.psum_tensor(name, shape, F32))

        sems = {n: ctx.enter_context(nc.semaphore(n)) for n in sem_names}

        ident = sb("identb", [128, 128], BF16)
        ones_row = sb("onesrow", [1, 128], BF16)
        ones_col = sb("onescol", [128, 1], BF16)
        negones_row = sb("negonesrow", [1, HD], BF16)
        EyeH = sb("EyeHs", [128, 8 * HD], BF16)
        Wwt = sb("Wwt", [128, 8 * HD], BF16)
        negwwsum = sb("negwwsum", [1, HD], BF16)
        Wrt = sb("Wrt", [128, 8 * D], BF16)
        Win2 = sb("Win2", [128, 8 * 128], BF16)
        WoutA = sb("WoutA", [65, 8 * 128], BF16)
        w1c = sb("w1c", [128, 8], F32)
        b1eff = sb("b1eff", [128, 8], F32)
        diagW2 = sb("diagW2", [128, 8 * 64], BF16)
        foldF = sb("foldFs", [128, 64], BF16)
        maskN = sb("maskN", [128, 128], F32)
        maskNT = sb("maskNT", [128, 128], F32)
        maskITX = sb("maskITX", [128, 128], F32)
        maskX = sb("maskX", [128, 128], F32)
        gpcol = sb("gpcol", [128, 128], F32)
        gtpair = sb("gtpair", [128, 1], F32)
        gC2col = sb("gC2col", [128, 1], F32)
        gCcol = sb("gCcol", [128, 1], F32)
        alphacol = sb("alphacol", [128, 1], F32)
        lngcol = sb("lngcol", [128, 1], F32)
        maskI = sb("maskI", [128, 4 * 128], BF16)
        smallA = sb("smallA", [1, 8], F32)
        smallB = sb("smallB", [1, 8], F32)

        def sbd(name, shape, dtype, n=2):
            return [sb(name + chr(65 + i), shape, dtype) for i in range(n)]

        xt = sb("xt", [128, 2 * D], F32)
        y0d = sbd("y0", [128, D], BF16)
        y0fmd = sbd("y0fm", [128, D], BF16)
        Htiled = sbd("Htile", [128, 8 * 128], BF16)
        gaugd = sbd("gaug", [65, 128], BF16)
        ffmd = sbd("ffm", [128, D], BF16)
        x1td = sbd("x1t", [128, D], F32)
        x1bfd = sbd("x1bf", [128, D], BF16)
        x1fmd = sbd("x1fm", [128, D], BF16)
        statcd = sbd("statc", [128, 12], F32)
        statcb = sb("statcb", [128, 2], BF16)
        statrow = sb("statrow", [2, SEG], BF16)
        zrowd = sbd("zrow", [128, D], F32)

        x1fmi = sb("x1fmi", [128, 2 * D], BF16)
        statin = sb("statin", [2, SEG], BF16)
        stat2cd = sbd("stat2c", [128, 4], F32)
        vtok = sb("vtok", [128, NTILE * HD], BF16)
        wktok = sb("wktok", [128, NTILE * HD], BF16)
        rkd0 = sb("rkd0", [128, SEG + 1], BF16)
        rkd1 = sb("rkd1", [128, SEG + 1], BF16)
        rkg0 = sb("rkg0", [128, SEG], BF16)
        rkg1 = sb("rkg1", [128, SEG], BF16)
        wkcT0 = sb("wkcT0", [128, SEG], BF16)
        wkcT1 = sb("wkcT1", [128, SEG], BF16)
        t1tokd = sbd("t1tok", [128, HD], BF16)
        t1sq1 = sb("t1sq", [128, HD], BF16)
        rktokd = sbd("rktok", [128, HD], BF16)
        nrmd = sbd("nrm", [128, 4], F32)
        PTp = sb("PTp", [128, NTILE * 128], BF16)
        PTgCd = sbd("PTgC", [128, 64], BF16)
        ITXp = sb("ITXp", [128, NTILE * 128], BF16)
        Xbad = sbd("Xba", [128, 64], BF16, 4)
        MbaTn = sb("MbaTn", [64, NTILE * 64], BF16)
        powbufd = sbd("powbuf", [128, 256], BF16, 4)
        ptprevd = sbd("ptprev", [128, 128], BF16, 4)
        vnewsb = sb("vnewsb", [128, HD], BF16)
        vssb = sb("vssb", [128, HD], BF16)
        osbd = sbd("osb", [128, HD], BF16)
        S32 = sb("S32", [128, 2 * HD], F32)
        Shi = sb("Shi", [128, 2 * HD], BF16)

        oblkd = sbd("oblk", [128, 16 * HD], BF16)
        o_fmd = sbd("o_fm", [128, 8 * 128], BF16)
        x2td = sbd("x2t", [128, D], F32)
        ztile = sb("ztile", [128, 2 * D], F32)

        scr = sb("scratch", [128, 2944], F32)
        scr2 = sb("scratch2", [128, 24], F32)
        scrb = sb("scratchb", [128, 1024], BF16)

        pb = [psum("pb%d" % i, [128, 512]) for i in range(6)]
        pwide = psum("pwide", [128, 1024])

        qs, pool, pe, act, dve = K.qs, K.pool, K.pe, K.act, K.dve
        pre, post = K.pre, K.post

        def mm(out, lhsT, rhs, start, stop, tp=None):
            kw = {"start": start, "stop": stop}
            if tp is not None:
                kw["tile_position"] = tp
            return pe.matmul(out, lhsT, rhs, **kw)

        BK = {"pb%d" % i: pb[i] for i in range(6)}
        BK["pw0"] = pwide[:, 0:512]
        BK["pw1"] = pwide[:, 512:1024]

        ev_tog = [0]

        def wavefront(n, stages):
            ns = len(stages)
            for w in range(n + ns - 1):
                for si in reversed(range(ns)):
                    i = w - si
                    if 0 <= i < n:
                        stages[si](i)

        def evac(dst, dtag, srcp, ptag, scale=None, extra_reads=()):
            e = ev_tog[0]
            ev_tog[0] ^= 1
            if e == 0:
                pre(act, reads=(ptag,) + tuple(extra_reads), writes=(dtag,))
                if scale is None:
                    h = act.copy(dst, srcp)
                else:
                    h = act.activation(dst, srcp, AF.Identity,
                                       scale=np.float32(scale))
                post(h, "act", reads=(ptag,) + tuple(extra_reads), writes=(dtag,))
            else:
                pre(dve, reads=(ptag,) + tuple(extra_reads), writes=(dtag,))
                if scale is None:
                    h = dve.tensor_copy(dst, srcp)
                else:
                    h = dve.tensor_scalar(dst, srcp, float(scale), None, ALU.mult)
                post(h, "dve", reads=(ptag,) + tuple(extra_reads), writes=(dtag,))

        h = qs.dma_start(out=ident[:], in_=ap["ident"]); h.inc("qs")
        h = qs.dma_start(out=foldF[:], in_=ap["foldF"]); h.inc("qs")
        h = qs.dma_start(out=ones_row[:], in_=ap["ones_row"]); h.inc("qs")
        h = qs.dma_start(out=ones_col[:], in_=ap["ones_col"]); h.inc("qs")
        h = qs.dma_start(out=negones_row[:], in_=ap["negones_row"]); h.inc("qs")
        h = qs.dma_start(out=EyeH[:], in_=ap["EyeH"]); h.inc("qs")
        h = qs.dma_start(out=w1c[:], in_=ap["W1c"]); h.inc("qs")
        h = qs.dma_start(out=scr2[:, 0:8], in_=ap["b1c"]); h.inc("qs")
        h = qs.dma_start(out=scr2[:, 8:16], in_=ap["W2c"]); h.inc("qs")
        h = qs.dma_start(out=scr2[:, 16:17], in_=ap["binc"]); h.inc("qs")
        h = qs.dma_start(out=smallA[0:1, 0:1], in_=ap["decay_h"]); h.inc("qs")
        h = qs.dma_start(out=smallA[0:1, 1:2], in_=ap["log_alpha_h"]); h.inc("qs")
        h = qs.dma_start(out=scr2[0:1, 17:21], in_=ap["segmask"]); h.inc("qs")
        c0 = K.c("qs")

        act.wait("qs", c0)
        h = act.activation(smallB[0:1, 0:1], smallA[0:1, 0:1], AF.Exp, scale=np.float32(-1.0))
        post(h, "act", writes=("smB",))
        pre(dve, reads=("smB",))
        h = dve.tensor_scalar(smallB[0:1, 0:1], smallB[0:1, 0:1], 1.0, None, ALU.add)
        K.fence(dve)
        h = dve.reciprocal(smallB[0:1, 1:2], smallB[0:1, 0:1])
        post(h, "dve", writes=("smB",))
        pre(act, reads=("smB",))
        h = act.activation(smallB[0:1, 2:3], smallB[0:1, 1:2], AF.Ln)
        h = act.activation(smallB[0:1, 3:4], smallA[0:1, 1:2], AF.Exp)
        post(h, "act", writes=("smB2",))
        pre(qs, reads=("smB2",))
        h = qs.dma_start(out=bc_buf[0:1, 0:1], in_=smallB[0:1, 2:3]); h.inc("qs")
        h = qs.dma_start(out=bc_buf[1:2, 0:1], in_=smallB[0:1, 3:4]); cbc = h.inc("qs")
        qs.wait("qs", cbc)
        h = qs.dma_start(out=lngcol[:], in_=bc_buf[0:1, 0:1].broadcast_to([128, 1]))
        post(h, "qs", writes=("lngcol",))
        h = qs.dma_start(out=alphacol[:], in_=bc_buf[1:2, 0:1].broadcast_to([128, 1]))
        post(h, "qs", writes=("alphacol",))

        mask_specs = [("diffN", "signN", maskN), ("diffNT", "signNT", maskNT),
                      ("diffITX", "signITX", maskITX), ("diffX", "signX", maskX)]
        for idx, (dn, sn, dst) in enumerate(mask_specs):
            o1 = 2048 + 256 * (idx % 2)
            o2 = o1 + 128
            pre(qs, writes=("mload%d" % (idx % 2),))
            h = qs.dma_start(out=scr[:, o1:o1 + 128], in_=ap[dn]); h.inc("qs")
            h = qs.dma_start(out=scr[:, o2:o2 + 128], in_=ap[sn])
            cm = post(h, "qs", writes=("mload%d" % (idx % 2),))
            act.wait("qs", cm)
            pre(act, reads=("lngcol",), writes=("msk%d" % idx,))
            h = act.activation(dst[:], scr[:, o1:o1 + 128], AF.Exp,
                               scale=lngcol[:, 0:1])
            post(h, "act", reads=("mload%d" % (idx % 2),), writes=("msk%d" % idx,))
            pre(dve, reads=("msk%d" % idx,), writes=("msk%d" % idx,))
            h = dve.tensor_tensor(dst[:], dst[:], scr[:, o2:o2 + 128], ALU.mult)
            post(h, "dve", reads=("mload%d" % (idx % 2),), writes=("msk%d" % idx,))
        h = qs.dma_start(out=scr[:, 2560:2688], in_=ap["colidx"]); h.inc("qs")
        h = qs.dma_start(out=scr[:, 2688:2689], in_=ap["c127"]); h.inc("qs")
        h = qs.dma_start(out=scr[:, 2689:2690], in_=ap["c64"]); h.inc("qs")
        h = qs.dma_start(out=scr[:, 2690:2691], in_=ap["c128"]); cgp = h.inc("qs")
        act.wait("qs", cgp)
        pre(act, reads=("lngcol",))
        h = act.activation(gpcol[:], scr[:, 2560:2688], AF.Exp, scale=lngcol[:, 0:1])
        post(h, "act", writes=("gpcol",))
        h = act.activation(gtpair[:], scr[:, 2688:2689], AF.Exp, scale=lngcol[:, 0:1])
        post(h, "act", writes=("gtpair",))
        h = act.activation(gCcol[:], scr[:, 2689:2690], AF.Exp, scale=lngcol[:, 0:1])
        post(h, "act", writes=("gCcol",))
        h = act.activation(gC2col[:], scr[:, 2690:2691], AF.Exp, scale=lngcol[:, 0:1])
        post(h, "act", writes=("gC2col",))

        dve.wait("qs", c0)
        h = dve.tensor_copy(scrb[0:1, 0:4], scr2[0:1, 17:21])
        post(h, "dve", writes=("segb",))
        pre(pe, reads=("segb",), writes=("pb0",))
        h = mm(pb[0][0:128, 0:4], ones_row[0:1, 0:128],
               scrb[0:1, 0:4], True, True)
        post(h, "pe", reads=("segb",), writes=("pb0",))
        pre(act, reads=("pb0",), writes=("segc",))
        h = act.copy(statcd[0][:, 8:12], pb[0][0:128, 0:4])
        post(h, "act", reads=("pb0",), writes=("segc",))
        pre(dve, reads=("segc",), writes=("maskI",))
        for k in range(4):
            h = dve.tensor_scalar(maskI[:, 128 * k:128 * k + 128], ident[:],
                                  statcd[0][:, 8 + k:9 + k], None, ALU.mult)
        post(h, "dve", reads=("segc",), writes=("maskI",))

        dve.wait("qs", c0)
        h = dve.tensor_scalar(b1eff[:], w1c[:], scr2[:, 16:17], None, ALU.mult)
        K.fence(dve)
        h = dve.tensor_tensor(b1eff[:], b1eff[:], scr2[:, 0:8], ALU.add)
        post(h, "dve", writes=("b1eff",))
        pre(dve, writes=("diagW2",))
        for j in range(8):
            h = dve.tensor_scalar(diagW2[:, 64 * j:64 * j + 64], foldF[:],
                                  scr2[:, 8 + j:9 + j], None, ALU.mult)
        post(h, "dve", writes=("diagW2",))
        h = dve.memset(gaugd[0][64:65, :], 1.0)
        h = dve.memset(gaugd[1][64:65, :], 1.0)
        post(h, "dve", writes=("gaug1",))

        pre(qs, writes=("scrstage",))
        h = qs.dma_start(out=scr[:, 1024:2048], in_=ap["Wwh"][0:128, :]); h.inc("qs")
        h = qs.dma_start(out=scr[:, 0:1024], in_=ap["Wwh"][128:256, :])
        cww = post(h, "qs", writes=("scrstage",))
        dve.wait("qs", cww)
        for half in range(2):
            pre(dve, reads=("scrstage",), writes=("scrb",))
            h = dve.tensor_copy(scrb[:, 0:1024],
                                scr[:, 1024:2048] if half == 0 else scr[:, 0:1024])
            post(h, "dve", reads=("scrstage",), writes=("scrb",))
            pre(pe, reads=("scrb",), writes=("pb0", "pb1"))
            for k in range(8):
                h = mm(pb[k // 4][:, 128 * (k % 4):128 * (k % 4) + 128],
                       scrb[:, 128 * k:128 * k + 128], ident[:], True, True)
            post(h, "pe", reads=("scrb",), writes=("pb0", "pb1"))
            pre(act, reads=("pb0", "pb1"), writes=("Wwt",))
            for k in range(8):
                h = act.copy(Wwt[:, 256 * k + 128 * half:256 * k + 128 * half + 128],
                             pb[k // 4][:, 128 * (k % 4):128 * (k % 4) + 128])
            post(h, "act", reads=("pb0", "pb1"), writes=("Wwt",))
        pre(pe, reads=("Wwt",), writes=("pb2",))
        for k in range(8):
            h = mm(pb[2][0:1, 0:256], ones_col[:, 0:1], Wwt[:, 256 * k:256 * k + 256],
                   k == 0, k == 7)
        post(h, "pe", reads=("Wwt",), writes=("pb2",))
        pre(act, reads=("pb2",), writes=("negwwsum",))
        h = act.activation(negwwsum[:], pb[2][0:1, 0:256], AF.Identity, scale=np.float32(-1.0))
        post(h, "act", reads=("pb2",), writes=("negwwsum",))

        for r in range(8):
            pre(qs, writes=("scrstage",))
            h = qs.dma_start(out=scr[:, 1024:2048],
                             in_=ap["Wr"][128 * r:128 * r + 128, :])
            cw = post(h, "qs", writes=("scrstage",))
            dve.wait("qs", cw)
            pre(dve, reads=("scrstage",), writes=("scrb",))
            h = dve.tensor_copy(scrb[:, 0:1024], scr[:, 1024:2048])
            post(h, "dve", reads=("scrstage",), writes=("scrb",))
            pre(pe, reads=("scrb",), writes=("pb0", "pb1"))
            for cix in range(8):
                h = mm(pb[cix // 4][:, 128 * (cix % 4):128 * (cix % 4) + 128],
                       scrb[:, 128 * cix:128 * cix + 128], ident[:], True, True)
            post(h, "pe", reads=("scrb",), writes=("pb0", "pb1"))
            pre(act, reads=("pb0", "pb1"), writes=("Wrt",))
            for cix in range(8):
                h = act.copy(Wrt[:, 1024 * cix + 128 * r:1024 * cix + 128 * r + 128],
                             pb[cix // 4][:, 128 * (cix % 4):128 * (cix % 4) + 128])
            post(h, "act", reads=("pb0", "pb1"), writes=("Wrt",))

        pre(qs, writes=("scrstage",))
        h = qs.dma_start(out=scr[:, 1024:1536].rearrange("p (c m) -> p c m", c=8),
                         in_=ap["Win"].rearrange("(c p) m -> p c m", p=128))
        cw = post(h, "qs", writes=("scrstage",))
        dve.wait("qs", cw)
        pre(dve, reads=("scrstage",), writes=("Win2",))
        for cix in range(8):
            h = dve.tensor_copy(Win2[:, 128 * cix:128 * cix + 64],
                                scr[:, 1024 + 64 * cix:1024 + 64 * cix + 64])
            h = dve.tensor_copy(Win2[:, 128 * cix + 64:128 * cix + 128],
                                scr[:, 1024 + 64 * cix:1024 + 64 * cix + 64])
        post(h, "dve", reads=("scrstage",), writes=("Win2",))

        pre(qs, writes=("scrstage",))
        h = qs.dma_start(out=scr[0:64, 1024:2048], in_=ap["Wout"]); h.inc("qs")
        h = qs.dma_start(out=scr[0:1, 0:1024], in_=ap["bout"]); h.inc("qs")
        h = qs.dma_start(out=scr[0:64, 2691:2692], in_=ap["b2col"])
        cw = post(h, "qs", writes=("scrstage",))
        dve.wait("qs", cw)
        pre(dve, reads=("scrstage",), writes=("WoutA", "b2b"))
        h = dve.tensor_scalar(WoutA[0:64, :], scr[0:64, 1024:2048], 0.5, None, ALU.mult)
        post(h, "dve", reads=("scrstage",), writes=("WoutA",))
        h = dve.tensor_copy(statcb[0:64, 0:1], scr[0:64, 2691:2692])
        post(h, "dve", reads=("scrstage",), writes=("b2b",))
        for nh in range(2):
            pre(pe, reads=("WoutA", "b2b"), writes=("pb3",))
            h = mm(pb[3][0:1, 0:512], statcb[0:64, 0:1],
                   WoutA[0:64, 512 * nh:512 * nh + 512], True, True)
            post(h, "pe", reads=("WoutA", "b2b"), writes=("pb3",))
            pre(dve, reads=("pb3", "scrstage"), writes=("WoutArow",))
            h = dve.scalar_tensor_tensor(WoutA[64:65, 512 * nh:512 * nh + 512],
                                         scr[0:1, 512 * nh:512 + 512 * nh], 0.5,
                                         pb[3][0:1, 0:512], ALU.mult, ALU.add)
            post(h, "dve", reads=("pb3", "scrstage"), writes=("WoutArow",))

        def ln_stats(srcp, tag, pp):
            statc = statcd[pp]
            stag = "statc%d" % pp
            pre(dve, reads=(tag,), writes=(stag,))
            h = dve.reduce_sum(statc[:, 0:1], srcp, axis=AX)
            post(h, "dve", reads=(tag,), writes=(stag,))
            pre(dve, writes=(stag,))
            h = dve.memset(statc[:, 1:2], 0.0)
            post(h, "dve", writes=(stag,))
            pre(act, reads=(tag, stag), writes=("scrb", stag))
            h = act.activation(scrb[:, 0:1024], srcp, AF.Square, accum_out=statc[:, 1:2])
            post(h, "act", reads=(tag,), writes=(stag, "scrb"))
            pre(dve, reads=(stag,), writes=(stag,))
            h = dve.tensor_scalar(statc[:, 5:6], statc[:, 0:1], RD, None, ALU.mult)
            post(h, "dve", writes=("mu5_%d" % pp,))
            h = dve.tensor_scalar(statc[:, 2:3], statc[:, 1:2], RD,
                                  EPS, ALU.mult, ALU.add)
            K.sread(dve, "mu5_%d" % pp)
            h = dve.tensor_scalar(statc[:, 3:4], statc[:, 5:6],
                                  statc[:, 5:6], -1.0, ALU.mult, ALU.mult)
            K.fence(dve)
            h = dve.tensor_tensor(statc[:, 3:4], statc[:, 2:3], statc[:, 3:4], ALU.add)
            post(h, "dve", writes=(stag,))
            pre(act, reads=(stag,), writes=(stag,))
            h = act.activation(statc[:, 4:5], statc[:, 3:4], AF.Sqrt)
            post(h, "act", writes=(stag,))
            pre(dve, reads=(stag,), writes=(stag,))
            h = dve.reciprocal(statc[:, 6:7], statc[:, 4:5])
            post(h, "dve", writes=(stag,))
            K.fence(dve)
            h = dve.tensor_scalar(statc[:, 7:8], statc[:, 5:6],
                                  statc[:, 6:7], -1.0, ALU.mult, ALU.mult)
            post(h, "dve", writes=(stag,))

        def ffn_block(xin_ap, xin_tag, xout_ap, xout_tag, pp):
            statc = statcd[pp]
            stag = "statc%d" % pp
            y0, y0fm, Htile = y0d[pp], y0fmd[pp], Htiled[pp]
            gaug, ffm = gaugd[pp], ffmd[pp]
            ty0, ty0fm, tH = "y0_%d" % pp, "y0fm_%d" % pp, "Htile_%d" % pp
            tga, tffm = "gaug_%d" % pp, "ffm_%d" % pp
            pre(act, reads=(xin_tag, stag), writes=(ty0,))
            K.sread(act, stag)
            h = act.activation(y0[:], xin_ap, AF.Identity,
                               scale=statc[:, 6:7], bias=statc[:, 7:8])
            post(h, "act", reads=(xin_tag, stag), writes=(ty0,))
            pre(pe, reads=(ty0,), writes=("pb0", "pb1"))
            for k in range(8):
                h = mm(pb[k // 4][:, 128 * (k % 4):128 * (k % 4) + 128],
                       y0[:, 128 * k:128 * k + 128], ident[:], True, True)
            post(h, "pe", reads=(ty0,), writes=("pb0", "pb1"))
            evac(y0fm[:, 0:512], ty0fm, pb[0][:, 0:512], "pb0")
            evac(y0fm[:, 512:1024], ty0fm, pb[1][:, 0:512], "pb1")
            pre(pe, reads=(ty0fm, "Win2"), writes=("pb2",))
            for k in range(8):
                h = mm(pb[2][:, 0:128], Win2[:, 128 * k:128 * k + 128],
                       y0fm[:, 128 * k:128 * k + 128], k == 0, k == 7)
            post(h, "pe", reads=(ty0fm,), writes=("pb2",))
            pre(act, reads=("pb2", "b1eff"), writes=(tH,))
            for j in range(8):
                h = act.activation(Htile[:, 128 * j:128 * j + 128], pb[2][:, 0:128],
                                   AF.Relu, scale=w1c[:, j:j + 1], bias=b1eff[:, j:j + 1])
            post(h, "act", reads=("pb2",), writes=(tH,))
            pre(pe, reads=(tH, "diagW2"), writes=("pb3",))
            for j in range(8):
                h = mm(pb[3][0:64, 0:128], diagW2[:, 64 * j:64 * j + 64],
                       Htile[:, 128 * j:128 * j + 128], j == 0, j == 7)
            post(h, "pe", reads=(tH,), writes=("pb3",))
            evac(gaug[0:64, :], tga, pb[3][0:64, 0:128], "pb3",
                 extra_reads=("gaug1",))
            pre(pe, reads=(tga, "WoutA", "WoutArow", "gaug1"), writes=("pb4", "pb5"))
            for cix in range(8):
                h = mm(pb[4 + cix // 4][:, 128 * (cix % 4):128 * (cix % 4) + 128],
                       WoutA[:, 128 * cix:128 * cix + 128], gaug[:, 0:128], True, True)
            post(h, "pe", reads=(tga,), writes=("pb4", "pb5"))
            evac(ffm[:, 0:512], tffm, pb[4][:, 0:512], "pb4")
            evac(ffm[:, 512:1024], tffm, pb[5][:, 0:512], "pb5")
            pre(pe, reads=(tffm,), writes=("pw0", "pw1"))
            for cix in range(8):
                h = mm(pwide[:, 128 * cix:128 * cix + 128],
                       ffm[:, 128 * cix:128 * cix + 128], ident[:], True, True)
            post(h, "pe", reads=(tffm,), writes=("pw0", "pw1"))
            pre(dve, reads=("pw0", "pw1", xin_tag), writes=(xout_tag,))
            h = dve.scalar_tensor_tensor(xout_ap, xin_ap, 1.0, pwide[:, 0:1024],
                                         ALU.mult, ALU.add)
            post(h, "dve", reads=("pw0", "pw1", xin_tag), writes=(xout_tag,))

        for t in range(NTILE if stage >= 1 else 0):
            half = t % 2
            xs = xt[:, D * half:D * half + D]
            xtag = "xt%d" % half
            statc = statcd[half]
            stag = "statc%d" % half
            x1t, x1bf, x1fm, zrow = (x1td[half], x1bfd[half], x1fmd[half],
                                     zrowd[half])
            tx1 = "x1t_%d" % half
            pre(qs, writes=(xtag,))
            h = qs.dma_start(out=xs, in_=ap["x_own"][128 * t:128 * t + 128, :])
            cl = post(h, "qs", writes=(xtag,))
            dve.wait("qs", cl)
            act.wait("qs", cl)
            ln_stats(xs, xtag, half)
            if False and t == 0:
                pre(qs, reads=(xtag, stag))
                h = qs.dma_start(out=dbg_p[:, 0:1024], in_=xs); h.inc("qs")
                h = qs.dma_start(out=dbg_p[:, 1024:1036], in_=statc[:])
                post(h, "qs", reads=(xtag, stag))
            ffn_block(xs, xtag, x1t[:], tx1, half)
            if False and t == 0:
                pre(qs, reads=(tx1, "y0_0", "y0fm_0", "Htile_0", "gaug_0", "ffm_0"))
                h = qs.dma_start(out=dbg_p[:, 2048:3072], in_=x1t[:]); h.inc("qs")
                h = qs.dma_start(out=dbg_pb[:, 0:1024], in_=y0d[0][:]); h.inc("qs")
                h = qs.dma_start(out=dbg_pb[:, 1024:2048], in_=y0fmd[0][:]); h.inc("qs")
                h = qs.dma_start(out=dbg_pb[:, 2048:3072], in_=Htiled[0][:]); h.inc("qs")
                h = qs.dma_start(out=dbg_pb[0:65, 3072:3200], in_=gaugd[0][:]); h.inc("qs")
                h = qs.dma_start(out=dbg_pb[:, 4096:5120], in_=ffmd[0][:])
                post(h, "qs", reads=(tx1, "y0_0", "y0fm_0", "Htile_0", "gaug_0", "ffm_0"))
            ln_stats(x1t[:], tx1, half)
            pre(dve, reads=(tx1, stag), writes=("zrow%d" % half, stag))
            h = dve.tensor_scalar(statc[:, 2:3], statc[:, 6:7], 1.0, None, ALU.add)
            post(h, "dve", writes=("zp2",))
            K.fence(dve)
            h = dve.tensor_scalar(zrow[:], x1t[:], statc[:, 2:3],
                                  statc[:, 7:8], ALU.mult, ALU.add)
            post(h, "dve", reads=(tx1,), writes=("zrow%d" % half, stag))
            pre(qs, reads=("zrow%d" % half,))
            h = qs.dma_start(out=z_spill[128 * t:128 * t + 128, :], in_=zrow[:])
            post(h, "qs", reads=("zrow%d" % half,))
            pre(dve, reads=(stag,), writes=("statcb",))
            h = dve.tensor_copy(statcb[:, 0:1], statc[:, 5:6])
            h = dve.tensor_copy(statcb[:, 1:2], statc[:, 6:7])
            post(h, "dve", reads=(stag,), writes=("statcb",))
            pre(pe, reads=("statcb",), writes=("pb2",))
            h = mm(pb[2][0:2, 128:256], statcb[:, 0:2], ident[:], True, True)
            post(h, "pe", reads=("statcb",), writes=("pb2",))
            pre(act, reads=("pb2",), writes=("statrow",))
            h = act.copy(statrow[:, 128 * t:128 * t + 128], pb[2][0:2, 128:256])
            post(h, "act", reads=("pb2",), writes=("statrow",))
            pre(dve, reads=(tx1,), writes=("x1bf%d" % half,))
            h = dve.tensor_copy(x1bf[:], x1t[:])
            post(h, "dve", reads=(tx1,), writes=("x1bf%d" % half,))
            pre(pe, reads=("x1bf%d" % half,), writes=("pb0", "pb1"))
            for k in range(8):
                h = mm(pb[k // 4][:, 128 * (k % 4):128 * (k % 4) + 128],
                       x1bf[:, 128 * k:128 * k + 128], ident[:], True, True)
            post(h, "pe", reads=("x1bf%d" % half,), writes=("pb0", "pb1"))
            evac(x1fm[:, 0:512], "x1fm%d" % half, pb[0][:, 0:512], "pb0")
            evac(x1fm[:, 512:1024], "x1fm%d" % half, pb[1][:, 0:512], "pb1")
            pre(qs, reads=("x1fm%d" % half,))
            qb = (t // 4) * (D + 2)
            h = qs.dma_start(
                out=ag_in[qb:qb + D, 128 * (t % 4):128 * (t % 4) + 128]
                    .rearrange("(k p) t -> p k t", p=128),
                in_=x1fm[:].rearrange("p (k t) -> p k t", k=8))
            post(h, "qs", reads=("x1fm%d" % half,))
            if t % 4 == 3:
                q4 = t // 4
                pre(qs, reads=("statrow",))
                h = qs.dma_start(out=ag_in[qb + D:qb + D + 2, :],
                                 in_=statrow[:, 512 * q4:512 * q4 + 512])
                cq = post(h, "qs", reads=("statrow",))
                if stage >= 2:
                    pool.wait("qs", cq)
                    h = pool.collective_compute(
                        "AllGather", ALU.bypass,
                        replica_groups=[[0, 1, 2, 3], [4, 5, 6, 7]],
                        ins=[ag_in[qb:qb + (D + 2), :]],
                        outs=[ag_out[4 * qb:4 * qb + 4 * (D + 2), :]])
                    h.inc("cc")
        cc_x1 = K.c("cc")

        pre(dve, writes=("S32",))
        h = dve.memset(S32[:], 0.0)
        post(h, "dve", writes=("S32",))
        pre(act, reads=("S32",), writes=("Shi",))
        h = act.copy(Shi[:], S32[:])
        post(h, "act", reads=("S32",), writes=("Shi",))

        ccx_piece = [cc_x1 - 3, cc_x1 - 2, cc_x1 - 1, cc_x1]
        qg_seg = {}
        for s in range(NSEG if stage >= 2 else 0):
            pre(qs, writes=("statin",))
            for q4 in range(4):
                qs.wait("cc", ccx_piece[q4])
                pbase = q4 * 4 * (D + 2) + s * (D + 2)
                h = qs.dma_start(out=statin[:, 512 * q4:512 * q4 + 512],
                                 in_=ag_out[pbase + D:pbase + D + 2, :])
                if q4 < 3:
                    h.inc("qs")
            post(h, "qs", writes=("statin",))
            if s == 0:
                pre(dve, writes=("rkcarry",))
                h = dve.memset(rkd0[:, 0:1], 0.0)
                h = dve.memset(rkd1[:, 0:1], 0.0)
                post(h, "dve", writes=("rkcarry",))
            else:
                pre(dve, reads=("rkd",), writes=("rkcarry",))
                h = dve.tensor_copy(rkd0[:, 0:1], rkd0[:, SEG:SEG + 1])
                h = dve.tensor_copy(rkd1[:, 0:1], rkd1[:, SEG:SEG + 1])
                post(h, "dve", reads=("rkd",), writes=("rkcarry",))

            if DEBUG and s == 2:
                qs.wait("cc", 5)
                h = qs.dma_start(out=dbg_pb[:, 3328:3584], in_=ago_out[0:128, :])
                post(h, "qs")
            def t_dma(t):
                half = t % 2
                xin = x1fmi[:, D * half:D * half + D]
                xtag = "x1fmi%d" % half
                qs.wait("cc", ccx_piece[t // 4])
                pbase = (t // 4) * 4 * (D + 2) + s * (D + 2)
                pre(qs, reads=("statin",), writes=(xtag,))
                h = qs.dma_start(
                    out=xin.rearrange("p (k t) -> p k t", k=8),
                    in_=ag_out[pbase:pbase + D,
                               128 * (t % 4):128 * (t % 4) + 128]
                        .rearrange("(k p) t -> p k t", p=128))
                post(h, "qs", writes=(xtag,))

            def t_proj(t):
                half = t % 2
                xin = x1fmi[:, D * half:D * half + D]
                xtag = "x1fmi%d" % half
                stat2c = stat2cd[half]
                ts2 = "stat2c%d" % half
                t1tok = t1tokd[half]
                tt1 = "t1tok%d" % half
                bv = ("pb0", "pb4")[half]
                bt1 = ("pb1", "pb5")[half]
                baux = ("pb2", "pw0")[half]
                pre(pe, reads=("statin",), writes=(baux,))
                h = mm(BK[baux][0:128, 256:258], statin[0:2, 128 * t:128 * t + 128],
                       ident[0:2, 0:2], True, True)
                post(h, "pe", reads=("statin",), writes=(baux,))
                pre(act, reads=(baux,), writes=(ts2,))
                h = act.copy(stat2c[:, 0:2], BK[baux][0:128, 256:258])
                post(h, "act", reads=(baux,), writes=(ts2,))
                pre(pe, reads=(xtag, "Wwt", "statin", "negwwsum"), writes=(bv,))
                for k in range(8):
                    h = mm(BK[bv][:, 0:256], xin[:, 128 * k:128 * k + 128],
                           Wwt[:, 256 * k:256 * k + 256], k == 0, False)
                h = mm(BK[bv][:, 0:256], statin[0:1, 128 * t:128 * t + 128],
                       negwwsum[:], False, True)
                post(h, "pe", reads=(xtag, "statin"), writes=(bv,))
                pre(act, reads=(bv, ts2), writes=("vtok",))
                K.sread(act, ts2)
                h = act.activation(vtok[:, 256 * t:256 * t + 256], BK[bv][:, 0:256],
                                   AF.Identity, scale=stat2c[:, 1:2])
                post(h, "act", reads=(bv, ts2), writes=("vtok",))
                pre(pe, reads=(xtag, "EyeH", "statin"), writes=(bt1,))
                for k in range(8):
                    h = mm(BK[bt1][:, 0:256], xin[:, 128 * k:128 * k + 128],
                           EyeH[:, 256 * k:256 * k + 256], k == 0, False)
                h = mm(BK[bt1][:, 0:256], statin[0:1, 128 * t:128 * t + 128],
                       negones_row[:], False, True)
                post(h, "pe", reads=(xtag, "statin"), writes=(bt1,))
                evac(t1tok[:], tt1, BK[bt1][:, 0:256], bt1)

            def t_norm(t):
                half = t % 2
                t1tok, t1sq, rktok, nrm = (t1tokd[half], t1sq1,
                                           rktokd[half], nrmd[half])
                tt1, tsq, trk, tnr = ("t1tok%d" % half, "t1sq",
                                      "rktok%d" % half, "nrm%d" % half)
                pre(pool, reads=(tt1,), writes=(tsq,))
                h = pool.tensor_tensor(t1sq[:], t1tok[:], t1tok[:], ALU.mult)
                post(h, "pool", reads=(tt1,), writes=(tsq,))
                pre(dve, reads=(tsq,), writes=(tnr,))
                h = dve.reduce_sum(nrm[:, 0:1], t1sq[:], axis=AX)
                post(h, "dve", reads=(tsq,), writes=(tnr,))
                pre(act, reads=(tnr,), writes=(tnr,))
                h = act.activation(nrm[:, 1:2], nrm[:, 0:1], AF.Sqrt)
                post(h, "act", writes=(tnr,))
                pre(dve, reads=(tnr,), writes=(tnr,))
                h = dve.tensor_scalar(nrm[:, 2:3], nrm[:, 1:2], 1e-12, None, ALU.max)
                K.fence(dve)
                h = dve.reciprocal(nrm[:, 3:4], nrm[:, 2:3])
                post(h, "dve", writes=(tnr,))
                pre(act, reads=(tnr, tt1), writes=(trk,))
                K.sread(act, tnr)
                h = act.activation(rktok[:], t1tok[:], AF.Identity, scale=nrm[:, 3:4])
                post(h, "act", reads=(tnr, tt1), writes=(trk,))

            def t_build(t):
                half = t % 2
                rktok = rktokd[half]
                trk = "rktok%d" % half
                bwk = ("pb3", "pw1")[half]
                pre(pe, reads=(trk,), writes=(bwk,))
                h = mm(BK[bwk][:, 256:384], rktok[:, 0:128], ident[:], True, True)
                h = mm(BK[bwk][:, 384:512], rktok[:, 128:256], ident[:], True, True)
                post(h, "pe", reads=(trk,), writes=(bwk,))
                pre(act, reads=(bwk,), writes=("rkd",))
                h = act.copy(rkd0[:, 128 * t + 1:128 * t + 129], BK[bwk][:, 256:384])
                post(h, "act", reads=(bwk,), writes=("rkd",))
                pre(dve, reads=(bwk,), writes=("rkd",))
                h = dve.tensor_copy(rkd1[:, 128 * t + 1:128 * t + 129],
                                    BK[bwk][:, 384:512])
                post(h, "dve", reads=(bwk,), writes=("rkd",))
                pre(dve, reads=("rkd", "gpcol"), writes=("rkg",))
                h = dve.tensor_tensor(rkg0[:, 128 * t:128 * t + 128],
                                      rkd0[:, 128 * t + 1:128 * t + 129], gpcol[:], ALU.mult)
                h = dve.tensor_tensor(rkg1[:, 128 * t:128 * t + 128],
                                      rkd1[:, 128 * t + 1:128 * t + 129], gpcol[:], ALU.mult)
                post(h, "dve", reads=("rkd",), writes=("rkg",))
                rds = ("rkd", "rkcarry") if t == 0 else ("rkd",)
                pre(pe, reads=rds, writes=(bwk,))
                h = mm(BK[bwk][:, 0:128], rkd0[:, 128 * t:128 * t + 128], ident[:], True, True)
                h = mm(BK[bwk][:, 128:256], rkd1[:, 128 * t:128 * t + 128], ident[:], True, True)
                post(h, "pe", reads=rds, writes=(bwk,))
                evac(wktok[:, 256 * t:256 * t + 256], "wktok", BK[bwk][:, 0:256], bwk)

            wavefront(NTILE, [t_dma, t_proj, t_norm, t_build])

            def p_gram(p):
                r4 = p % 4
                powbuf, Xba = powbufd[r4], Xbad[r4]
                tpw, txb = "powbuf%d" % r4, "Xba%d" % r4
                bG = ("pb0", "pb4")[p % 2]
                wk0 = rkd0[:, 128 * p:128 * p + 128]
                wk1 = rkd1[:, 128 * p:128 * p + 128]
                rk0 = rkd0[:, 128 * p + 1:128 * p + 129]
                rk1 = rkd1[:, 128 * p + 1:128 * p + 129]
                rds = ("rkd", "rkcarry") if p == 0 else ("rkd",)
                pre(pe, reads=rds, writes=(bG,))
                h = mm(BK[bG][:, 0:128], wk0, wk0, True, False)
                h = mm(BK[bG][:, 0:128], wk1, wk1, False, True)
                h = mm(BK[bG][:, 128:256], wk0, rk0, True, False)
                h = mm(BK[bG][:, 128:256], wk1, rk1, False, True)
                post(h, "pe", reads=rds, writes=(bG,))
                pre(dve, reads=(bG, "msk0", "msk1", "msk2", "msk3"),
                    writes=(tpw, "ITXp", txb))
                h = dve.tensor_tensor(powbuf[:, 0:128], BK[bG][:, 0:128], maskN[:], ALU.mult)
                h = dve.tensor_tensor(powbuf[:, 128:256], BK[bG][:, 0:128], maskNT[:], ALU.mult)
                h = dve.tensor_tensor(ITXp[:, 128 * p:128 * p + 128], BK[bG][:, 128:256],
                                      maskITX[:], ALU.mult)
                h = dve.tensor_tensor(Xba[64:128, 0:64], BK[bG][64:128, 0:64],
                                      maskX[64:128, 0:64], ALU.mult)
                post(h, "dve", reads=(bG,), writes=(tpw, "ITXp", txb))

            def p_dbl(p, lv0, lv1):
                r4 = p % 4
                powbuf, ptprev = powbufd[r4], ptprevd[r4]
                tpw, tpt = "powbuf%d" % r4, "ptprev%d" % r4
                bPT = ("pb1", "pb5")[p % 2]
                bSQ = "pb2"
                if lv0 == 0:
                    pre(pe, reads=(tpw,), writes=(bPT,))
                    h = mm(BK[bPT][:, 0:128], ident[:], ident[:], True, False)
                    h = mm(BK[bPT][:, 0:128], powbuf[:, 0:128], ident[:], False, False)
                    post(h, "pe", reads=(tpw,), writes=(bPT,))
                    evac(ptprev[:], tpt, BK[bPT][:, 0:128], bPT)
                for lvl in range(lv0, lv1):
                    pre(pe, reads=(tpw,), writes=(bSQ,))
                    h = mm(BK[bSQ][:, 0:128], powbuf[:, 128:256], powbuf[:, 0:128],
                           True, True)
                    if lvl < 4:
                        h = mm(BK[bSQ][:, 128:256], powbuf[:, 0:128], powbuf[:, 128:256],
                               True, True)
                    post(h, "pe", reads=(tpw,), writes=(bSQ,))
                    w = 256 if lvl < 4 else 128
                    evac(powbuf[:, 0:w], tpw, BK[bSQ][:, 0:w], bSQ)
                    pre(pe, reads=(tpw, tpt), writes=())
                    h = mm(BK[bPT][:, 0:128], powbuf[:, 0:128], ptprev[:],
                           False, lvl == 4)
                    post(h, "pe", reads=(tpw, tpt), writes=(bPT,))
                    if lvl < 4:
                        evac(ptprev[:], tpt, BK[bPT][:, 0:128], bPT)
                    else:
                        evac(PTp[:, 128 * p:128 * p + 128], "PTp",
                             BK[bPT][:, 0:128], bPT)

            def p_tail(p):
                r4 = p % 4
                Xba, PTgC = Xbad[r4], PTgCd[p % 2]
                txb, tgc = "Xba%d" % r4, "PTgC%d" % (p % 2)
                bWA = ("pb0", "pb4")[p % 2]
                bWB = "pb2"
                pre(dve, reads=("PTp", "gCcol"), writes=(tgc,))
                h = dve.tensor_scalar(PTgC[64:128, 0:64],
                                      PTp[64:128, 128 * p + 64:128 * p + 128],
                                      gCcol[64:128, 0:1], None, ALU.mult)
                post(h, "dve", reads=("PTp",), writes=(tgc,))
                pre(pe, reads=("wktok", "PTp", tgc), writes=(bWA, bWB))
                h = mm(BK[bWA][:, 0:64], wktok[0:64, 256 * p:256 * p + 128],
                       PTp[0:64, 128 * p:128 * p + 64], True, True)
                h = mm(BK[bWA][:, 64:128], wktok[0:64, 256 * p + 128:256 * p + 256],
                       PTp[0:64, 128 * p:128 * p + 64], True, True)
                h = mm(BK[bWB][:, 0:64], wktok[64:128, 256 * p:256 * p + 128],
                       PTgC[64:128, 0:64], True, True)
                h = mm(BK[bWB][:, 64:128], wktok[64:128, 256 * p + 128:256 * p + 256],
                       PTgC[64:128, 0:64], True, True)
                post(h, "pe", reads=("wktok", "PTp", tgc), writes=(bWA, bWB))
                evac(wkcT0[:, 128 * p:128 * p + 64], "wkcT", BK[bWA][:, 0:64], bWA,
                     scale=-1.0)
                evac(wkcT1[:, 128 * p:128 * p + 64], "wkcT", BK[bWA][:, 64:128], bWA,
                     scale=-1.0)
                evac(wkcT0[:, 128 * p + 64:128 * p + 128], "wkcT", BK[bWB][:, 0:64],
                     bWB, scale=-1.0)
                evac(wkcT1[:, 128 * p + 64:128 * p + 128], "wkcT", BK[bWB][:, 64:128],
                     bWB, scale=-1.0)
                pre(pe, reads=(txb, "PTp"), writes=(bWB,))
                h = mm(BK[bWB][0:64, 256:320], Xba[64:128, 0:64],
                       PTp[64:128, 128 * p + 64:128 * p + 128], True, True)
                post(h, "pe", reads=(txb, "PTp"), writes=(bWB,))
                evac(MbaTn[0:64, 64 * p:64 * p + 64], "MbaTn",
                     BK[bWB][0:64, 256:320], bWB, scale=-1.0)

            def s_dump(p, off):
                if DEBUG and s == 0:
                    pre(qs, reads=("Shi", "vnew_a", "vnew_b"))
                    h = qs.dma_start(out=dbg_pb[:, off:off + 512], in_=Shi[:]); h.inc("qs")
                    h = qs.dma_start(out=dbg_pb[:, off + 512:off + 768], in_=vnewsb[:])
                    post(h, "qs", reads=("Shi", "vnew_a", "vnew_b"))

            def s_a(p):
                if p == 0:
                    s_dump(p, 0)
                pre(pe, reads=("PTp", "vtok", "wkcT", "Shi"), writes=("pw0",))
                h = mm(BK["pw0"][0:64, 0:256], PTp[0:64, 128 * p:128 * p + 64],
                       vtok[0:64, 256 * p:256 * p + 256], True, False)
                h = mm(BK["pw0"][0:64, 0:256], wkcT0[:, 128 * p:128 * p + 64],
                       Shi[:, 0:256], False, False)
                h = mm(BK["pw0"][0:64, 0:256], wkcT1[:, 128 * p:128 * p + 64],
                       Shi[:, 256:512], False, True)
                post(h, "pe", reads=("PTp", "vtok", "wkcT", "Shi"), writes=("pw0",))
                pre(act, reads=("pw0",), writes=("vnew_a",))
                h = act.copy(vnewsb[0:64, :], BK["pw0"][0:64, 0:256])
                post(h, "act", reads=("pw0",), writes=("vnew_a",))

            def s_b(p):
                pre(pe, reads=("PTp", "vtok", "wkcT", "Shi", "MbaTn", "vnew_a"),
                    writes=("pw1",))
                h = mm(BK["pw1"][64:128, 0:256], PTp[64:128, 128 * p + 64:128 * p + 128],
                       vtok[64:128, 256 * p:256 * p + 256], True, False, tp=(64, 64))
                h = mm(BK["pw1"][64:128, 0:256], wkcT0[:, 128 * p + 64:128 * p + 128],
                       Shi[:, 0:256], False, False, tp=(0, 64))
                h = mm(BK["pw1"][64:128, 0:256], wkcT1[:, 128 * p + 64:128 * p + 128],
                       Shi[:, 256:512], False, False, tp=(0, 64))
                h = mm(BK["pw1"][64:128, 0:256], MbaTn[0:64, 64 * p:64 * p + 64],
                       vnewsb[0:64, :], False, True, tp=(0, 64))
                post(h, "pe", reads=("PTp", "vtok", "wkcT", "Shi", "MbaTn", "vnew_a"),
                     writes=("pw1",))
                pre(act, reads=("pw1",), writes=("vnew_b",))
                h = act.copy(vnewsb[64:128, :], BK["pw1"][64:128, 0:256])
                post(h, "act", reads=("pw1",), writes=("vnew_b",))
                pre(dve, reads=("pw0", "pw1", "gtpair"), writes=("vssb",))
                h = dve.tensor_scalar(vssb[0:64, :], BK["pw0"][0:64, 0:256],
                                      gtpair[0:64, 0:1], None, ALU.mult)
                h = dve.tensor_scalar(vssb[64:128, :], BK["pw1"][64:128, 0:256],
                                      gtpair[64:128, 0:1], None, ALU.mult)
                post(h, "dve", reads=("pw0", "pw1"), writes=("vssb",))

            def s_o(p):
                pre(pe, reads=("rkg", "Shi", "ITXp", "vnew_a", "vnew_b"), writes=("pw0",))
                h = mm(BK["pw0"][:, 256:512], rkg0[:, 128 * p:128 * p + 128], Shi[:, 0:256],
                       True, False)
                h = mm(BK["pw0"][:, 256:512], rkg1[:, 128 * p:128 * p + 128], Shi[:, 256:512],
                       False, False)
                h = mm(BK["pw0"][:, 256:512], ITXp[:, 128 * p:128 * p + 128], vnewsb[:],
                       False, True)
                post(h, "pe", reads=("rkg", "Shi", "ITXp", "vnew_a", "vnew_b"),
                     writes=("pw0",))
                osb = osbd[p % 2]
                tosb = "osb%d" % (p % 2)
                pre(act, reads=("pw0", "alphacol"), writes=(tosb,))
                h = act.activation(osb[:], BK["pw0"][:, 256:512], AF.Identity,
                                   scale=alphacol[:, 0:1])
                post(h, "act", reads=("pw0",), writes=(tosb,))
                if p > 0:
                    q = p - 1
                    pre(qs, reads=("osb%d" % (q % 2),))
                    h = qs.dma_start(out=ago_in_l[s][128 * q:128 * q + 128, :],
                                     in_=osbd[q % 2][:])
                    post(h, "qg", reads=("osb%d" % (q % 2),))
                if p == NTILE - 1:
                    pre(qs, reads=(tosb, "Shi"))
                    h = qs.dma_start(out=ago_in_l[s][128 * p:128 * p + 128, :],
                                     in_=osb[:])
                    post(h, "qg", reads=(tosb,))

            def s_s(p):
                if p == 1:
                    s_dump(p, 1024)
                if DEBUG and s == 0 and p == 0:
                    pre(qs, reads=("osb", "ITXp", "rkg"))
                    h = qs.dma_start(out=dbg_pb[:, 2048:2304], in_=osb[:]); h.inc("qs")
                    h = qs.dma_start(out=dbg_pb[:, 2304:2432], in_=ITXp[:, 0:128]); h.inc("qs")
                    h = qs.dma_start(out=dbg_pb[:, 2432:2560], in_=rkg0[:, 0:128]); h.inc("qs")
                    h = qs.dma_start(out=dbg_pb[:, 2560:2688], in_=rkg1[:, 0:128])
                    post(h, "qs", reads=("osb", "ITXp", "rkg"))
                pre(pe, reads=("wktok", "vssb"), writes=("pb3",))
                h = mm(pb[3][:, 0:256], wktok[:, 256 * p:256 * p + 128], vssb[:],
                       True, True)
                h = mm(pb[3][:, 256:512], wktok[:, 256 * p + 128:256 * p + 256], vssb[:],
                       True, True)
                post(h, "pe", reads=("wktok", "vssb"), writes=("pb3",))
                pre(dve, reads=("pb3", "gC2col", "S32"), writes=("S32",))
                K.fence(dve)
                h = dve.scalar_tensor_tensor(S32[:], S32[:], gC2col[:, 0:1],
                                             pb[3][:, 0:512], ALU.mult, ALU.add)
                post(h, "dve", reads=("pb3",), writes=("S32",))
                pre(act, reads=("S32",), writes=("Shi",))
                h = act.copy(Shi[:], S32[:])
                post(h, "act", reads=("S32",), writes=("Shi",))

            if stage >= 3:
                LAG = 5
                for w in range(NTILE + LAG):
                    if stage >= 4 and LAG <= w:
                        s_a(w - LAG)
                    if 4 <= w < NTILE + 4:
                        p_tail(w - 4)
                    if stage >= 4 and LAG <= w:
                        s_b(w - LAG)
                    if 3 <= w < NTILE + 3:
                        p_dbl(w - 3, 4, 5)
                    if stage >= 4 and LAG <= w:
                        s_o(w - LAG)
                    if 2 <= w < NTILE + 2:
                        p_dbl(w - 2, 2, 4)
                    if stage >= 4 and LAG <= w:
                        s_s(w - LAG)
                    if 1 <= w < NTILE + 1:
                        p_dbl(w - 1, 0, 2)
                    if w < NTILE:
                        p_gram(w)

            if stage >= 5 and OAG_PIECED and s > 0:
                if DEBUG and s == 1 and K.R("dumpgate").writer:
                    pool.wait(*K.R("dumpgate").writer)
                pool.wait("qg", qg_seg[s - 1])
                h = pool.collective_compute(
                    "AllGather", ALU.bypass,
                    replica_groups=[[0, 1, 2, 3], [4, 5, 6, 7]],
                    ins=[ago_in_l[s - 1]],
                    outs=[ago_out[(s - 1) * 4 * SEG:s * 4 * SEG, :]])
                h.inc("cc")
            if DEBUG:
                pre(qs, reads=("S32",))
                h = qs.dma_start(out=dbg_p[:, 512 * s:512 * s + 512], in_=S32[:])
                post(h, "qs", reads=("S32",))
                if False:
                    pre(qs, reads=("vtok", "wkcT", "PTp", "ITXp", "MbaTn"))
                    h = qs.dma_start(out=dbg_pb[:, 0:2048], in_=PTp[:]); h.inc("qs")
                    h = qs.dma_start(out=dbg_pb[:, 2048:4096], in_=ITXp[:]); h.inc("qs")
                    h = qs.dma_start(out=dbg_pb[:, 4096:6144], in_=wkcT0[:]); h.inc("qs")
                    h = qs.dma_start(out=dbg_pb[:, 6144:7168], in_=vtok[:, 0:1024]); h.inc("qs")
                    h = qs.dma_start(out=dbg_pb[0:64, 7168:8192], in_=MbaTn[:])
                    post(h, "qs", reads=("vtok", "wkcT"))
            if DEBUG and s == 0:
                qs.wait("qg", K.c("qg"))
                h = qs.dma_start(out=dbg_pb[:, 2688:2944],
                                 in_=ago_in_l[0][0:128, :])
                dump_val = post(h, "qs")
                K.R("dumpgate").writer = ("qs", dump_val)
            qg_seg[s] = K.c("qg")
            if stage >= 5 and OAG_PIECED and s == NSEG - 1:
                pool.wait("qg", qg_seg[s])
                h = pool.collective_compute(
                    "AllGather", ALU.bypass,
                    replica_groups=[[0, 1, 2, 3], [4, 5, 6, 7]],
                    ins=[ago_in_l[s]],
                    outs=[ago_out[s * 4 * SEG:(s + 1) * 4 * SEG, :]])
                h.inc("cc")
        if stage >= 5 and not OAG_PIECED:
            pool.wait("qg", K.c("qg"))
            for s4 in range(NSEG):
                h = pool.collective_compute(
                    "AllGather", ALU.bypass,
                    replica_groups=[[0, 1, 2, 3], [4, 5, 6, 7]],
                    ins=[ago_in_l[s4]],
                    outs=[ago_out[s4 * 4 * SEG:(s4 + 1) * 4 * SEG, :]])
                h.inc("cc")
        cc_o = K.c("cc")

        cc_o = K.c("cc")

        qs.wait("cc", cc_o)
        for t in range(NTILE if stage >= 5 else 0):
            half = t % 2
            zt = ztile[:, D * half:D * half + D]
            ztag = "ztile%d" % half
            oblk = oblkd[half]
            o_fm = o_fmd[half]
            x2t = x2td[half]
            tob, tofm, tx2 = "oblk%d" % half, "o_fm%d" % half, "x2t_%d" % half
            btr = (("pb0", "pb1"), ("pb4", "pb5"))[half]
            pre(qs, writes=(tob, ztag))
            for j in range(4):
                for k in range(4):
                    row = k * 4 * SEG + j * SEG + 128 * t
                    h = qs.dma_start(out=oblk[:, HD * (4 * j + k):HD * (4 * j + k) + HD],
                                     in_=ago_out[row:row + 128, :])
                    h.inc("qs")
            h = qs.dma_start(out=zt, in_=z_spill[128 * t:128 * t + 128, :])
            cl = post(h, "qs", writes=(tob, ztag))
            pe.wait("qs", cl)
            pre(pe, reads=(tob, "maskI"), writes=btr)
            for j in range(4):
                for r in range(2):
                    cix = 2 * j + r
                    bank = BK[btr[cix // 4]]
                    off = 128 * (cix % 4)
                    for k in range(4):
                        h = mm(bank[:, off:off + 128],
                               oblk[:, HD * (4 * j + k) + 128 * r:
                                    HD * (4 * j + k) + 128 * r + 128],
                               maskI[:, 128 * k:128 * k + 128], k == 0, k == 3)
            post(h, "pe", reads=(tob, "maskI"), writes=btr)
            evac(o_fm[:, 0:512], tofm, BK[btr[0]][:, 0:512], btr[0])
            evac(o_fm[:, 512:1024], tofm, BK[btr[1]][:, 0:512], btr[1])
            pre(pe, reads=(tofm, "Wrt"), writes=("pw0", "pw1"))
            for cix in range(8):
                for nh in range(2):
                    h = mm(pwide[:, 512 * nh:512 * nh + 512],
                           o_fm[:, 128 * cix:128 * cix + 128],
                           Wrt[:, 1024 * cix + 512 * nh:1024 * cix + 512 * nh + 512],
                           cix == 0, cix == 7)
            post(h, "pe", reads=(tofm, "Wrt"), writes=("pw0", "pw1"))
            pre(dve, reads=("pw0", "pw1", ztag), writes=(tx2,))
            h = dve.scalar_tensor_tensor(x2t[:], zt, 1.0, pwide[:, 0:1024],
                                         ALU.mult, ALU.add)
            post(h, "dve", reads=("pw0", "pw1", ztag), writes=(tx2,))
            ln_stats(x2t[:], tx2, half)
            ffn_block(x2t[:], tx2, x1td[half][:], "x1t_%d" % half, half)
            pre(qs, reads=("x1t_%d" % half,))
            h = qs.dma_start(out=out_own[128 * t:128 * t + 128, :], in_=x1td[half][:])
            post(h, "qs", reads=("x1t_%d" % half,))
        if DEBUG:
            qs.wait("qs", K.c("qs"))
            qs.wait("qg", K.c("qg"))
            h = qs.dma_start(out=dbg_o, in_=ago_out[0:T, :]); h.inc("qs")
            h = qs.dma_start(out=dbg_z, in_=z_spill); h.inc("qs")
        if stage < 5:
            pre(dve, writes=("zrow",))
            h = dve.memset(zrow[:], 0.0)
            post(h, "dve", writes=("zrow",))
            for t in range(NTILE):
                pre(qs, reads=("zrow",))
                h = qs.dma_start(out=out_own[128 * t:128 * t + 128, :], in_=zrow[:])
                post(h, "qs", reads=("zrow",))
        qs.wait("qs", K.c("qs"))
        pool.wait("qg", K.c("qg"))

        K.emit(nc, sems)
    return nc


_NC_CACHE = None


def _build_in_maps(inputs):
    x = np.asarray(inputs["L0_attn_out"], dtype=np.float32)
    Ww = np.asarray(inputs["L0_attn_Ww"], dtype=np.float32)
    Wr = np.asarray(inputs["L0_attn_Wr"], dtype=np.float32)
    decay = np.asarray(inputs["L0_attn_decay"], dtype=np.float32)
    log_alpha = np.asarray(inputs["L0_attn_log_alpha"], dtype=np.float32)
    Win = np.asarray(inputs["L0_ffn_Win"], dtype=np.float32)
    bin_ = np.asarray(inputs["L0_ffn_bin_"], dtype=np.float32)
    W1 = np.asarray(inputs["L0_ffn_W1"], dtype=np.float32)
    b1 = np.asarray(inputs["L0_ffn_b1"], dtype=np.float32)
    W2 = np.asarray(inputs["L0_ffn_W2"], dtype=np.float32)
    b2 = np.asarray(inputs["L0_ffn_b2"], dtype=np.float32)
    Wout = np.asarray(inputs["L0_ffn_Wout"], dtype=np.float32)
    bout = np.asarray(inputs["L0_ffn_bout"], dtype=np.float32)

    consts = _host_consts()
    W1c = _w_cols(W1)
    b1c = _w_cols(b1)
    W2c = _w_cols(W2)
    binc = np.ascontiguousarray(bin_[np.arange(128) % 64][:, None].astype(np.float32))
    in_maps = []
    for c in range(NCORE):
        b, h = c // 4, c % 4
        s = c % 4
        eyeh = np.zeros((128, 8 * HD), dtype=np.float32)
        eyeh[np.arange(128), HD * 2 * h + np.arange(128)] = 1.0
        eyeh[np.arange(128), HD * (2 * h + 1) + 128 + np.arange(128)] = 1.0
        m = {
            "x_own": np.ascontiguousarray(x[b, s * SEG:(s + 1) * SEG, :]),
            "Wwh": np.ascontiguousarray(Ww[h * HD:(h + 1) * HD, :]),
            "Wr": Wr,
            "Win": Win,
            "W1c": W1c, "b1c": b1c, "W2c": W2c, "binc": binc,
            "Wout": Wout, "bout": np.ascontiguousarray(bout[None, :]),
            "b2col": np.ascontiguousarray(b2[:, None]),
            "decay_h": np.ascontiguousarray(decay[h:h + 1][:, None]),
            "log_alpha_h": np.ascontiguousarray(log_alpha[h:h + 1][:, None]),
            "segmask": np.ascontiguousarray(
                (np.arange(4) == s).astype(np.float32)[None, :]),
            "EyeH": eyeh.astype(bf16),
        }
        for kk, v in consts.items():
            m[kk] = v
        in_maps.append(m)
    return in_maps


def kernel(**inputs):
    global _NC_CACHE
    if _NC_CACHE is None:
        _NC_CACHE = build_nc()
    in_maps = _build_in_maps(inputs)
    res = run_bass_kernel_spmd(_NC_CACHE, in_maps, core_ids=list(range(NCORE)))
    out = np.zeros((B, T, D), dtype=np.float32)
    for c in range(NCORE):
        b, s = c // 4, c % 4
        out[b, s * SEG:(s + 1) * SEG, :] = res.results[c]["out_own"]
    return out


# version 08
# speedup vs baseline: 1.5368x; 1.2370x over previous
import contextlib
import numpy as np
import ml_dtypes
import concourse.bass as bass
import concourse.mybir as mybir
from concourse.bass_utils import run_bass_kernel_spmd

F32 = mybir.dt.float32
BF16 = mybir.dt.bfloat16
AF = mybir.ActivationFunctionType
ALU = mybir.AluOpType
AX = mybir.AxisListType.X
bf16 = ml_dtypes.bfloat16

B, T, D, H, HD, C = 2, 8192, 1024, 4, 256, 64
PAIR = 2 * C
SEG = 2048
NSEG = 4
NTILE = SEG // 128
NCORE = 8
EPS = 1e-5
RD = 1.0 / D


class _Op:
    __slots__ = ("eng", "kind", "attr", "args", "kwargs", "incs", "sem", "val")


class _Handle:
    def __init__(self, ker, op):
        self.ker, self.op = ker, op
        self.vals = {}

    def inc(self, sem, amt=None):
        if sem in self.vals:
            return self.vals[sem]
        amt = amt if amt is not None else (16 if sem.startswith("q") else 1)
        self.op.incs.append((sem, amt))
        self.ker.cnt[sem] = self.ker.cnt.get(sem, 0) + amt
        self.vals[sem] = self.ker.cnt[sem]
        return self.vals[sem]


class EngQ:
    OWN = {"pe": "pe"}
    AUTOINC = ("act", "dve", "pool")
    NOINC = ("dma_start", "collective_compute", "wait_ge")

    def __init__(self, ker, name):
        self._ker, self._name = ker, name

    def __getattr__(self, attr):
        ker, name = self._ker, self._name

        def call(*args, **kwargs):
            op = _Op()
            op.eng, op.kind, op.attr, op.args, op.kwargs, op.incs = (
                name, "ins", attr, args, kwargs, [])
            ker.ops.append(op)
            h = _Handle(ker, op)
            if name in EngQ.AUTOINC and attr not in EngQ.NOINC:
                h.inc(name)
            return h

        return call

    def wait(self, sem, val, hard=False):
        ker, name = self._ker, self._name
        if val <= 0:
            return
        if not hard and EngQ.OWN.get(name) == sem:
            return
        key = (name, sem)
        if ker.wm.get(key, 0) >= val:
            return
        ker.wm[key] = val
        op = _Op()
        op.eng, op.kind, op.sem, op.val = name, "wait", sem, val
        ker.ops.append(op)


class Res:
    __slots__ = ("writer", "readers")

    def __init__(self):
        self.writer = None
        self.readers = []


class Ker:
    def __init__(self):
        self.ops = []
        self.cnt = {}
        self.wm = {}
        self.res = {}
        for n in ("pe", "act", "dve", "pool", "qs"):
            setattr(self, n, EngQ(self, n))

    def R(self, name):
        if name not in self.res:
            self.res[name] = Res()
        return self.res[name]

    def c(self, sem):
        return self.cnt.get(sem, 0)

    def fence(self, q):
        name = q._name
        q.wait(name, self.cnt.get(name, 0))

    def sread(self, q, *names):
        for r in names:
            w = self.R(r).writer
            if w:
                q.wait(w[0], w[1], hard=True)

    def pre(self, q, reads=(), writes=()):
        for r in reads:
            w = self.R(r).writer
            if w:
                q.wait(*w)
        for wname in writes:
            rr = self.R(wname)
            for rd in rr.readers:
                q.wait(*rd)
            rr.readers = []
            if rr.writer:
                q.wait(*rr.writer)

    def post(self, h, sem, reads=(), writes=()):
        v = h.inc(sem)
        for r in reads:
            self.R(r).readers.append((sem, v))
        for w in writes:
            rr = self.R(w)
            rr.writer = (sem, v)
            rr.readers = []
        return v

    def emit(self, nc, sems):
        by_eng = {"pe": [], "act": [], "dve": [], "pool": [], "qs": []}
        for op in self.ops:
            by_eng[op.eng].append(op)

        def runner(eng, ops):
            for op in ops:
                if op.kind == "wait":
                    eng.wait_ge(sems[op.sem], op.val)
                else:
                    ins = getattr(eng, op.attr)(*op.args, **op.kwargs)
                    for (sname, amt) in op.incs:
                        ins.then_inc(sems[sname], amt)

        with nc.Block() as block:
            @block.tensor
            def _(eng):
                runner(eng, by_eng["pe"])

            @block.scalar
            def _(eng):
                runner(eng, by_eng["act"])

            @block.vector
            def _(eng):
                runner(eng, by_eng["dve"])

            @block.gpsimd
            def _(eng):
                runner(eng, by_eng["pool"])

            @block.sync
            def _(eng):
                runner(eng, by_eng["qs"])


def _host_consts():
    i_ = np.arange(PAIR)[:, None].astype(np.int64)
    j_ = np.arange(PAIR)[None, :].astype(np.int64)
    same = (i_ // C) == (j_ // C)
    strict = same & (i_ > j_)
    big = 1.0e4
    diffN = np.where(strict, (i_ - j_).astype(np.float64), big).astype(np.float32)
    signN = np.where(strict, -1.0, 0.0).astype(np.float32)
    diffNT = np.ascontiguousarray(diffN.T)
    signNT = np.ascontiguousarray(signN.T)
    diffITX = np.where(strict.T, (i_.T - j_.T).astype(np.float64), big)
    signITX = np.where(strict.T, 1.0, 0.0)
    jj = np.arange(C)[:, None]
    ii = np.arange(C)[None, :]
    diffITX[0:C, C:PAIR] = 63 - jj + ii
    signITX[0:C, C:PAIR] = 1.0
    diffX = np.full((PAIR, PAIR), big)
    diffX[C:PAIR, 0:C] = np.tile((63 - np.arange(C))[None, :], (C, 1))
    signX = np.zeros((PAIR, PAIR))
    signX[C:PAIR, 0:C] = 1.0
    colidx = np.tile(np.arange(PAIR, dtype=np.float32)[None, :], (PAIR, 1))
    c127 = (127.0 - np.arange(PAIR, dtype=np.float32))[:, None]
    c64 = np.full((PAIR, 1), 64.0, np.float32)
    c128 = np.full((PAIR, 1), 128.0, np.float32)
    return dict(
        diffN=diffN, signN=signN, diffNT=diffNT, signNT=signNT,
        diffITX=diffITX.astype(np.float32), signITX=signITX.astype(np.float32),
        diffX=diffX.astype(np.float32), signX=signX.astype(np.float32),
        colidx=colidx, c127=c127, c64=c64, c128=c128,
        ident=np.eye(PAIR, dtype=np.float32).astype(bf16),
        foldF=(np.arange(128)[:, None] % 64 == np.arange(64)[None, :]
               ).astype(np.float32).astype(bf16),
        ones_row=np.ones((1, PAIR), np.float32).astype(bf16),
        ones_col=np.ones((PAIR, 1), np.float32).astype(bf16),
        negones_row=np.full((1, HD), -1.0, np.float32).astype(bf16),
    )


_CONST_DTYPES = {"ident": BF16, "ones_row": BF16, "ones_col": BF16,
                 "negones_row": BF16, "foldF": BF16}


def _w_cols(Wx):
    out = np.zeros((128, 8), dtype=np.float32)
    p = np.arange(128)
    for j in range(8):
        out[:, j] = Wx[p % 64, 2 * j + p // 64]
    return out


STAGE = 5
PSUB = 99
DEBUG = False
OAG_PIECED = True


def build_nc(stage=None):
    stage = STAGE if stage is None else stage
    nc = bass.Bass("TRN2", target_bir_lowering=False, debug=False)
    ap = {}

    def din(name, shape, dtype=F32):
        ap[name] = nc.dram_tensor(name, shape, dtype, kind="ExternalInput").ap()
        return ap[name]

    din("x_own", [SEG, D])
    din("Wwh", [HD, D])
    din("Wr", [D, D])
    din("Win", [D, 64])
    din("W1c", [128, 8]); din("b1c", [128, 8]); din("W2c", [128, 8])
    din("binc", [128, 1])
    din("Wout", [64, D]); din("bout", [1, D]); din("b2col", [64, 1])
    din("decay_h", [1, 1]); din("log_alpha_h", [1, 1])
    din("segmask", [1, 4])
    din("EyeH", [128, 8 * HD], BF16)
    for k, v in _host_consts().items():
        din(k, list(v.shape), _CONST_DTYPES.get(k, F32))

    out_own = nc.dram_tensor("out_own", [SEG, D], F32, kind="ExternalOutput").ap()
    if DEBUG:
        dbg_ag = nc.dram_tensor("dbg_ag", [D + 2, SEG], BF16, kind="ExternalOutput").ap()
        dbg_o = nc.dram_tensor("dbg_o", [T, HD], BF16, kind="ExternalOutput").ap()
        dbg_z = nc.dram_tensor("dbg_z", [SEG, D], F32, kind="ExternalOutput").ap()
        dbg_p = nc.dram_tensor("dbg_p", [128, 4096], F32, kind="ExternalOutput").ap()
        dbg_pb = nc.dram_tensor("dbg_pb", [128, 8192], BF16, kind="ExternalOutput").ap()

    ag_in = nc.dram_tensor("ag_in", [4 * (D + 2), 512], BF16).ap()
    ag_out = nc.dram_tensor("ag_out", [16 * (D + 2), 512], BF16).ap()
    ago_in_l = [nc.dram_tensor("ago_in%d" % i, [SEG, HD], BF16).ap()
                for i in range(NSEG)]
    ago_out = nc.dram_tensor("ago_out", [NSEG * T, HD], BF16).ap()
    z_spill = nc.dram_tensor("z_spill", [SEG, D], F32).ap()
    bc_buf = nc.dram_tensor("bc_buf", [8, 1], F32).ap()

    K = Ker()
    sem_names = ["pe", "act", "dve", "pool", "qs", "qg", "cc", "qA", "qB"]

    with contextlib.ExitStack() as ctx:
        def sb(name, shape, dtype):
            return ctx.enter_context(nc.sbuf_tensor(name, shape, dtype))

        def psum(name, shape):
            return ctx.enter_context(nc.psum_tensor(name, shape, F32))

        sems = {n: ctx.enter_context(nc.semaphore(n)) for n in sem_names}

        ident = sb("identb", [128, 128], BF16)
        ones_row = sb("onesrow", [1, 128], BF16)
        ones_col = sb("onescol", [128, 1], BF16)
        negones_row = sb("negonesrow", [1, HD], BF16)
        EyeH = sb("EyeHs", [128, 8 * HD], BF16)
        Wwt = sb("Wwt", [128, 8 * HD], BF16)
        negwwsum = sb("negwwsum", [1, HD], BF16)
        Wrt = sb("Wrt", [128, 8 * D], BF16)
        Win2 = sb("Win2", [128, 8 * 128], BF16)
        WoutA = sb("WoutA", [65, 8 * 128], BF16)
        w1c = sb("w1c", [128, 8], F32)
        b1eff = sb("b1eff", [128, 8], F32)
        diagW2 = sb("diagW2", [128, 8 * 64], BF16)
        foldF = sb("foldFs", [128, 64], BF16)
        maskN = sb("maskN", [128, 128], F32)
        maskNT = sb("maskNT", [128, 128], F32)
        maskITX = sb("maskITX", [128, 128], F32)
        maskX = sb("maskX", [128, 128], F32)
        gpcol = sb("gpcol", [128, 128], F32)
        gtpair = sb("gtpair", [128, 1], F32)
        gC2col = sb("gC2col", [128, 1], F32)
        gCcol = sb("gCcol", [128, 1], F32)
        alphacol = sb("alphacol", [128, 1], F32)
        lngcol = sb("lngcol", [128, 1], F32)
        maskI = sb("maskI", [128, 4 * 128], BF16)
        smallA = sb("smallA", [1, 8], F32)
        smallB = sb("smallB", [1, 8], F32)

        def sbd(name, shape, dtype, n=2):
            return [sb(name + chr(65 + i), shape, dtype) for i in range(n)]

        xt = sb("xt", [128, 2 * D], F32)
        y0d = sbd("y0", [128, D], BF16)
        y0fmd = sbd("y0fm", [128, D], BF16)
        Htiled = sbd("Htile", [128, 8 * 128], BF16)
        gaugd = sbd("gaug", [65, 128], BF16)
        ffmd = sbd("ffm", [128, D], BF16)
        x1td = sbd("x1t", [128, D], F32)
        x1bfd = sbd("x1bf", [128, D], BF16)
        x1fmd = sbd("x1fm", [128, D], BF16)
        statcd = sbd("statc", [128, 12], F32)
        statcb = sb("statcb", [128, 2], BF16)
        statrow = sb("statrow", [2, SEG], BF16)
        zrowd = sbd("zrow", [128, D], F32)

        x1fmi = sb("x1fmi", [128, 2 * D], BF16)
        statin = sb("statin", [2, SEG], BF16)
        stat2cd = sbd("stat2c", [128, 4], F32)
        vtok = sb("vtok", [128, NTILE * HD], BF16)
        wktok = sb("wktok", [128, NTILE * HD], BF16)
        rkd0 = sb("rkd0", [128, SEG + 1], BF16)
        rkd1 = sb("rkd1", [128, SEG + 1], BF16)
        rkg0 = sb("rkg0", [128, SEG], BF16)
        rkg1 = sb("rkg1", [128, SEG], BF16)
        wkcT0 = sb("wkcT0", [128, SEG], BF16)
        wkcT1 = sb("wkcT1", [128, SEG], BF16)
        t1tokd = sbd("t1tok", [128, HD], BF16)
        t1sq1 = sb("t1sq", [128, HD], BF16)
        rktokd = sbd("rktok", [128, HD], BF16)
        nrmd = sbd("nrm", [128, 4], F32)
        PTp = sb("PTp", [128, NTILE * 128], BF16)
        PTgCd = sbd("PTgC", [128, 64], BF16)
        ITXp = sb("ITXp", [128, NTILE * 128], BF16)
        Xbad = sbd("Xba", [128, 64], BF16, 4)
        MbaTn = sb("MbaTn", [64, NTILE * 64], BF16)
        powbufd = sbd("powbuf", [128, 256], BF16, 4)
        ptprevd = sbd("ptprev", [128, 128], BF16, 4)
        vnewsb = sb("vnewsb", [128, HD], BF16)
        vssb = sb("vssb", [128, HD], BF16)
        osbd = sbd("osb", [128, HD], BF16)
        S32 = sb("S32", [128, 2 * HD], F32)
        Shi = sb("Shi", [128, 2 * HD], BF16)

        oblkd = sbd("oblk", [128, 16 * HD], BF16)
        o_fmd = sbd("o_fm", [128, 8 * 128], BF16)
        x2td = sbd("x2t", [128, D], F32)
        ztile = sb("ztile", [128, 2 * D], F32)

        scr = sb("scratch", [128, 2944], F32)
        scr2 = sb("scratch2", [128, 24], F32)
        scrb = sb("scratchb", [128, 1024], BF16)

        pb = [psum("pb%d" % i, [128, 512]) for i in range(6)]
        pwide = psum("pwide", [128, 1024])

        qs, pool, pe, act, dve = K.qs, K.pool, K.pe, K.act, K.dve
        pre, post = K.pre, K.post

        def mm(out, lhsT, rhs, start, stop, tp=None):
            kw = {"start": start, "stop": stop}
            if tp is not None:
                kw["tile_position"] = tp
            return pe.matmul(out, lhsT, rhs, **kw)

        BK = {"pb%d" % i: pb[i] for i in range(6)}
        BK["pw0"] = pwide[:, 0:512]
        BK["pw1"] = pwide[:, 512:1024]

        ev_tog = [0]

        def wavefront(n, stages):
            ns = len(stages)
            for w in range(n + ns - 1):
                for si in reversed(range(ns)):
                    i = w - si
                    if 0 <= i < n:
                        stages[si](i)

        def evac(dst, dtag, srcp, ptag, scale=None, extra_reads=()):
            e = ev_tog[0]
            ev_tog[0] ^= 1
            if e == 0:
                pre(act, reads=(ptag,) + tuple(extra_reads), writes=(dtag,))
                if scale is None:
                    h = act.copy(dst, srcp)
                else:
                    h = act.activation(dst, srcp, AF.Identity,
                                       scale=np.float32(scale))
                post(h, "act", reads=(ptag,) + tuple(extra_reads), writes=(dtag,))
            else:
                pre(dve, reads=(ptag,) + tuple(extra_reads), writes=(dtag,))
                if scale is None:
                    h = dve.tensor_copy(dst, srcp)
                else:
                    h = dve.tensor_scalar(dst, srcp, float(scale), None, ALU.mult)
                post(h, "dve", reads=(ptag,) + tuple(extra_reads), writes=(dtag,))

        h = qs.dma_start(out=ident[:], in_=ap["ident"]); h.inc("qs")
        h = qs.dma_start(out=foldF[:], in_=ap["foldF"]); h.inc("qs")
        h = qs.dma_start(out=ones_row[:], in_=ap["ones_row"]); h.inc("qs")
        h = qs.dma_start(out=ones_col[:], in_=ap["ones_col"]); h.inc("qs")
        h = qs.dma_start(out=negones_row[:], in_=ap["negones_row"]); h.inc("qs")
        h = qs.dma_start(out=EyeH[:], in_=ap["EyeH"]); h.inc("qs")
        h = qs.dma_start(out=w1c[:], in_=ap["W1c"]); h.inc("qs")
        h = qs.dma_start(out=scr2[:, 0:8], in_=ap["b1c"]); h.inc("qs")
        h = qs.dma_start(out=scr2[:, 8:16], in_=ap["W2c"]); h.inc("qs")
        h = qs.dma_start(out=scr2[:, 16:17], in_=ap["binc"]); h.inc("qs")
        h = qs.dma_start(out=smallA[0:1, 0:1], in_=ap["decay_h"]); h.inc("qs")
        h = qs.dma_start(out=smallA[0:1, 1:2], in_=ap["log_alpha_h"]); h.inc("qs")
        h = qs.dma_start(out=scr2[0:1, 17:21], in_=ap["segmask"]); h.inc("qs")
        c0 = K.c("qs")

        act.wait("qs", c0)
        h = act.activation(smallB[0:1, 0:1], smallA[0:1, 0:1], AF.Exp, scale=np.float32(-1.0))
        post(h, "act", writes=("smB",))
        pre(dve, reads=("smB",))
        h = dve.tensor_scalar(smallB[0:1, 0:1], smallB[0:1, 0:1], 1.0, None, ALU.add)
        K.fence(dve)
        h = dve.reciprocal(smallB[0:1, 1:2], smallB[0:1, 0:1])
        post(h, "dve", writes=("smB",))
        pre(act, reads=("smB",))
        h = act.activation(smallB[0:1, 2:3], smallB[0:1, 1:2], AF.Ln)
        h = act.activation(smallB[0:1, 3:4], smallA[0:1, 1:2], AF.Exp)
        post(h, "act", writes=("smB2",))
        pre(qs, reads=("smB2",))
        h = qs.dma_start(out=bc_buf[0:1, 0:1], in_=smallB[0:1, 2:3]); h.inc("qs")
        h = qs.dma_start(out=bc_buf[1:2, 0:1], in_=smallB[0:1, 3:4]); cbc = h.inc("qs")
        qs.wait("qs", cbc)
        h = qs.dma_start(out=lngcol[:], in_=bc_buf[0:1, 0:1].broadcast_to([128, 1]))
        post(h, "qs", writes=("lngcol",))
        h = qs.dma_start(out=alphacol[:], in_=bc_buf[1:2, 0:1].broadcast_to([128, 1]))
        post(h, "qs", writes=("alphacol",))

        mask_specs = [("diffN", "signN", maskN), ("diffNT", "signNT", maskNT),
                      ("diffITX", "signITX", maskITX), ("diffX", "signX", maskX)]
        for idx, (dn, sn, dst) in enumerate(mask_specs):
            o1 = 2048 + 256 * (idx % 2)
            o2 = o1 + 128
            pre(qs, writes=("mload%d" % (idx % 2),))
            h = qs.dma_start(out=scr[:, o1:o1 + 128], in_=ap[dn]); h.inc("qs")
            h = qs.dma_start(out=scr[:, o2:o2 + 128], in_=ap[sn])
            cm = post(h, "qs", writes=("mload%d" % (idx % 2),))
            act.wait("qs", cm)
            pre(act, reads=("lngcol",), writes=("msk%d" % idx,))
            h = act.activation(dst[:], scr[:, o1:o1 + 128], AF.Exp,
                               scale=lngcol[:, 0:1])
            post(h, "act", reads=("mload%d" % (idx % 2),), writes=("msk%d" % idx,))
            pre(dve, reads=("msk%d" % idx,), writes=("msk%d" % idx,))
            h = dve.tensor_tensor(dst[:], dst[:], scr[:, o2:o2 + 128], ALU.mult)
            post(h, "dve", reads=("mload%d" % (idx % 2),), writes=("msk%d" % idx,))
        h = qs.dma_start(out=scr[:, 2560:2688], in_=ap["colidx"]); h.inc("qs")
        h = qs.dma_start(out=scr[:, 2688:2689], in_=ap["c127"]); h.inc("qs")
        h = qs.dma_start(out=scr[:, 2689:2690], in_=ap["c64"]); h.inc("qs")
        h = qs.dma_start(out=scr[:, 2690:2691], in_=ap["c128"]); cgp = h.inc("qs")
        act.wait("qs", cgp)
        pre(act, reads=("lngcol",))
        h = act.activation(gpcol[:], scr[:, 2560:2688], AF.Exp, scale=lngcol[:, 0:1])
        post(h, "act", writes=("gpcol",))
        h = act.activation(gtpair[:], scr[:, 2688:2689], AF.Exp, scale=lngcol[:, 0:1])
        post(h, "act", writes=("gtpair",))
        h = act.activation(gCcol[:], scr[:, 2689:2690], AF.Exp, scale=lngcol[:, 0:1])
        post(h, "act", writes=("gCcol",))
        h = act.activation(gC2col[:], scr[:, 2690:2691], AF.Exp, scale=lngcol[:, 0:1])
        post(h, "act", writes=("gC2col",))

        dve.wait("qs", c0)
        h = dve.tensor_copy(scrb[0:1, 0:4], scr2[0:1, 17:21])
        post(h, "dve", writes=("segb",))
        pre(pe, reads=("segb",), writes=("pb0",))
        h = mm(pb[0][0:128, 0:4], ones_row[0:1, 0:128],
               scrb[0:1, 0:4], True, True)
        post(h, "pe", reads=("segb",), writes=("pb0",))
        pre(act, reads=("pb0",), writes=("segc",))
        h = act.copy(statcd[0][:, 8:12], pb[0][0:128, 0:4])
        post(h, "act", reads=("pb0",), writes=("segc",))
        pre(dve, reads=("segc",), writes=("maskI",))
        for k in range(4):
            h = dve.tensor_scalar(maskI[:, 128 * k:128 * k + 128], ident[:],
                                  statcd[0][:, 8 + k:9 + k], None, ALU.mult)
        post(h, "dve", reads=("segc",), writes=("maskI",))

        dve.wait("qs", c0)
        h = dve.tensor_scalar(b1eff[:], w1c[:], scr2[:, 16:17], None, ALU.mult)
        K.fence(dve)
        h = dve.tensor_tensor(b1eff[:], b1eff[:], scr2[:, 0:8], ALU.add)
        post(h, "dve", writes=("b1eff",))
        pre(dve, writes=("diagW2",))
        for j in range(8):
            h = dve.tensor_scalar(diagW2[:, 64 * j:64 * j + 64], foldF[:],
                                  scr2[:, 8 + j:9 + j], None, ALU.mult)
        post(h, "dve", writes=("diagW2",))
        h = dve.memset(gaugd[0][64:65, :], 1.0)
        h = dve.memset(gaugd[1][64:65, :], 1.0)
        post(h, "dve", writes=("gaug1",))

        pre(qs, writes=("scrstage",))
        h = qs.dma_start(out=scr[:, 1024:2048], in_=ap["Wwh"][0:128, :]); h.inc("qs")
        h = qs.dma_start(out=scr[:, 0:1024], in_=ap["Wwh"][128:256, :])
        cww = post(h, "qs", writes=("scrstage",))
        dve.wait("qs", cww)
        for half in range(2):
            pre(dve, reads=("scrstage",), writes=("scrb",))
            h = dve.tensor_copy(scrb[:, 0:1024],
                                scr[:, 1024:2048] if half == 0 else scr[:, 0:1024])
            post(h, "dve", reads=("scrstage",), writes=("scrb",))
            pre(pe, reads=("scrb",), writes=("pb0", "pb1"))
            for k in range(8):
                h = mm(pb[k // 4][:, 128 * (k % 4):128 * (k % 4) + 128],
                       scrb[:, 128 * k:128 * k + 128], ident[:], True, True)
            post(h, "pe", reads=("scrb",), writes=("pb0", "pb1"))
            pre(act, reads=("pb0", "pb1"), writes=("Wwt",))
            for k in range(8):
                h = act.copy(Wwt[:, 256 * k + 128 * half:256 * k + 128 * half + 128],
                             pb[k // 4][:, 128 * (k % 4):128 * (k % 4) + 128])
            post(h, "act", reads=("pb0", "pb1"), writes=("Wwt",))
        pre(pe, reads=("Wwt",), writes=("pb2",))
        for k in range(8):
            h = mm(pb[2][0:1, 0:256], ones_col[:, 0:1], Wwt[:, 256 * k:256 * k + 256],
                   k == 0, k == 7)
        post(h, "pe", reads=("Wwt",), writes=("pb2",))
        pre(act, reads=("pb2",), writes=("negwwsum",))
        h = act.activation(negwwsum[:], pb[2][0:1, 0:256], AF.Identity, scale=np.float32(-1.0))
        post(h, "act", reads=("pb2",), writes=("negwwsum",))

        for r in range(8):
            pre(qs, writes=("scrstage",))
            h = qs.dma_start(out=scr[:, 1024:2048],
                             in_=ap["Wr"][128 * r:128 * r + 128, :])
            cw = post(h, "qs", writes=("scrstage",))
            dve.wait("qs", cw)
            pre(dve, reads=("scrstage",), writes=("scrb",))
            h = dve.tensor_copy(scrb[:, 0:1024], scr[:, 1024:2048])
            post(h, "dve", reads=("scrstage",), writes=("scrb",))
            pre(pe, reads=("scrb",), writes=("pb0", "pb1"))
            for cix in range(8):
                h = mm(pb[cix // 4][:, 128 * (cix % 4):128 * (cix % 4) + 128],
                       scrb[:, 128 * cix:128 * cix + 128], ident[:], True, True)
            post(h, "pe", reads=("scrb",), writes=("pb0", "pb1"))
            pre(act, reads=("pb0", "pb1"), writes=("Wrt",))
            for cix in range(8):
                h = act.copy(Wrt[:, 1024 * cix + 128 * r:1024 * cix + 128 * r + 128],
                             pb[cix // 4][:, 128 * (cix % 4):128 * (cix % 4) + 128])
            post(h, "act", reads=("pb0", "pb1"), writes=("Wrt",))

        pre(qs, writes=("scrstage",))
        h = qs.dma_start(out=scr[:, 1024:1536].rearrange("p (c m) -> p c m", c=8),
                         in_=ap["Win"].rearrange("(c p) m -> p c m", p=128))
        cw = post(h, "qs", writes=("scrstage",))
        dve.wait("qs", cw)
        pre(dve, reads=("scrstage",), writes=("Win2",))
        for cix in range(8):
            h = dve.tensor_copy(Win2[:, 128 * cix:128 * cix + 64],
                                scr[:, 1024 + 64 * cix:1024 + 64 * cix + 64])
            h = dve.tensor_copy(Win2[:, 128 * cix + 64:128 * cix + 128],
                                scr[:, 1024 + 64 * cix:1024 + 64 * cix + 64])
        post(h, "dve", reads=("scrstage",), writes=("Win2",))

        pre(qs, writes=("scrstage",))
        h = qs.dma_start(out=scr[0:64, 1024:2048], in_=ap["Wout"]); h.inc("qs")
        h = qs.dma_start(out=scr[0:1, 0:1024], in_=ap["bout"]); h.inc("qs")
        h = qs.dma_start(out=scr[0:64, 2691:2692], in_=ap["b2col"])
        cw = post(h, "qs", writes=("scrstage",))
        dve.wait("qs", cw)
        pre(dve, reads=("scrstage",), writes=("WoutA", "b2b"))
        h = dve.tensor_scalar(WoutA[0:64, :], scr[0:64, 1024:2048], 0.5, None, ALU.mult)
        post(h, "dve", reads=("scrstage",), writes=("WoutA",))
        h = dve.tensor_copy(statcb[0:64, 0:1], scr[0:64, 2691:2692])
        post(h, "dve", reads=("scrstage",), writes=("b2b",))
        for nh in range(2):
            pre(pe, reads=("WoutA", "b2b"), writes=("pb3",))
            h = mm(pb[3][0:1, 0:512], statcb[0:64, 0:1],
                   WoutA[0:64, 512 * nh:512 * nh + 512], True, True)
            post(h, "pe", reads=("WoutA", "b2b"), writes=("pb3",))
            pre(dve, reads=("pb3", "scrstage"), writes=("WoutArow",))
            h = dve.scalar_tensor_tensor(WoutA[64:65, 512 * nh:512 * nh + 512],
                                         scr[0:1, 512 * nh:512 + 512 * nh], 0.5,
                                         pb[3][0:1, 0:512], ALU.mult, ALU.add)
            post(h, "dve", reads=("pb3", "scrstage"), writes=("WoutArow",))

        def ln_stats(srcp, tag, pp):
            statc = statcd[pp]
            stag = "statc%d" % pp
            pre(dve, reads=(tag,), writes=(stag,))
            h = dve.reduce_sum(statc[:, 0:1], srcp, axis=AX)
            post(h, "dve", reads=(tag,), writes=(stag,))
            pre(dve, writes=(stag,))
            h = dve.memset(statc[:, 1:2], 0.0)
            post(h, "dve", writes=(stag,))
            pre(act, reads=(tag, stag), writes=("scrb", stag))
            h = act.activation(scrb[:, 0:1024], srcp, AF.Square, accum_out=statc[:, 1:2])
            post(h, "act", reads=(tag,), writes=(stag, "scrb"))
            pre(dve, reads=(stag,), writes=(stag,))
            h = dve.tensor_scalar(statc[:, 5:6], statc[:, 0:1], RD, None, ALU.mult)
            post(h, "dve", writes=("mu5_%d" % pp,))
            h = dve.tensor_scalar(statc[:, 2:3], statc[:, 1:2], RD,
                                  EPS, ALU.mult, ALU.add)
            K.sread(dve, "mu5_%d" % pp)
            h = dve.tensor_scalar(statc[:, 3:4], statc[:, 5:6],
                                  statc[:, 5:6], -1.0, ALU.mult, ALU.mult)
            K.fence(dve)
            h = dve.tensor_tensor(statc[:, 3:4], statc[:, 2:3], statc[:, 3:4], ALU.add)
            post(h, "dve", writes=(stag,))
            pre(act, reads=(stag,), writes=(stag,))
            h = act.activation(statc[:, 4:5], statc[:, 3:4], AF.Sqrt)
            post(h, "act", writes=(stag,))
            pre(dve, reads=(stag,), writes=(stag,))
            h = dve.reciprocal(statc[:, 6:7], statc[:, 4:5])
            post(h, "dve", writes=(stag,))
            K.fence(dve)
            h = dve.tensor_scalar(statc[:, 7:8], statc[:, 5:6],
                                  statc[:, 6:7], -1.0, ALU.mult, ALU.mult)
            post(h, "dve", writes=(stag,))

        def ffn_a(xin_ap, xin_tag, pp):
            statc = statcd[pp]
            stag = "statc%d" % pp
            y0, y0fm, Htile = y0d[pp], y0fmd[pp], Htiled[pp]
            ty0, ty0fm, tH = "y0_%d" % pp, "y0fm_%d" % pp, "Htile_%d" % pp
            pre(act, reads=(xin_tag, stag), writes=(ty0,))
            K.sread(act, stag)
            h = act.activation(y0[:], xin_ap, AF.Identity,
                               scale=statc[:, 6:7], bias=statc[:, 7:8])
            post(h, "act", reads=(xin_tag, stag), writes=(ty0,))
            pre(pe, reads=(ty0,), writes=("pb0", "pb1"))
            for k in range(8):
                h = mm(pb[k // 4][:, 128 * (k % 4):128 * (k % 4) + 128],
                       y0[:, 128 * k:128 * k + 128], ident[:], True, True)
            post(h, "pe", reads=(ty0,), writes=("pb0", "pb1"))
            evac(y0fm[:, 0:512], ty0fm, pb[0][:, 0:512], "pb0")
            evac(y0fm[:, 512:1024], ty0fm, pb[1][:, 0:512], "pb1")
            pre(pe, reads=(ty0fm, "Win2"), writes=("pb2",))
            for k in range(8):
                h = mm(pb[2][:, 0:128], Win2[:, 128 * k:128 * k + 128],
                       y0fm[:, 128 * k:128 * k + 128], k == 0, k == 7)
            post(h, "pe", reads=(ty0fm,), writes=("pb2",))
            pre(act, reads=("pb2", "b1eff"), writes=(tH,))
            for j in range(8):
                h = act.activation(Htile[:, 128 * j:128 * j + 128], pb[2][:, 0:128],
                                   AF.Relu, scale=w1c[:, j:j + 1], bias=b1eff[:, j:j + 1])
            post(h, "act", reads=("pb2",), writes=(tH,))

        def ffn_b(xin_ap, xin_tag, xout_ap, xout_tag, pp):
            Htile, gaug, ffm = Htiled[pp], gaugd[pp], ffmd[pp]
            tH = "Htile_%d" % pp
            tga, tffm = "gaug_%d" % pp, "ffm_%d" % pp
            pre(pe, reads=(tH, "diagW2"), writes=("pb3",))
            for j in range(8):
                h = mm(pb[3][0:64, 0:128], diagW2[:, 64 * j:64 * j + 64],
                       Htile[:, 128 * j:128 * j + 128], j == 0, j == 7)
            post(h, "pe", reads=(tH,), writes=("pb3",))
            evac(gaug[0:64, :], tga, pb[3][0:64, 0:128], "pb3",
                 extra_reads=("gaug1",))
            pre(pe, reads=(tga, "WoutA", "WoutArow", "gaug1"), writes=("pb4", "pb5"))
            for cix in range(8):
                h = mm(pb[4 + cix // 4][:, 128 * (cix % 4):128 * (cix % 4) + 128],
                       WoutA[:, 128 * cix:128 * cix + 128], gaug[:, 0:128], True, True)
            post(h, "pe", reads=(tga,), writes=("pb4", "pb5"))
            evac(ffm[:, 0:512], tffm, pb[4][:, 0:512], "pb4")
            evac(ffm[:, 512:1024], tffm, pb[5][:, 0:512], "pb5")
            pre(pe, reads=(tffm,), writes=("pw0", "pw1"))
            for cix in range(8):
                h = mm(pwide[:, 128 * cix:128 * cix + 128],
                       ffm[:, 128 * cix:128 * cix + 128], ident[:], True, True)
            post(h, "pe", reads=(tffm,), writes=("pw0", "pw1"))
            pre(dve, reads=("pw0", "pw1", xin_tag), writes=(xout_tag,))
            h = dve.scalar_tensor_tensor(xout_ap, xin_ap, 1.0, pwide[:, 0:1024],
                                         ALU.mult, ALU.add)
            post(h, "dve", reads=("pw0", "pw1", xin_tag), writes=(xout_tag,))

        def p1_dma(t):
            if t >= NTILE:
                return
            half = t % 2
            xs = xt[:, D * half:D * half + D]
            xtag = "xt%d" % half
            pre(qs, writes=(xtag,))
            h = qs.dma_start(out=xs, in_=ap["x_own"][128 * t:128 * t + 128, :])
            post(h, ("qA", "qB")[half], writes=(xtag,))

        def p1_s1(t):
            half = t % 2
            xs = xt[:, D * half:D * half + D]
            xtag = "xt%d" % half
            w = K.R(xtag).writer
            dve.wait(*w)
            act.wait(*w)
            ln_stats(xs, xtag, half)
            ffn_a(xs, xtag, half)

        def p1_s2(t):
            half = t % 2
            xs = xt[:, D * half:D * half + D]
            xtag = "xt%d" % half
            ffn_b(xs, xtag, x1td[half][:], "x1t_%d" % half, half)

        def p1_s3(t):
            half = t % 2
            statc = statcd[half]
            stag = "statc%d" % half
            x1t, x1bf, x1fm, zrow = (x1td[half], x1bfd[half], x1fmd[half],
                                     zrowd[half])
            tx1 = "x1t_%d" % half
            ln_stats(x1t[:], tx1, half)
            pre(dve, reads=(tx1, stag), writes=("zrow%d" % half, stag))
            h = dve.tensor_scalar(statc[:, 2:3], statc[:, 6:7], 1.0, None, ALU.add)
            post(h, "dve", writes=("zp2",))
            K.fence(dve)
            h = dve.tensor_scalar(zrow[:], x1t[:], statc[:, 2:3],
                                  statc[:, 7:8], ALU.mult, ALU.add)
            post(h, "dve", reads=(tx1,), writes=("zrow%d" % half, stag))
            pre(qs, reads=("zrow%d" % half,))
            h = qs.dma_start(out=z_spill[128 * t:128 * t + 128, :], in_=zrow[:])
            post(h, "qs", reads=("zrow%d" % half,))
            pre(dve, reads=(stag,), writes=("statcb",))
            h = dve.tensor_copy(statcb[:, 0:1], statc[:, 5:6])
            h = dve.tensor_copy(statcb[:, 1:2], statc[:, 6:7])
            post(h, "dve", reads=(stag,), writes=("statcb",))
            pre(pe, reads=("statcb",), writes=("pb2",))
            h = mm(pb[2][0:2, 128:256], statcb[:, 0:2], ident[:], True, True)
            post(h, "pe", reads=("statcb",), writes=("pb2",))
            pre(act, reads=("pb2",), writes=("statrow",))
            h = act.copy(statrow[:, 128 * t:128 * t + 128], pb[2][0:2, 128:256])
            post(h, "act", reads=("pb2",), writes=("statrow",))
            pre(dve, reads=(tx1,), writes=("x1bf%d" % half,))
            h = dve.tensor_copy(x1bf[:], x1t[:])
            post(h, "dve", reads=(tx1,), writes=("x1bf%d" % half,))
            pre(pe, reads=("x1bf%d" % half,), writes=("pb0", "pb1"))
            for k in range(8):
                h = mm(pb[k // 4][:, 128 * (k % 4):128 * (k % 4) + 128],
                       x1bf[:, 128 * k:128 * k + 128], ident[:], True, True)
            post(h, "pe", reads=("x1bf%d" % half,), writes=("pb0", "pb1"))
            evac(x1fm[:, 0:512], "x1fm%d" % half, pb[0][:, 0:512], "pb0")
            evac(x1fm[:, 512:1024], "x1fm%d" % half, pb[1][:, 0:512], "pb1")
            pre(qs, reads=("x1fm%d" % half,))
            qb = (t // 4) * (D + 2)
            h = qs.dma_start(
                out=ag_in[qb:qb + D, 128 * (t % 4):128 * (t % 4) + 128]
                    .rearrange("(k p) t -> p k t", p=128),
                in_=x1fm[:].rearrange("p (k t) -> p k t", k=8))
            post(h, "qs", reads=("x1fm%d" % half,))
            if t % 4 == 3:
                q4 = t // 4
                pre(qs, reads=("statrow",))
                h = qs.dma_start(out=ag_in[qb + D:qb + D + 2, :],
                                 in_=statrow[:, 512 * q4:512 * q4 + 512])
                cq = post(h, "qs", reads=("statrow",))
                if stage >= 2:
                    pool.wait("qs", cq)
                    h = pool.collective_compute(
                        "AllGather", ALU.bypass,
                        replica_groups=[[0, 1, 2, 3], [4, 5, 6, 7]],
                        ins=[ag_in[qb:qb + (D + 2), :]],
                        outs=[ag_out[4 * qb:4 * qb + 4 * (D + 2), :]])
                    h.inc("cc")

        if stage >= 1:
            p1_dma(0)
            for w in range(NTILE + 2):
                if 2 <= w:
                    p1_s3(w - 2)
                if 1 <= w <= NTILE:
                    p1_s2(w - 1)
                if w < NTILE:
                    p1_dma(w + 1)
                    p1_s1(w)
        cc_x1 = K.c("cc")

        pre(dve, writes=("S32",))
        h = dve.memset(S32[:], 0.0)
        post(h, "dve", writes=("S32",))
        pre(act, reads=("S32",), writes=("Shi",))
        h = act.copy(Shi[:], S32[:])
        post(h, "act", reads=("S32",), writes=("Shi",))

        ccx_piece = [cc_x1 - 3, cc_x1 - 2, cc_x1 - 1, cc_x1]
        qg_seg = {}
        for s in range(NSEG if stage >= 2 else 0):
            for q4 in range(4):
                qs.wait("cc", ccx_piece[q4])
                pre(qs, writes=("statin%d" % q4,))
                pbase = q4 * 4 * (D + 2) + s * (D + 2)
                h = qs.dma_start(out=statin[:, 512 * q4:512 * q4 + 512],
                                 in_=ag_out[pbase + D:pbase + D + 2, :])
                post(h, "qs", writes=("statin%d" % q4,))
            if s == 0:
                pre(dve, writes=("rkcarry",))
                h = dve.memset(rkd0[:, 0:1], 0.0)
                h = dve.memset(rkd1[:, 0:1], 0.0)
                post(h, "dve", writes=("rkcarry",))
            else:
                pre(dve, reads=("rkd",), writes=("rkcarry",))
                h = dve.tensor_copy(rkd0[:, 0:1], rkd0[:, SEG:SEG + 1])
                h = dve.tensor_copy(rkd1[:, 0:1], rkd1[:, SEG:SEG + 1])
                post(h, "dve", reads=("rkd",), writes=("rkcarry",))

            if DEBUG and s == 2:
                qs.wait("cc", 5)
                h = qs.dma_start(out=dbg_pb[:, 3328:3584], in_=ago_out[0:128, :])
                post(h, "qs")
            def t_dma(t):
                half = t % 2
                xin = x1fmi[:, D * half:D * half + D]
                xtag = "x1fmi%d" % half
                qs.wait("cc", ccx_piece[t // 4])
                pbase = (t // 4) * 4 * (D + 2) + s * (D + 2)
                pre(qs, writes=(xtag,))
                h = qs.dma_start(
                    out=xin.rearrange("p (k t) -> p k t", k=8),
                    in_=ag_out[pbase:pbase + D,
                               128 * (t % 4):128 * (t % 4) + 128]
                        .rearrange("(k p) t -> p k t", p=128))
                post(h, "qs", writes=(xtag,))

            def t_proj(t):
                half = t % 2
                xin = x1fmi[:, D * half:D * half + D]
                xtag = "x1fmi%d" % half
                stat2c = stat2cd[half]
                ts2 = "stat2c%d" % half
                t1tok = t1tokd[half]
                tt1 = "t1tok%d" % half
                bv = ("pb0", "pb4")[half]
                bt1 = ("pb1", "pb5")[half]
                baux = ("pb2", "pw0")[half]
                pre(pe, reads=("statin%d" % (t // 4),), writes=(baux,))
                h = mm(BK[baux][0:128, 256:258], statin[0:2, 128 * t:128 * t + 128],
                       ident[0:2, 0:2], True, True)
                post(h, "pe", reads=("statin",), writes=(baux,))
                pre(act, reads=(baux,), writes=(ts2,))
                h = act.copy(stat2c[:, 0:2], BK[baux][0:128, 256:258])
                post(h, "act", reads=(baux,), writes=(ts2,))
                pre(pe, reads=(xtag, "Wwt", "statin%d" % (t // 4), "negwwsum"),
                    writes=(bv,))
                for k in range(8):
                    h = mm(BK[bv][:, 0:256], xin[:, 128 * k:128 * k + 128],
                           Wwt[:, 256 * k:256 * k + 256], k == 0, False)
                h = mm(BK[bv][:, 0:256], statin[0:1, 128 * t:128 * t + 128],
                       negwwsum[:], False, True)
                post(h, "pe", reads=(xtag, "statin%d" % (t // 4)), writes=(bv,))
                pre(act, reads=(bv, ts2), writes=("vtok",))
                K.sread(act, ts2)
                h = act.activation(vtok[:, 256 * t:256 * t + 256], BK[bv][:, 0:256],
                                   AF.Identity, scale=stat2c[:, 1:2])
                post(h, "act", reads=(bv, ts2), writes=("vtok",))
                pre(pe, reads=(xtag, "EyeH", "statin%d" % (t // 4)), writes=(bt1,))
                for k in range(8):
                    h = mm(BK[bt1][:, 0:256], xin[:, 128 * k:128 * k + 128],
                           EyeH[:, 256 * k:256 * k + 256], k == 0, False)
                h = mm(BK[bt1][:, 0:256], statin[0:1, 128 * t:128 * t + 128],
                       negones_row[:], False, True)
                post(h, "pe", reads=(xtag, "statin%d" % (t // 4)), writes=(bt1,))
                evac(t1tok[:], tt1, BK[bt1][:, 0:256], bt1)

            def t_norm(t):
                half = t % 2
                t1tok, t1sq, rktok, nrm = (t1tokd[half], t1sq1,
                                           rktokd[half], nrmd[half])
                tt1, tsq, trk, tnr = ("t1tok%d" % half, "t1sq",
                                      "rktok%d" % half, "nrm%d" % half)
                pre(pool, reads=(tt1,), writes=(tsq,))
                h = pool.tensor_tensor(t1sq[:], t1tok[:], t1tok[:], ALU.mult)
                post(h, "pool", reads=(tt1,), writes=(tsq,))
                pre(dve, reads=(tsq,), writes=(tnr,))
                h = dve.reduce_sum(nrm[:, 0:1], t1sq[:], axis=AX)
                post(h, "dve", reads=(tsq,), writes=(tnr,))
                pre(act, reads=(tnr,), writes=(tnr,))
                h = act.activation(nrm[:, 1:2], nrm[:, 0:1], AF.Sqrt)
                post(h, "act", writes=(tnr,))
                pre(dve, reads=(tnr,), writes=(tnr,))
                h = dve.tensor_scalar(nrm[:, 2:3], nrm[:, 1:2], 1e-12, None, ALU.max)
                K.fence(dve)
                h = dve.reciprocal(nrm[:, 3:4], nrm[:, 2:3])
                post(h, "dve", writes=(tnr,))
                pre(act, reads=(tnr, tt1), writes=(trk,))
                K.sread(act, tnr)
                h = act.activation(rktok[:], t1tok[:], AF.Identity, scale=nrm[:, 3:4])
                post(h, "act", reads=(tnr, tt1), writes=(trk,))

            def t_build(t):
                half = t % 2
                rktok = rktokd[half]
                trk = "rktok%d" % half
                bwk = ("pb3", "pw1")[half]
                pre(pe, reads=(trk,), writes=(bwk,))
                h = mm(BK[bwk][:, 256:384], rktok[:, 0:128], ident[:], True, True)
                h = mm(BK[bwk][:, 384:512], rktok[:, 128:256], ident[:], True, True)
                post(h, "pe", reads=(trk,), writes=(bwk,))
                pre(act, reads=(bwk,), writes=("rkd",))
                h = act.copy(rkd0[:, 128 * t + 1:128 * t + 129], BK[bwk][:, 256:384])
                post(h, "act", reads=(bwk,), writes=("rkd",))
                pre(dve, reads=(bwk,), writes=("rkd",))
                h = dve.tensor_copy(rkd1[:, 128 * t + 1:128 * t + 129],
                                    BK[bwk][:, 384:512])
                post(h, "dve", reads=(bwk,), writes=("rkd",))
                pre(dve, reads=("rkd", "gpcol"), writes=("rkg",))
                h = dve.tensor_tensor(rkg0[:, 128 * t:128 * t + 128],
                                      rkd0[:, 128 * t + 1:128 * t + 129], gpcol[:], ALU.mult)
                h = dve.tensor_tensor(rkg1[:, 128 * t:128 * t + 128],
                                      rkd1[:, 128 * t + 1:128 * t + 129], gpcol[:], ALU.mult)
                post(h, "dve", reads=("rkd",), writes=("rkg",))
                rds = ("rkd", "rkcarry") if t == 0 else ("rkd",)
                pre(pe, reads=rds, writes=(bwk,))
                h = mm(BK[bwk][:, 0:128], rkd0[:, 128 * t:128 * t + 128], ident[:], True, True)
                h = mm(BK[bwk][:, 128:256], rkd1[:, 128 * t:128 * t + 128], ident[:], True, True)
                post(h, "pe", reads=rds, writes=(bwk,))
                evac(wktok[:, 256 * t:256 * t + 256], "wktok", BK[bwk][:, 0:256], bwk)

            wavefront(NTILE, [t_dma, t_proj, t_norm, t_build])

            def p_gram(p):
                r4 = p % 4
                powbuf, Xba = powbufd[r4], Xbad[r4]
                tpw, txb = "powbuf%d" % r4, "Xba%d" % r4
                bG = ("pb0", "pb4")[p % 2]
                wk0 = rkd0[:, 128 * p:128 * p + 128]
                wk1 = rkd1[:, 128 * p:128 * p + 128]
                rk0 = rkd0[:, 128 * p + 1:128 * p + 129]
                rk1 = rkd1[:, 128 * p + 1:128 * p + 129]
                rds = ("rkd", "rkcarry") if p == 0 else ("rkd",)
                pre(pe, reads=rds, writes=(bG,))
                h = mm(BK[bG][:, 0:128], wk0, wk0, True, False)
                h = mm(BK[bG][:, 0:128], wk1, wk1, False, True)
                h = mm(BK[bG][:, 128:256], wk0, rk0, True, False)
                h = mm(BK[bG][:, 128:256], wk1, rk1, False, True)
                post(h, "pe", reads=rds, writes=(bG,))
                pre(dve, reads=(bG, "msk0", "msk1", "msk2", "msk3"),
                    writes=(tpw, "ITXp", txb))
                h = dve.tensor_tensor(powbuf[:, 0:128], BK[bG][:, 0:128], maskN[:], ALU.mult)
                h = dve.tensor_tensor(powbuf[:, 128:256], BK[bG][:, 0:128], maskNT[:], ALU.mult)
                h = dve.tensor_tensor(ITXp[:, 128 * p:128 * p + 128], BK[bG][:, 128:256],
                                      maskITX[:], ALU.mult)
                h = dve.tensor_tensor(Xba[64:128, 0:64], BK[bG][64:128, 0:64],
                                      maskX[64:128, 0:64], ALU.mult)
                post(h, "dve", reads=(bG,), writes=(tpw, "ITXp", txb))

            def p_dbl(p, lv0, lv1):
                r4 = p % 4
                powbuf, ptprev = powbufd[r4], ptprevd[r4]
                tpw, tpt = "powbuf%d" % r4, "ptprev%d" % r4
                bPT = ("pb1", "pb5")[p % 2]
                bSQ = "pb2"
                if lv0 == 0:
                    pre(pe, reads=(tpw,), writes=(bPT,))
                    h = mm(BK[bPT][:, 0:128], ident[:], ident[:], True, False)
                    h = mm(BK[bPT][:, 0:128], powbuf[:, 0:128], ident[:], False, False)
                    post(h, "pe", reads=(tpw,), writes=(bPT,))
                    evac(ptprev[:], tpt, BK[bPT][:, 0:128], bPT)
                for lvl in range(lv0, lv1):
                    pre(pe, reads=(tpw,), writes=(bSQ,))
                    h = mm(BK[bSQ][:, 0:128], powbuf[:, 128:256], powbuf[:, 0:128],
                           True, True)
                    if lvl < 4:
                        h = mm(BK[bSQ][:, 128:256], powbuf[:, 0:128], powbuf[:, 128:256],
                               True, True)
                    post(h, "pe", reads=(tpw,), writes=(bSQ,))
                    w = 256 if lvl < 4 else 128
                    evac(powbuf[:, 0:w], tpw, BK[bSQ][:, 0:w], bSQ)
                    pre(pe, reads=(tpw, tpt), writes=())
                    h = mm(BK[bPT][:, 0:128], powbuf[:, 0:128], ptprev[:],
                           False, lvl == 4)
                    post(h, "pe", reads=(tpw, tpt), writes=(bPT,))
                    if lvl < 4:
                        evac(ptprev[:], tpt, BK[bPT][:, 0:128], bPT)
                    else:
                        evac(PTp[:, 128 * p:128 * p + 128], "PTp",
                             BK[bPT][:, 0:128], bPT)

            def p_tail(p):
                r4 = p % 4
                Xba, PTgC = Xbad[r4], PTgCd[p % 2]
                txb, tgc = "Xba%d" % r4, "PTgC%d" % (p % 2)
                bWA = ("pb0", "pb4")[p % 2]
                bWB = "pb2"
                pre(dve, reads=("PTp", "gCcol"), writes=(tgc,))
                h = dve.tensor_scalar(PTgC[64:128, 0:64],
                                      PTp[64:128, 128 * p + 64:128 * p + 128],
                                      gCcol[64:128, 0:1], None, ALU.mult)
                post(h, "dve", reads=("PTp",), writes=(tgc,))
                pre(pe, reads=("wktok", "PTp", tgc), writes=(bWA, bWB))
                h = mm(BK[bWA][:, 0:64], wktok[0:64, 256 * p:256 * p + 128],
                       PTp[0:64, 128 * p:128 * p + 64], True, True)
                h = mm(BK[bWA][:, 64:128], wktok[0:64, 256 * p + 128:256 * p + 256],
                       PTp[0:64, 128 * p:128 * p + 64], True, True)
                h = mm(BK[bWB][:, 0:64], wktok[64:128, 256 * p:256 * p + 128],
                       PTgC[64:128, 0:64], True, True)
                h = mm(BK[bWB][:, 64:128], wktok[64:128, 256 * p + 128:256 * p + 256],
                       PTgC[64:128, 0:64], True, True)
                post(h, "pe", reads=("wktok", "PTp", tgc), writes=(bWA, bWB))
                evac(wkcT0[:, 128 * p:128 * p + 64], "wkcT", BK[bWA][:, 0:64], bWA,
                     scale=-1.0)
                evac(wkcT1[:, 128 * p:128 * p + 64], "wkcT", BK[bWA][:, 64:128], bWA,
                     scale=-1.0)
                evac(wkcT0[:, 128 * p + 64:128 * p + 128], "wkcT", BK[bWB][:, 0:64],
                     bWB, scale=-1.0)
                evac(wkcT1[:, 128 * p + 64:128 * p + 128], "wkcT", BK[bWB][:, 64:128],
                     bWB, scale=-1.0)
                pre(pe, reads=(txb, "PTp"), writes=(bWB,))
                h = mm(BK[bWB][0:64, 256:320], Xba[64:128, 0:64],
                       PTp[64:128, 128 * p + 64:128 * p + 128], True, True)
                post(h, "pe", reads=(txb, "PTp"), writes=(bWB,))
                evac(MbaTn[0:64, 64 * p:64 * p + 64], "MbaTn",
                     BK[bWB][0:64, 256:320], bWB, scale=-1.0)

            def s_dump(p, off):
                if DEBUG and s == 0:
                    pre(qs, reads=("Shi", "vnew_a", "vnew_b"))
                    h = qs.dma_start(out=dbg_pb[:, off:off + 512], in_=Shi[:]); h.inc("qs")
                    h = qs.dma_start(out=dbg_pb[:, off + 512:off + 768], in_=vnewsb[:])
                    post(h, "qs", reads=("Shi", "vnew_a", "vnew_b"))

            def s_a(p):
                if p == 0:
                    s_dump(p, 0)
                pre(pe, reads=("PTp", "vtok", "wkcT", "Shi"), writes=("pw0",))
                h = mm(BK["pw0"][0:64, 0:256], PTp[0:64, 128 * p:128 * p + 64],
                       vtok[0:64, 256 * p:256 * p + 256], True, False)
                h = mm(BK["pw0"][0:64, 0:256], wkcT0[:, 128 * p:128 * p + 64],
                       Shi[:, 0:256], False, False)
                h = mm(BK["pw0"][0:64, 0:256], wkcT1[:, 128 * p:128 * p + 64],
                       Shi[:, 256:512], False, True)
                post(h, "pe", reads=("PTp", "vtok", "wkcT", "Shi"), writes=("pw0",))
                pre(act, reads=("pw0",), writes=("vnew_a",))
                h = act.copy(vnewsb[0:64, :], BK["pw0"][0:64, 0:256])
                post(h, "act", reads=("pw0",), writes=("vnew_a",))

            def s_b(p):
                pre(pe, reads=("PTp", "vtok", "wkcT", "Shi", "MbaTn", "vnew_a"),
                    writes=("pw1",))
                h = mm(BK["pw1"][64:128, 0:256], PTp[64:128, 128 * p + 64:128 * p + 128],
                       vtok[64:128, 256 * p:256 * p + 256], True, False, tp=(64, 64))
                h = mm(BK["pw1"][64:128, 0:256], wkcT0[:, 128 * p + 64:128 * p + 128],
                       Shi[:, 0:256], False, False, tp=(0, 64))
                h = mm(BK["pw1"][64:128, 0:256], wkcT1[:, 128 * p + 64:128 * p + 128],
                       Shi[:, 256:512], False, False, tp=(0, 64))
                h = mm(BK["pw1"][64:128, 0:256], MbaTn[0:64, 64 * p:64 * p + 64],
                       vnewsb[0:64, :], False, True, tp=(0, 64))
                post(h, "pe", reads=("PTp", "vtok", "wkcT", "Shi", "MbaTn", "vnew_a"),
                     writes=("pw1",))
                pre(act, reads=("pw1",), writes=("vnew_b",))
                h = act.copy(vnewsb[64:128, :], BK["pw1"][64:128, 0:256])
                post(h, "act", reads=("pw1",), writes=("vnew_b",))
                pre(dve, reads=("pw0", "pw1", "gtpair"), writes=("vssb",))
                h = dve.tensor_scalar(vssb[0:64, :], BK["pw0"][0:64, 0:256],
                                      gtpair[0:64, 0:1], None, ALU.mult)
                h = dve.tensor_scalar(vssb[64:128, :], BK["pw1"][64:128, 0:256],
                                      gtpair[64:128, 0:1], None, ALU.mult)
                post(h, "dve", reads=("pw0", "pw1"), writes=("vssb",))

            def s_o(p):
                pre(pe, reads=("rkg", "Shi", "ITXp", "vnew_a", "vnew_b"), writes=("pw0",))
                h = mm(BK["pw0"][:, 256:512], rkg0[:, 128 * p:128 * p + 128], Shi[:, 0:256],
                       True, False)
                h = mm(BK["pw0"][:, 256:512], rkg1[:, 128 * p:128 * p + 128], Shi[:, 256:512],
                       False, False)
                h = mm(BK["pw0"][:, 256:512], ITXp[:, 128 * p:128 * p + 128], vnewsb[:],
                       False, True)
                post(h, "pe", reads=("rkg", "Shi", "ITXp", "vnew_a", "vnew_b"),
                     writes=("pw0",))
                osb = osbd[p % 2]
                tosb = "osb%d" % (p % 2)
                pre(act, reads=("pw0", "alphacol"), writes=(tosb,))
                h = act.activation(osb[:], BK["pw0"][:, 256:512], AF.Identity,
                                   scale=alphacol[:, 0:1])
                post(h, "act", reads=("pw0",), writes=(tosb,))
                if p > 0:
                    q = p - 1
                    pre(qs, reads=("osb%d" % (q % 2),))
                    h = qs.dma_start(out=ago_in_l[s][128 * q:128 * q + 128, :],
                                     in_=osbd[q % 2][:])
                    post(h, "qg", reads=("osb%d" % (q % 2),))
                if p == NTILE - 1:
                    pre(qs, reads=(tosb, "Shi"))
                    h = qs.dma_start(out=ago_in_l[s][128 * p:128 * p + 128, :],
                                     in_=osb[:])
                    post(h, "qg", reads=(tosb,))

            def s_s(p):
                if p == 1:
                    s_dump(p, 1024)
                if DEBUG and s == 0 and p == 0:
                    pre(qs, reads=("osb", "ITXp", "rkg"))
                    h = qs.dma_start(out=dbg_pb[:, 2048:2304], in_=osb[:]); h.inc("qs")
                    h = qs.dma_start(out=dbg_pb[:, 2304:2432], in_=ITXp[:, 0:128]); h.inc("qs")
                    h = qs.dma_start(out=dbg_pb[:, 2432:2560], in_=rkg0[:, 0:128]); h.inc("qs")
                    h = qs.dma_start(out=dbg_pb[:, 2560:2688], in_=rkg1[:, 0:128])
                    post(h, "qs", reads=("osb", "ITXp", "rkg"))
                pre(pe, reads=("wktok", "vssb"), writes=("pb3",))
                h = mm(pb[3][:, 0:256], wktok[:, 256 * p:256 * p + 128], vssb[:],
                       True, True)
                h = mm(pb[3][:, 256:512], wktok[:, 256 * p + 128:256 * p + 256], vssb[:],
                       True, True)
                post(h, "pe", reads=("wktok", "vssb"), writes=("pb3",))
                pre(dve, reads=("pb3", "gC2col", "S32"), writes=("S32",))
                K.fence(dve)
                h = dve.scalar_tensor_tensor(S32[:], S32[:], gC2col[:, 0:1],
                                             pb[3][:, 0:512], ALU.mult, ALU.add)
                post(h, "dve", reads=("pb3",), writes=("S32",))
                pre(act, reads=("S32",), writes=("Shi",))
                h = act.copy(Shi[:], S32[:])
                post(h, "act", reads=("S32",), writes=("Shi",))

            if stage >= 3:
                LAG = 5
                for w in range(NTILE + LAG):
                    if stage >= 4 and LAG <= w:
                        s_a(w - LAG)
                    if 4 <= w < NTILE + 4:
                        p_tail(w - 4)
                    if stage >= 4 and LAG <= w:
                        s_b(w - LAG)
                    if 3 <= w < NTILE + 3:
                        p_dbl(w - 3, 4, 5)
                    if stage >= 4 and LAG <= w:
                        s_o(w - LAG)
                    if 2 <= w < NTILE + 2:
                        p_dbl(w - 2, 2, 4)
                    if stage >= 4 and LAG <= w:
                        s_s(w - LAG)
                    if 1 <= w < NTILE + 1:
                        p_dbl(w - 1, 0, 2)
                    if w < NTILE:
                        p_gram(w)

            if stage >= 5 and OAG_PIECED and s > 0:
                if DEBUG and s == 1 and K.R("dumpgate").writer:
                    pool.wait(*K.R("dumpgate").writer)
                pool.wait("qg", qg_seg[s - 1])
                h = pool.collective_compute(
                    "AllGather", ALU.bypass,
                    replica_groups=[[0, 1, 2, 3], [4, 5, 6, 7]],
                    ins=[ago_in_l[s - 1]],
                    outs=[ago_out[(s - 1) * 4 * SEG:s * 4 * SEG, :]])
                h.inc("cc")
            if DEBUG:
                pre(qs, reads=("S32",))
                h = qs.dma_start(out=dbg_p[:, 512 * s:512 * s + 512], in_=S32[:])
                post(h, "qs", reads=("S32",))
                if False:
                    pre(qs, reads=("vtok", "wkcT", "PTp", "ITXp", "MbaTn"))
                    h = qs.dma_start(out=dbg_pb[:, 0:2048], in_=PTp[:]); h.inc("qs")
                    h = qs.dma_start(out=dbg_pb[:, 2048:4096], in_=ITXp[:]); h.inc("qs")
                    h = qs.dma_start(out=dbg_pb[:, 4096:6144], in_=wkcT0[:]); h.inc("qs")
                    h = qs.dma_start(out=dbg_pb[:, 6144:7168], in_=vtok[:, 0:1024]); h.inc("qs")
                    h = qs.dma_start(out=dbg_pb[0:64, 7168:8192], in_=MbaTn[:])
                    post(h, "qs", reads=("vtok", "wkcT"))
            if DEBUG and s == 0:
                qs.wait("qg", K.c("qg"))
                h = qs.dma_start(out=dbg_pb[:, 2688:2944],
                                 in_=ago_in_l[0][0:128, :])
                dump_val = post(h, "qs")
                K.R("dumpgate").writer = ("qs", dump_val)
            qg_seg[s] = K.c("qg")
            if stage >= 5 and OAG_PIECED and s == NSEG - 1:
                pool.wait("qg", qg_seg[s])
                h = pool.collective_compute(
                    "AllGather", ALU.bypass,
                    replica_groups=[[0, 1, 2, 3], [4, 5, 6, 7]],
                    ins=[ago_in_l[s]],
                    outs=[ago_out[s * 4 * SEG:(s + 1) * 4 * SEG, :]])
                h.inc("cc")
        if stage >= 5 and not OAG_PIECED:
            pool.wait("qg", K.c("qg"))
            for s4 in range(NSEG):
                h = pool.collective_compute(
                    "AllGather", ALU.bypass,
                    replica_groups=[[0, 1, 2, 3], [4, 5, 6, 7]],
                    ins=[ago_in_l[s4]],
                    outs=[ago_out[s4 * 4 * SEG:(s4 + 1) * 4 * SEG, :]])
                h.inc("cc")
        cc_o = K.c("cc")

        cc_o = K.c("cc")

        qs.wait("cc", cc_o)

        def p4_dma(t):
            half = t % 2
            tob = "oblk%d" % half
            ztag = "ztile%d" % half
            qsem = ("qA", "qB")[half]
            oblk = oblkd[half]
            pre(qs, writes=(tob, ztag))
            for j in range(4):
                for k in range(4):
                    row = k * 4 * SEG + j * SEG + 128 * t
                    h = qs.dma_start(out=oblk[:, HD * (4 * j + k):HD * (4 * j + k) + HD],
                                     in_=ago_out[row:row + 128, :])
                    h.inc(qsem)
            h = qs.dma_start(out=ztile[:, D * half:D * half + D],
                             in_=z_spill[128 * t:128 * t + 128, :])
            post(h, qsem, writes=(tob, ztag))

        def p4_s1(t):
            half = t % 2
            zt = ztile[:, D * half:D * half + D]
            ztag = "ztile%d" % half
            oblk = oblkd[half]
            o_fm = o_fmd[half]
            x2t = x2td[half]
            tob, tofm, tx2 = "oblk%d" % half, "o_fm%d" % half, "x2t_%d" % half
            btr = (("pb0", "pb1"), ("pb4", "pb5"))[half]
            pre(pe, reads=(tob, "maskI"), writes=btr)
            for j in range(4):
                for r in range(2):
                    cix = 2 * j + r
                    bank = BK[btr[cix // 4]]
                    off = 128 * (cix % 4)
                    for k in range(4):
                        h = mm(bank[:, off:off + 128],
                               oblk[:, HD * (4 * j + k) + 128 * r:
                                    HD * (4 * j + k) + 128 * r + 128],
                               maskI[:, 128 * k:128 * k + 128], k == 0, k == 3)
            post(h, "pe", reads=(tob, "maskI"), writes=btr)
            evac(o_fm[:, 0:512], tofm, BK[btr[0]][:, 0:512], btr[0])
            evac(o_fm[:, 512:1024], tofm, BK[btr[1]][:, 0:512], btr[1])
            pre(pe, reads=(tofm, "Wrt"), writes=("pw0", "pw1"))
            for cix in range(8):
                for nh in range(2):
                    h = mm(pwide[:, 512 * nh:512 * nh + 512],
                           o_fm[:, 128 * cix:128 * cix + 128],
                           Wrt[:, 1024 * cix + 512 * nh:1024 * cix + 512 * nh + 512],
                           cix == 0, cix == 7)
            post(h, "pe", reads=(tofm, "Wrt"), writes=("pw0", "pw1"))
            pre(dve, reads=("pw0", "pw1", ztag), writes=(tx2,))
            h = dve.scalar_tensor_tensor(x2t[:], zt, 1.0, pwide[:, 0:1024],
                                         ALU.mult, ALU.add)
            post(h, "dve", reads=("pw0", "pw1", ztag), writes=(tx2,))

        def p4_s2(t):
            half = t % 2
            x2t = x2td[half]
            tx2 = "x2t_%d" % half
            ln_stats(x2t[:], tx2, half)
            ffn_a(x2t[:], tx2, half)

        def p4_s3(t):
            half = t % 2
            x2t = x2td[half]
            tx2 = "x2t_%d" % half
            ffn_b(x2t[:], tx2, x1td[half][:], "x1t_%d" % half, half)
            pre(pool, reads=("x1t_%d" % half,))
            h = pool.dma_start(out=out_own[128 * t:128 * t + 128, :], in_=x1td[half][:])
            post(h, "qg", reads=("x1t_%d" % half,))

        if stage >= 5:
            p4_dma(0)
            for w in range(NTILE + 2):
                if w >= 2:
                    p4_s3(w - 2)
                if 1 <= w <= NTILE:
                    p4_s2(w - 1)
                if w < NTILE:
                    p4_s1(w)
                    if w + 1 < NTILE:
                        p4_dma(w + 1)
        if DEBUG:
            qs.wait("qs", K.c("qs"))
            qs.wait("qg", K.c("qg"))
            h = qs.dma_start(out=dbg_o, in_=ago_out[0:T, :]); h.inc("qs")
            h = qs.dma_start(out=dbg_z, in_=z_spill); h.inc("qs")
        if stage < 5:
            pre(dve, writes=("zrow",))
            h = dve.memset(zrow[:], 0.0)
            post(h, "dve", writes=("zrow",))
            for t in range(NTILE):
                pre(qs, reads=("zrow",))
                h = qs.dma_start(out=out_own[128 * t:128 * t + 128, :], in_=zrow[:])
                post(h, "qs", reads=("zrow",))
        qs.wait("qs", K.c("qs"))
        pool.wait("qg", K.c("qg"))

        K.emit(nc, sems)
    return nc


_NC_CACHE = None


def _build_in_maps(inputs):
    x = np.asarray(inputs["L0_attn_out"], dtype=np.float32)
    Ww = np.asarray(inputs["L0_attn_Ww"], dtype=np.float32)
    Wr = np.asarray(inputs["L0_attn_Wr"], dtype=np.float32)
    decay = np.asarray(inputs["L0_attn_decay"], dtype=np.float32)
    log_alpha = np.asarray(inputs["L0_attn_log_alpha"], dtype=np.float32)
    Win = np.asarray(inputs["L0_ffn_Win"], dtype=np.float32)
    bin_ = np.asarray(inputs["L0_ffn_bin_"], dtype=np.float32)
    W1 = np.asarray(inputs["L0_ffn_W1"], dtype=np.float32)
    b1 = np.asarray(inputs["L0_ffn_b1"], dtype=np.float32)
    W2 = np.asarray(inputs["L0_ffn_W2"], dtype=np.float32)
    b2 = np.asarray(inputs["L0_ffn_b2"], dtype=np.float32)
    Wout = np.asarray(inputs["L0_ffn_Wout"], dtype=np.float32)
    bout = np.asarray(inputs["L0_ffn_bout"], dtype=np.float32)

    consts = _host_consts()
    W1c = _w_cols(W1)
    b1c = _w_cols(b1)
    W2c = _w_cols(W2)
    binc = np.ascontiguousarray(bin_[np.arange(128) % 64][:, None].astype(np.float32))
    in_maps = []
    for c in range(NCORE):
        b, h = c // 4, c % 4
        s = c % 4
        eyeh = np.zeros((128, 8 * HD), dtype=np.float32)
        eyeh[np.arange(128), HD * 2 * h + np.arange(128)] = 1.0
        eyeh[np.arange(128), HD * (2 * h + 1) + 128 + np.arange(128)] = 1.0
        m = {
            "x_own": np.ascontiguousarray(x[b, s * SEG:(s + 1) * SEG, :]),
            "Wwh": np.ascontiguousarray(Ww[h * HD:(h + 1) * HD, :]),
            "Wr": Wr,
            "Win": Win,
            "W1c": W1c, "b1c": b1c, "W2c": W2c, "binc": binc,
            "Wout": Wout, "bout": np.ascontiguousarray(bout[None, :]),
            "b2col": np.ascontiguousarray(b2[:, None]),
            "decay_h": np.ascontiguousarray(decay[h:h + 1][:, None]),
            "log_alpha_h": np.ascontiguousarray(log_alpha[h:h + 1][:, None]),
            "segmask": np.ascontiguousarray(
                (np.arange(4) == s).astype(np.float32)[None, :]),
            "EyeH": eyeh.astype(bf16),
        }
        for kk, v in consts.items():
            m[kk] = v
        in_maps.append(m)
    return in_maps


def kernel(**inputs):
    global _NC_CACHE
    if _NC_CACHE is None:
        _NC_CACHE = build_nc()
    in_maps = _build_in_maps(inputs)
    res = run_bass_kernel_spmd(_NC_CACHE, in_maps, core_ids=list(range(NCORE)))
    out = np.zeros((B, T, D), dtype=np.float32)
    for c in range(NCORE):
        b, s = c // 4, c % 4
        out[b, s * SEG:(s + 1) * SEG, :] = res.results[c]["out_own"]
    return out
